# version 11
# baseline (speedup 1.0000x reference)
import numpy as np
import ml_dtypes
import concourse.bass as bass
import concourse.mybir as mybir
from concourse.bass_utils import run_bass_kernel_spmd

F32 = mybir.dt.float32
BF16 = mybir.dt.bfloat16
AF = mybir.ActivationFunctionType
ALU = mybir.AluOpType
AX = mybir.AxisListType

D = 1024
H = 8
K = 128
C = 64
HID = 4096
NE = 8
EPS_L2 = 1e-6
EPS_RMS = 1e-5
EPS_LN = 1e-5
RSQK = float(1.0 / np.sqrt(K))
NCORES = 8
QT = 512


class Sem:
    def __init__(self, handle):
        self.h = handle
        self.n = 0

    def inc(self, k=1):
        self.n += k
        return self.n


class Prog:
    ENGS = ("tensor", "vector", "scalar", "gpsimd", "sync")

    def __init__(self, block, own_sems):
        self.block = block
        self.q = {e: [] for e in self.ENGS}
        self.own = own_sems
        self.pend = {e: 0 for e in self.ENGS}

    def wait(self, eng, sem, val):
        if val > 0:
            self.q[eng].append(lambda E, s=sem, v=val: E.wait_ge(s.h, v))

    def op(self, eng, fn, inc=None, inc_n=1):
        if inc is None:
            self.q[eng].append(fn)
            return None
        v = inc.inc(inc_n)
        def run(E, fn=fn, s=inc, n=inc_n):
            fn(E).then_inc(s.h, n)
        self.q[eng].append(run)
        return v

    def sop(self, eng, fn):
        s = self.own[eng]
        self.wait(eng, s, self.pend[eng])
        v = self.op(eng, fn, inc=s, inc_n=1)
        self.pend[eng] = v
        return v

    def flush(self):
        for name in self.ENGS:
            lst = self.q[name]
            if lst:
                def body(E, lst=lst):
                    for f in lst:
                        f(E)
                getattr(self.block, name)(body)
        self.q = {e: [] for e in self.ENGS}


def host_constants():
    cst = {}
    eye = np.eye(128)
    cst["c_id_bf"] = eye.astype(ml_dtypes.bfloat16)
    cst["c_id_f32"] = eye.astype(np.float32)
    cst["c_ones_1x128"] = np.ones((1, 128), np.float32)
    cst["c_ones_128x1"] = np.ones((128, 1), np.float32)
    tril = np.tril(np.ones((C, C)))
    bd = np.zeros((128, 128)); bd[:C, :C] = tril.T; bd[C:, C:] = tril.T
    cst["c_bd_trilT"] = bd.astype(np.float32)
    bd2 = np.zeros((128, 128)); bd2[:C, :C] = 1.0; bd2[C:, C:] = 1.0
    cst["c_bd_ones"] = bd2.astype(np.float32)
    cst["c_negtril_i"] = np.tile(-np.tril(np.ones((C, C))), (2, 1)).astype(np.float32)
    cst["c_negtril_s"] = np.tile(-np.tril(np.ones((C, C)), -1), (2, 1)).astype(np.float32)
    cst["c_triu_i"] = np.tile(np.triu(np.ones((C, C))), (2, 1)).astype(np.float32)
    cst["c_triu_s"] = np.tile(np.triu(np.ones((C, C)), 1), (2, 1)).astype(np.float32)
    sel = np.zeros((H, 4 * 128))
    for hp in range(4):
        sel[hp, hp * 128:hp * 128 + 64] = 1.0
        sel[hp + 4, hp * 128 + 64:hp * 128 + 128] = 1.0
    cst["c_sel_hh4"] = sel.astype(np.float32)
    dup = np.zeros((128, 256))
    for p in range(128):
        dup[p % 64, p] = 1.0
        dup[64 + p % 64, 128 + p] = 1.0
    cst["c_dup64"] = dup.astype(np.float32)
    colsel = np.zeros((H, H * 128))
    for h in range(H):
        colsel[h, h * 128:(h + 1) * 128] = 1.0
    cst["c_colsel"] = colsel.astype(np.float32)
    return cst


def build(T=4096, skip_attn=False, dbg=(), stop_after=None):
    from contextlib import ExitStack
    nc = bass.Bass("TRN2", target_bir_lowering=False)
    TOWN = T // 2
    NTT = TOWN // 128
    T5W = min(512, TOWN)
    SUBS = T5W // 128
    NT512 = TOWN // T5W
    HQ = HID // 1024
    NQ = T // QT
    TTQ = QT // 128
    NCH = T // C

    dram = {}

    def din(name, shape, dt=F32):
        dram[name] = nc.dram_tensor(name, list(shape), dt, kind="ExternalInput").ap()
        return dram[name]

    for nm, arr in host_constants().items():
        din(nm, arr.shape, BF16 if arr.dtype == ml_dtypes.bfloat16 else F32)
    x_own = din("x_own", (TOWN, D))
    wgateT = din("wgateT", (NE, D, HID))
    wupT = din("wupT", (NE, D, HID))
    wdownT = din("wdownT", (NE, HID, D))
    wgT = din("wgT", (D, NE))
    bg_bc = din("bg_bc", (128, NE))
    if not skip_attn:
        xtm = din("xtm", (T, D))
        xT = din("xT", (D, T))
        xT_own = din("xT_own", (D, TOWN))
        wvT = din("wvT", (D, D))
        wgaT = din("wgaT", (D, D))
        woT = din("woT", (D, D))
        wabT = din("wabT", (D, 16))
        conv_wD = din("conv_wD", (128, 32))
        conv_bD = din("conv_bD", (128, 8))
        dtb_bc = din("dtb_bc", (128, H))
        alog_row = din("alog_row", (1, H))
        nwm = din("nwm", (128, NCH))
    out_y = nc.dram_tensor("out_y", [TOWN, D], F32, kind="ExternalOutput").ap()

    dbg_out = {}
    for name, shape, ddt in dbg:
        dbg_out[name] = nc.dram_tensor("dbg_" + name, list(shape), ddt,
                                       kind="ExternalOutput").ap()

    with ExitStack() as ctx:
        def mk_sb(stack, side="left"):
            def f(name, shape, dt):
                return stack.enter_context(
                    nc.sbuf_tensor(name, shape, dt, side=side))
            return f

        sb = mk_sb(ctx)
        id_bf = sb("id_bf", [128, 128], BF16)
        id_f32 = sb("id_f32", [128, 128], F32)
        ones_1x128 = sb("ones_1x128", [1, 128], F32)
        ones_128x1 = sb("ones_128x1", [128, 1], F32)
        bd_trilT = sb("bd_trilT", [128, 128], F32)
        bd_ones = sb("bd_ones", [128, 128], F32)
        negtril_i = sb("negtril_i", [128, C], F32)
        negtril_s = sb("negtril_s", [128, C], F32)
        triu_i = sb("triu_i", [128, C], F32)
        triu_s = sb("triu_s", [128, C], F32)
        sel_hh4 = sb("sel_hh4", [H, 4 * 128], F32)
        dup64 = sb("dup64", [128, 256], F32)
        colsel = sb("colsel", [H, H * 128], F32)
        wgT_sb = sb("wgT_sb", [128, 8, NE], BF16)
        bg_sb = sb("bg_sb", [128, NE], F32)
        wstage = sb("wstage", [128, NTT, NE], F32)
        lncol = sb("lncol", [128, 8], F32)
        l_sb = sb("l_sb", [128, NE], F32)
        t8a = sb("t8a", [128, NE], F32)
        t8b = sb("t8b", [128, NE], F32)
        pbanks = [ctx.enter_context(nc.psum_tensor(f"pb{i}", [128, 512], F32))
                  for i in range(8)]
        b0, b1, b2, b3, b4, b5, b6, b7 = pbanks
        h_ld = ctx.enter_context(nc.semaphore("s_ld"))
        h_lg = ctx.enter_context(nc.semaphore("s_lg"))
        h_pe = ctx.enter_context(nc.semaphore("s_pe"))
        h_ve = ctx.enter_context(nc.semaphore("s_ve"))
        h_ac = ctx.enter_context(nc.semaphore("s_ac"))
        h_gp = ctx.enter_context(nc.semaphore("s_gp"))
        h_st = ctx.enter_context(nc.semaphore("s_st"))
        block = ctx.enter_context(nc.Block())
        ld, lg, pe, ve, ac, gp, st = (Sem(h) for h in
                                      (h_ld, h_lg, h_pe, h_ve, h_ac, h_gp, h_st))
        pr = Prog(block, {"tensor": pe, "vector": ve, "scalar": ac,
                          "gpsimd": gp, "sync": st})

        def dma(eng, sem, dst, src):
            return pr.op(eng, lambda E, dd=dst, ss=src: E.dma_start(out=dd, in_=ss),
                         inc=sem, inc_n=16)

        def barrier_all(eng):
            pr.wait(eng, pe, pe.n)
            pr.wait(eng, ve, ve.n)
            pr.wait(eng, ac, ac.n)
            pr.wait(eng, gp, gp.n)

        def dbg_dump(name, src_ap):
            if name not in dbg_out:
                return
            barrier_all("sync")
            dma("sync", st, dbg_out[name], src_ap)

        for t_sb, nm in [
            (id_bf, "c_id_bf"), (id_f32, "c_id_f32"),
            (ones_1x128, "c_ones_1x128"), (ones_128x1, "c_ones_128x1"),
            (bd_trilT, "c_bd_trilT"), (bd_ones, "c_bd_ones"),
            (negtril_i, "c_negtril_i"), (negtril_s, "c_negtril_s"),
            (triu_i, "c_triu_i"), (triu_s, "c_triu_s"),
            (sel_hh4, "c_sel_hh4"), (colsel, "c_colsel"),
            (dup64, "c_dup64"),
        ]:
            dma("sync", ld, t_sb[:], dram[nm])
        dma("sync", ld, bg_sb[:], dram["bg_bc"])
        dma("gpsimd", lg, wgT_sb[:],
            dram["wgT"].rearrange("(dt p) e -> p dt e", p=128))
        v_cst = ld.n
        v_wgt = lg.n

        def ln_tile(dst_ap, src_ap, scr_half):
            pr.wait("scalar", ve, pr.pend["vector"])
            pr.sop("scalar", lambda E: E.activation(
                scr_half, src_ap[:, 0:512], AF.Square, accum_out=lncol[:, 0:1]))
            vb_ = pr.sop("scalar", lambda E: E.activation(
                scr_half, src_ap[:, 512:1024], AF.Square, accum_out=lncol[:, 1:2]))
            pr.sop("vector", lambda E: E.reduce_sum(
                lncol[:, 2:3], src_ap, axis=AX.X))
            pr.wait("vector", ac, vb_)
            pr.sop("vector", lambda E: E.tensor_add(
                lncol[:, 1:2], lncol[:, 0:1], lncol[:, 1:2]))
            pr.sop("vector", lambda E: E.tensor_scalar(
                lncol[:, 3:4], lncol[:, 2:3], 1.0 / D, None, ALU.mult))
            pr.sop("vector", lambda E: E.tensor_scalar(
                lncol[:, 4:5], lncol[:, 1:2], 1.0 / D, None, ALU.mult))
            pr.sop("vector", lambda E: E.tensor_mul(
                lncol[:, 5:6], lncol[:, 3:4], lncol[:, 3:4]))
            pr.sop("vector", lambda E: E.tensor_sub(
                lncol[:, 5:6], lncol[:, 4:5], lncol[:, 5:6]))
            v2 = pr.sop("vector", lambda E: E.tensor_scalar(
                lncol[:, 5:6], lncol[:, 5:6], EPS_LN, None, ALU.add))
            pr.wait("scalar", ve, v2)
            pr.sop("scalar", lambda E: E.activation(
                lncol[:, 6:7], lncol[:, 5:6], AF.Ln))
            va_ = pr.sop("scalar", lambda E: E.activation(
                lncol[:, 6:7], lncol[:, 6:7], AF.Exp, scale=-0.5))
            pr.sop("vector", lambda E: E.tensor_scalar(
                lncol[:, 7:8], lncol[:, 3:4], -1.0, None, ALU.mult))
            pr.wait("vector", ac, va_)
            return pr.sop("vector", lambda E: E.tensor_scalar(
                dst_ap, src_ap, lncol[:, 7:8], lncol[:, 6:7], ALU.add, ALU.mult))

        if not skip_attn:
            sA1 = ExitStack()
            sb1 = mk_sb(sA1, side="right")
            o_normT = sb1("o_normT", [128, H, TOWN], BF16)

            sA2 = ExitStack()
            sb2 = mk_sb(sA2, side="right")
            grow = sb2("grow", [H, QT], F32)
            nbrow = sb2("nbrow", [H, QT], F32)
            egvrow = sb2("egvrow", [H, QT], F32)
            eglrow = sb2("eglrow", [H, QT], F32)
            eglb = sb2("eglb", [128, H, NCH], F32)
            S32 = sb2("S32", [128, H, 128], F32)
            wab_sb = sb2("wab_sb", [128, 8, 16], F32)
            cw_sb = sb2("cw_sb", [128, 8, 4], F32)
            cb_sb = sb2("cb_sb", [128, 8], F32)
            dtb_sb = sb2("dtb_sb", [128, H], F32)
            alog_sb = sb2("alog_sb", [1, H], F32)
            nexpA = sb2("nexpA", [128, H], F32)
            nwm_sb = sb2("nwm_sb", [128, NCH], F32)
            wv_sb = sb2("wv_sb", [128, 8, D], BF16)
            xT_bf2 = [sb2("xT_bf_a", [128, 8, QT], BF16),
                      sb2("xT_bf_b", [128, 8, QT], BF16)]
            nT = sb2("nT", [128, H, QT + 1], BF16)
            xvT = sb2("xvT", [128, 8, QT + 3], BF16)
            v_fm = sb2("v_fm", [128, 8, QT], BF16)
            vb_tm = sb2("vb_tm", [128, TTQ, H, 128], BF16)
            CS = sb2("CS", [128, TTQ, 32], F32)
            CSd = sb2("CSd", [128, TTQ, 2, 32], F32)
            x_tm_r = sb2("x_tm_r", [128, 2, D], F32)
            xT32_r = sb2("xT32_r", [128, 2, 8, 128], F32)
            n_tm_r = sb2("n_tm_r", [128, 2, D], BF16)
            scrA = sb2("scrA", [128, 1, D], F32)
            ss8 = sb2("ss8", [128, 2, 8], F32)
            rstd8 = sb2("rstd8", [128, 2, 8], F32)
            sp16 = sb2("sp16", [128, 2, 16], F32)
            spw = sb2("spw", [128, 2, 24], F32)
            RS = sb2("RS", [128, 2, 32], F32)
            cv_r = sb2("cv_r", [128, 1, 512], F32)
            mdiff = sb2("mdiff", [128, 4, C], F32)
            e1t = sb2("e1t", [128, 4, C], F32)
            e2t = sb2("e2t", [128, 4, C], F32)
            e1sneg = sb2("e1sneg", [128, 4, C], BF16)
            e2s = sb2("e2s", [128, 4, C], BF16)
            e2zi = sb2("e2zi", [128, 4, C], BF16)
            t2m = sb2("t2m", [128, 4, C], BF16)
            Abd = sb2("Abd", [128, H, 128], BF16)
            ATbd = sb2("ATbd", [128, H, 128], BF16)
            PQ_sb = [sb2("PQ_sb0", [128, H, 256], BF16),
                     sb2("PQ_sb1", [128, H, 256], BF16)]
            X_sb = [sb2("X_sb0", [128, H, 256], BF16),
                    sb2("X_sb1", [128, H, 256], BF16)]
            Xf = sb2("Xf", [128, H, 256], BF16)
            wkcT = sb2("wkcT", [128, H, 128], BF16)
            wkbeg = sb2("wkbeg", [128, H, 128], BF16)
            wkdw = sb2("wkdw", [128, 2, H, 128], BF16)
            attnT_r = sb2("attnT_r", [128, 2, H, C], BF16)
            rk_eT8 = sb2("rk_eT8", [128, H, C], BF16)
            v_new8 = sb2("v_new8", [128, H, 128], BF16)
            o2_sb = sb2("o2_sb", [128, 8, C], F32)
            ostage = sb2("ostage", [128, 2, 8, C], F32)
            rst_row = sb2("rst_row", [1, 512], F32)
            rstdbc = sb2("rstdbc", [128, 512], F32)
            onrm_t = sb2("onrm_t", [128, 8, C], BF16)
            Sbf8 = sb2("Sbf8", [128, H, 128], BF16)

            dma("gpsimd", lg, wv_sb[:],
                dram["wvT"].rearrange("(dt p) dv -> p dt dv", p=128))
            v_lg_wv = lg.n
            dma("sync", ld, wab_sb[:],
                dram["wabT"].rearrange("(dt p) e -> p dt e", p=128))
            dma("sync", ld, cw_sb.ap().rearrange("p a b -> p (a b)"),
                dram["conv_wD"])
            dma("sync", ld, cb_sb[:], dram["conv_bD"])
            dma("sync", ld, dtb_sb[:], dram["dtb_bc"])
            dma("sync", ld, alog_sb[:], dram["alog_row"])
            v_nwm = dma("sync", ld, nwm_sb[:], dram["nwm"])
            pr.wait("scalar", ld, v_nwm)
            va = pr.sop("scalar", lambda E: E.activation(
                alog_sb[:], alog_sb[:], AF.Exp))
            pr.wait("tensor", ac, va)
            vp = pr.op("tensor", lambda E: E.matmul(
                b7[:, 0:8], ones_1x128[:], alog_sb[:], start=True, stop=True),
                inc=pe)
            pr.wait("vector", pe, vp)
            v_b7_free = pr.sop("vector", lambda E: E.tensor_scalar(
                nexpA[:], b7[:, 0:8], -1.0, None, ALU.mult))
            pr.op("vector", lambda E: E.memset(Abd[:], 0.0))
            pr.op("vector", lambda E: E.memset(ATbd[:], 0.0))
            pr.op("vector", lambda E: E.memset(S32[:], 0.0))
            pr.op("vector", lambda E: E.memset(nT[:, :, 0:1], 0.0))
            pr.sop("vector", lambda E: E.memset(xvT[:, :, 0:3], 0.0))

            halo = {"n": 0, "xv": 0}
            wkdw_done = [0, 0]
            b6_guard = {"v": 0}
            prep_guard = [{"ac": 0, "ve": 0, "pe": 0} for _ in range(2)]
            b3_done = {"v": 0}

            def evac(use_ac, dst, src):
                if use_ac:
                    return (ac, pr.sop("scalar",
                                       lambda E, dst=dst, src=src: E.copy(dst, src)))
                return (ve, pr.sop("vector",
                                   lambda E, dst=dst, src=src: E.tensor_copy(dst, src)))

            def solve_AB(q, tq):
                gt = q * TTQ + tq
                tr = gt % 2
                lcq0 = 2 * tq
                pr.wait("vector", pe, pe.n)
                pr.wait("scalar", pe, pe.n)
                pr.wait("gpsimd", pe, pe.n)
                pr.wait("scalar", ve, pr.pend["vector"])
                for lc in (0, 1):
                    gc = 2 * gt + lc
                    lcq = lcq0 + lc
                    pr.wait("tensor", ve, pr.pend["vector"])
                    pr.wait("tensor", ac, pr.pend["scalar"])
                    pr.wait("tensor", gp, pr.pend["gpsimd"])
                    if lc == 1:
                        pr.wait("vector", gp, pr.pend["gpsimd"])
                        pr.wait("scalar", gp, pr.pend["gpsimd"])
                        pr.wait("gpsimd", ve, pr.pend["vector"])
                        pr.wait("scalar", ve, pr.pend["vector"])
                    for hp in range(4):
                        for sub in (0, 1):
                            h = hp + 4 * sub
                            pp = 64 * sub
                            wk1 = nT.ap()[:, h, 64 * lcq:64 * lcq + 64]
                            rk1 = nT.ap()[:, h, 64 * lcq + 1:64 * lcq + 65]
                            tp = (0, pp)
                            pr.op("tensor", lambda E, hp=hp, pp=pp, wk1=wk1, tp=tp:
                                  E.matmul(b0[pp:pp + 64,
                                              128 * hp + pp:128 * hp + pp + 64],
                                           wk1, wk1, start=True, stop=True,
                                           tile_position=tp))
                            pr.op("tensor", lambda E, hp=hp, pp=pp, wk1=wk1,
                                  rk1=rk1, tp=tp:
                                  E.matmul(b1[pp:pp + 64,
                                              128 * hp + pp:128 * hp + pp + 64],
                                           wk1, rk1, start=True, stop=True,
                                           tile_position=tp))
                        pr.op("tensor", lambda E, hp=hp, lcq=lcq: E.matmul(
                            b3[:, 64 * hp:64 * hp + 64],
                            sel_hh4[:, hp * 128:(hp + 1) * 128],
                            grow[0:8, 64 * lcq:64 * lcq + 64],
                            start=True, stop=True))
                        pr.op("tensor", lambda E, hp=hp, lcq=lcq: E.matmul(
                            b3[:, 256 + 64 * hp:256 + 64 * hp + 64],
                            sel_hh4[:, hp * 128:(hp + 1) * 128],
                            nbrow[0:8, 64 * lcq:64 * lcq + 64],
                            start=True, stop=True),
                            inc=pe if hp == 3 else None)
                    v_A_mm = pe.n
                    pr.wait("vector", pe, v_A_mm)
                    pr.wait("gpsimd", pe, v_A_mm)
                    pr.wait("scalar", pe, v_A_mm)
                    p0 = 64 * lc
                    for hp in range(4):
                        pr.op("vector", lambda E, hp=hp, p0=p0, tq=tq:
                              E.tensor_scalar(
                                  mdiff[0:64, hp, :], b3[0:64, 64 * hp:64 * hp + 64],
                                  CS[p0:p0 + 64, tq, hp:hp + 1], None,
                                  ALU.subtract))
                        pr.op("vector", lambda E, hp=hp, p0=p0, tq=tq:
                              E.tensor_scalar(
                                  mdiff[64:128, hp, :],
                                  b3[64:128, 64 * hp:64 * hp + 64],
                                  CS[p0:p0 + 64, tq, 4 + hp:5 + hp], None,
                                  ALU.subtract),
                              inc=ve if hp == 3 else None)
                    v_md = ve.n
                    m3 = mdiff.ap()[:, :, :]
                    pr.wait("vector", ve, v_md)
                    pr.op("vector", lambda E: E.tensor_mul(
                        e1t.ap()[:, :, :], m3,
                        negtril_i.ap()[:, :].unsqueeze(1).broadcast_to([128, 4, C])))
                    v_m = pr.op("vector", lambda E: E.tensor_mul(
                        e2t.ap()[:, :, :], m3,
                        triu_i.ap()[:, :].unsqueeze(1).broadcast_to([128, 4, C])),
                        inc=ve)
                    pr.wait("scalar", ve, v_m)
                    pr.op("scalar", lambda E: E.activation(
                        e1t.ap()[:, :, :].rearrange("p a b -> p (a b)"),
                        e1t.ap()[:, :, :].rearrange("p a b -> p (a b)"), AF.Exp))
                    v_e = pr.op("scalar", lambda E: E.activation(
                        e2t.ap()[:, :, :].rearrange("p a b -> p (a b)"),
                        e2t.ap()[:, :, :].rearrange("p a b -> p (a b)"), AF.Exp),
                        inc=ac)
                    pr.wait("gpsimd", ac, v_e)
                    pr.op("gpsimd", lambda E: E.tensor_mul(
                        e1sneg.ap()[:, :, :], e1t.ap()[:, :, :],
                        negtril_s.ap()[:, :].unsqueeze(1).broadcast_to([128, 4, C])))
                    pr.op("gpsimd", lambda E: E.tensor_mul(
                        e2s.ap()[:, :, :], e2t.ap()[:, :, :],
                        triu_s.ap()[:, :].unsqueeze(1).broadcast_to([128, 4, C])))
                    v_g = pr.op("gpsimd", lambda E: E.tensor_mul(
                        e2zi.ap()[:, :, :], e2t.ap()[:, :, :],
                        triu_i.ap()[:, :].unsqueeze(1).broadcast_to([128, 4, C])),
                        inc=gp)
                    pr.wait("vector", gp, v_g)
                    v_t2 = pr.op("vector", lambda E: E.tensor_mul(
                        t2m.ap()[:, :, :], e2s.ap()[:, :, :],
                        b3.ap()[:, 256:512].rearrange("p (a b) -> p a b", a=4)),
                        inc=ve)
                    pr.wait("vector", ve, v_t2)
                    pr.op("vector", lambda E, lc=lc, tr=tr: E.scalar_tensor_tensor(
                        attnT_r.ap()[64 * lc:64 * lc + 64, tr, 0:4, :],
                        b1.ap()[0:64, 0:512].rearrange("p (a b) -> p a b", b=128)
                        [:, :, 0:64],
                        RSQK, e2zi.ap()[0:64, :, :], ALU.mult, ALU.mult))
                    pr.op("vector", lambda E, lc=lc, tr=tr: E.scalar_tensor_tensor(
                        attnT_r.ap()[64 * lc:64 * lc + 64, tr, 4:8, :],
                        b1.ap()[64:128, 0:512].rearrange("p (a b) -> p a b", b=128)
                        [:, :, 64:128],
                        RSQK, e2zi.ap()[64:128, :, :], ALU.mult, ALU.mult))
                    for hp in range(4):
                        for sub, h in ((0, hp), (1, hp + 4)):
                            pp = 64 * sub
                            pr.op("vector", lambda E, hp=hp, h=h, lc=lc,
                                  pp=pp, tq=tq: E.scalar_tensor_tensor(
                                      Abd[64 * lc:64 * lc + 64, h,
                                          64 * lc:64 * lc + 64],
                                      b0[pp:pp + 64, 128 * hp + pp:128 * hp + pp + 64],
                                      CSd[pp:pp + 64, tq, lc, 8 + h:9 + h],
                                      e1sneg[pp:pp + 64, hp, :],
                                      ALU.mult, ALU.mult))
                            pr.op("vector", lambda E, hp=hp, h=h, lc=lc,
                                  pp=pp: E.tensor_mul(
                                      ATbd[64 * lc:64 * lc + 64, h,
                                           64 * lc:64 * lc + 64],
                                      b0[pp:pp + 64, 128 * hp + pp:128 * hp + pp + 64],
                                      t2m[pp:pp + 64, hp, :]),
                                  inc=ve if hp == 3 and sub == 1 else None)
                    pr.pend["vector"] = ve.n
                b3_done["v"] = pr.pend["vector"]
                pr.wait("tensor", ve, pr.pend["vector"])
                v_tpB = [0, 0]
                for h in range(H):
                    bk = (b4 if h < 4 else b5)
                    bsl = bk[:, 64 * (h % 4):64 * (h % 4) + 64].bitcast(BF16)
                    pr.op("tensor", lambda E, h=h, bsl=bsl, lcq0=lcq0:
                          E.transpose(bsl,
                                      nT[:, h, 64 * lcq0:64 * lcq0 + 128],
                                      id_bf[:]),
                          inc=pe if h % 4 == 3 else None)
                    if h == 3:
                        v_tpB[0] = pe.n
                v_tpB[1] = pe.n
                if tq == 0:
                    nT0_pe["v"] = v_tpB[1]
                for h in range(H):
                    bk = (b4 if h < 4 else b5)
                    bsl = bk[:, 64 * (h % 4):64 * (h % 4) + 64].bitcast(BF16)
                    if h % 4 == 0:
                        pr.wait("vector", pe, v_tpB[h // 4])
                    pr.sop("vector", lambda E, h=h, bsl=bsl, tq=tq:
                           E.tensor_scalar(
                               wkbeg[:, h, :], bsl, CS[:, tq, 16 + h:17 + h],
                               None, ALU.mult))
                    pr.sop("vector", lambda E, h=h, bsl=bsl, tq=tq, tr=tr:
                           E.tensor_scalar(
                               wkdw[:, tr, h, :], bsl, CS[:, tq, 24 + h:25 + h],
                               None, ALU.mult))
                wkdw_done[tr] = pr.pend["vector"]
            def solve_C(q, tq):
                gt = q * TTQ + tq
                tr = gt % 2
                pr.wait("scalar", ve, pr.pend["vector"])
                def xreg(h):
                    bk = pbanks[h // 2]
                    return bk[:, 256 * (h % 2):256 * (h % 2) + 256]

                pr.wait("tensor", ve, pr.pend["vector"])
                v_mm = [0] * H
                vev = [(ve, 0)] * 4

                def evac_round(bank_dsts):
                    for use_ac in (False, True):
                        eng = "scalar" if use_ac else "vector"
                        sem = ac if use_ac else ve
                        pr.wait(eng, sem, pr.pend[eng])
                        banks = [k for k in range(4) if ((k % 2 == 1) == use_ac)]
                        n_ops = sum(len(bank_dsts(k)) for k in banks)
                        k_op = 0
                        for k in banks:
                            pr.wait(eng, pe, v_mm[k * 2 + 1])
                            for dst, srcv in bank_dsts(k):
                                k_op += 1
                                last = (k_op == n_ops)
                                if use_ac:
                                    pr.op(eng, lambda E, dst=dst, srcv=srcv:
                                          E.copy(dst, srcv),
                                          inc=sem if last else None)
                                else:
                                    pr.op(eng, lambda E, dst=dst, srcv=srcv:
                                          E.tensor_copy(dst, srcv),
                                          inc=sem if last else None)
                        pr.pend[eng] = sem.n
                        for k in range(4):
                            if ((k % 2 == 1) == use_ac):
                                vev[k] = (sem, sem.n)

                for h in range(H):
                    reg = xreg(h)
                    pr.op("tensor", lambda E, h=h, reg=reg, tq=tq: E.matmul(
                        reg[:, 0:128], ATbd[:, h, :], vb_tm[:, tq, h, :],
                        start=True, stop=False))
                    pr.op("tensor", lambda E, h=h, reg=reg, tq=tq: E.matmul(
                        reg[:, 0:128], id_bf[:], vb_tm[:, tq, h, :],
                        start=False, stop=True))
                    pr.op("tensor", lambda E, h=h, reg=reg: E.matmul(
                        reg[:, 128:256], ATbd[:, h, :], wkbeg[:, h, :],
                        start=True, stop=False))
                    v_mm[h] = pr.op("tensor", lambda E, h=h, reg=reg: E.matmul(
                        reg[:, 128:256], id_bf[:], wkbeg[:, h, :],
                        start=False, stop=True), inc=pe)
                evac_round(lambda k: [(X_sb[0].ap()[:, 2 * k:2 * k + 2, :]
                                       .rearrange("p a b -> p (a b)"),
                                       pbanks[k][:, 0:512])])
                if stop_after == "solveC0":
                    return
                for j in range(1, 6):
                    for h in range(H):
                        reg = xreg(h)
                        Pp = Abd[:, h, :] if j == 1 else \
                            PQ_sb[(j - 1) % 2][:, h, 0:128]
                        Qp = ATbd[:, h, :] if j == 1 else \
                            PQ_sb[(j - 1) % 2][:, h, 128:256]
                        pr.wait("tensor", vev[h // 2][0], vev[h // 2][1])
                        if j < 5:
                            pr.op("tensor", lambda E, reg=reg, Pp=Pp, Qp=Qp:
                                  E.matmul(reg[:, 0:128], Qp, Pp,
                                           start=True, stop=True))
                        v_mm[h] = pr.op("tensor", lambda E, reg=reg, Pp=Pp, Qp=Qp:
                                        E.matmul(reg[:, 128:256], Pp, Qp,
                                                 start=True, stop=True), inc=pe)
                    evac_round(lambda k, j=j: (
                        [(PQ_sb[j % 2].ap()[:, 2 * k:2 * k + 2, :]
                          .rearrange("p a b -> p (a b)"),
                          pbanks[k][:, 0:512])] if j < 5 else
                        [(PQ_sb[j % 2][:, 2 * k, 128:256],
                          pbanks[k][:, 128:256]),
                         (PQ_sb[j % 2][:, 2 * k + 1, 128:256],
                          pbanks[k][:, 384:512])]))
                    for h in range(H):
                        reg = xreg(h)
                        pr.wait("tensor", vev[h // 2][0], vev[h // 2][1])
                        pr.op("tensor", lambda E, h=h, reg=reg, j=j: E.matmul(
                            reg[:], PQ_sb[j % 2][:, h, 128:256],
                            X_sb[(j - 1) % 2][:, h, :],
                            start=True, stop=False))
                        v_mm[h] = pr.op("tensor", lambda E, h=h, reg=reg, j=j:
                                        E.matmul(reg[:], id_bf[:],
                                                 X_sb[(j - 1) % 2][:, h, :],
                                                 start=False, stop=True), inc=pe)
                    if j < 5:
                        evac_round(lambda k, j=j: [
                            (X_sb[j % 2].ap()[:, 2 * k:2 * k + 2, :]
                             .rearrange("p a b -> p (a b)"),
                             pbanks[k][:, 0:512])])
                    else:
                        evac_round(lambda k: [
                            (Xf.ap()[:, 2 * k:2 * k + 2, :]
                             .rearrange("p a b -> p (a b)"),
                             pbanks[k][:, 0:512])])
                if stop_after == "solveCJ":
                    return
                pr.wait("tensor", ve, pr.pend["vector"])
                pr.wait("tensor", ac, pr.pend["scalar"])
                v_tpW = [0, 0]
                for h in range(H):
                    bk = (b4 if h < 4 else b5)
                    bsl = bk[:, 64 * (h % 4):64 * (h % 4) + 64].bitcast(BF16)
                    pr.op("tensor", lambda E, h=h, bsl=bsl: E.transpose(
                        bsl, Xf[:, h, 128:256], id_bf[:]),
                        inc=pe if h % 4 == 3 else None)
                    if h == 3:
                        v_tpW[0] = pe.n
                v_tpW[1] = pe.n
                pr.wait("vector", pe, v_tpW[0])
                pr.sop("vector", lambda E: E.tensor_copy(
                    wkcT.ap()[:, 0:4, :].rearrange("p a b -> p (a b)"),
                    b4.ap()[:, 0:256].bitcast(BF16)))
                pr.wait("vector", pe, v_tpW[1])
                pr.sop("vector", lambda E: E.tensor_copy(
                    wkcT.ap()[:, 4:8, :].rearrange("p a b -> p (a b)"),
                    b5.ap()[:, 0:256].bitcast(BF16)))

            def rms_tile(q, tq):
                gt = q * TTQ + tq
                for lc in (0, 1):
                    gc = 2 * gt + lc
                    oc = (C * gc) % TOWN
                    second = (C * gc) >= TOWN
                    va = pr.sop("scalar", lambda E, lc=lc: E.activation(
                        o2_sb.ap()[:, :, :].rearrange("p a b -> p (a b)"),
                        ostage.ap()[:, lc, :, :].rearrange("p a b -> p (a b)"),
                        AF.Square))
                    pr.wait("tensor", ac, va)
                    pr.wait("tensor", ve, pr.pend["vector"])
                    pr.op("tensor", lambda E: E.matmul(
                        b7[0:1, 0:256], ones_128x1[:],
                        o2_sb.ap()[:, 0:4, :].rearrange("p a b -> p (a b)"),
                        start=True, stop=True))
                    v_ss = pr.op("tensor", lambda E: E.matmul(
                        b7[0:1, 256:512], ones_128x1[:],
                        o2_sb.ap()[:, 4:8, :].rearrange("p a b -> p (a b)"),
                        start=True, stop=True), inc=pe)
                    pr.wait("vector", pe, v_ss)
                    v9 = pr.sop("vector", lambda E: E.tensor_scalar(
                        rst_row[:], b7[0:1, 0:512], 1.0 / K, EPS_RMS,
                        ALU.mult, ALU.add))
                    pr.wait("scalar", ve, v9)
                    pr.sop("scalar", lambda E: E.activation(
                        rst_row[:], rst_row[:], AF.Ln))
                    v11 = pr.sop("scalar", lambda E: E.activation(
                        rst_row[:], rst_row[:], AF.Exp, scale=-0.5))
                    pr.wait("tensor", ve, v9)
                    pr.wait("tensor", ac, v11)
                    v12 = pr.op("tensor", lambda E: E.matmul(
                        b7[:], ones_1x128[:], rst_row[:], start=True, stop=True),
                        inc=pe)
                    pr.wait("vector", pe, v12)
                    pr.sop("vector", lambda E: E.tensor_copy(rstdbc[:], b7[:]))
                    if not second:
                        pr.sop("vector", lambda E, lc=lc, gc=gc, oc=oc:
                               E.scalar_tensor_tensor(
                                   o_normT.ap()[:, :, oc:oc + C],
                                   ostage.ap()[:, lc, :, :],
                                   nwm_sb[:, gc:gc + 1],
                                   rstdbc.ap()[:, :].rearrange("p (a b) -> p a b", a=8),
                                   ALU.mult, ALU.mult))
                    else:
                        pr.sop("vector", lambda E, lc=lc, gc=gc:
                               E.scalar_tensor_tensor(
                                   onrm_t.ap()[:, :, :],
                                   ostage.ap()[:, lc, :, :],
                                   nwm_sb[:, gc:gc + 1],
                                   rstdbc.ap()[:, :].rearrange("p (a b) -> p a b", a=8),
                                   ALU.mult, ALU.mult))
                        pr.sop("vector", lambda E, oc=oc: E.tensor_add(
                            o_normT.ap()[:, :, oc:oc + C],
                            o_normT.ap()[:, :, oc:oc + C],
                            onrm_t.ap()[:, :, :]))

            def scan_chunk(q, tq, lc):
                gt = q * TTQ + tq
                tr = gt % 2
                gc = 2 * gt + lc
                lcq = 2 * tq + lc
                oc = (C * gc) % TOWN
                second = (C * gc) >= TOWN
                pr.wait("vector", pe, pe.n)
                pr.wait("scalar", pe, pe.n)
                pr.wait("vector", ve, pr.pend["vector"])
                v_sbf = pr.op("vector", lambda E: E.tensor_copy(
                    Sbf8.ap()[:, :, :].rearrange("p a b -> p (a b)"),
                    S32.ap()[:, :, :].rearrange("p a b -> p (a b)")), inc=ve)
                pr.pend["vector"] = v_sbf
                pr.wait("tensor", ve, v_sbf)
                for h in range(H):
                    bk = b4 if h < 4 else b5
                    pr.op("tensor", lambda E, h=h, bk=bk, tr=tr, lc=lc: E.matmul(
                        bk[64 * lc:64 * lc + 64, 128 * (h % 4):128 * (h % 4) + 128],
                        wkcT[:, h, 64 * lc:64 * lc + 64], Sbf8[:, h, :],
                        start=True, stop=True, tile_position=(0, 64 * lc)),
                        inc=pe if h % 4 == 3 else None)
                v_vn_mm = pe.n
                pr.wait("vector", pe, v_vn_mm)
                pr.wait("vector", ve, pr.pend["vector"])
                pr.op("vector", lambda E, lc=lc: E.tensor_sub(
                    v_new8.ap()[64 * lc:64 * lc + 64, 0:4, :],
                    Xf.ap()[64 * lc:64 * lc + 64, 0:4, 0:128],
                    b4.ap()[64 * lc:64 * lc + 64, :].rearrange(
                        "p (a b) -> p a b", a=4)))
                v_vnew = pr.op("vector", lambda E, lc=lc: E.tensor_sub(
                    v_new8.ap()[64 * lc:64 * lc + 64, 4:8, :],
                    Xf.ap()[64 * lc:64 * lc + 64, 4:8, 0:128],
                    b5.ap()[64 * lc:64 * lc + 64, :].rearrange(
                        "p (a b) -> p a b", a=4)), inc=ve)
                pr.pend["vector"] = v_vnew
                pr.wait("tensor", ve, b6_guard["v"])
                for h in range(H):
                    pr.op("tensor", lambda E, h=h, lcq=lcq: E.matmul(
                        b6[:, 64 * h:64 * h + 64],
                        colsel[:, h * 128:(h + 1) * 128],
                        egvrow[0:8, 64 * lcq:64 * lcq + 64],
                        start=True, stop=True),
                        inc=pe if h == 7 else None)
                v_egb = pe.n
                pr.wait("vector", pe, v_egb)
                pr.wait("vector", ve, pr.pend["vector"])
                v_rke = pr.op("vector", lambda E, lcq=lcq: E.tensor_mul(
                    rk_eT8.ap()[:, :, :],
                    nT.ap()[:, :, 64 * lcq + 1:64 * lcq + 65],
                    b6.ap()[:, :].rearrange("p (a b) -> p a b", a=8)), inc=ve)
                pr.pend["vector"] = v_rke
                b6_guard["v"] = v_rke
                pr.wait("tensor", ve, v_rke)
                for h in range(H):
                    bk = b4 if h < 4 else b5
                    sl = bk[:, 64 * (h % 4):64 * (h % 4) + 64]
                    pr.op("tensor", lambda E, h=h, sl=sl: E.matmul(
                        sl, Sbf8[:, h, :], rk_eT8[:, h, :],
                        start=True, stop=False))
                    pr.op("tensor", lambda E, h=h, sl=sl, tr=tr, lc=lc: E.matmul(
                        sl, v_new8[64 * lc:64 * lc + 64, h, :],
                        attnT_r[64 * lc:64 * lc + 64, tr, h, :],
                        start=False, stop=True),
                        inc=pe if h % 4 == 3 else None)
                v_oT = pe.n
                pr.wait("vector", pe, v_oT)
                pr.wait("scalar", pe, v_oT)
                v_ev4 = pr.sop("vector", lambda E, lc=lc: E.tensor_copy(
                    ostage.ap()[:, lc, 0:4, :].rearrange("p a b -> p (a b)"),
                    b4[:, 0:256]))
                v_ev5 = pr.sop("scalar", lambda E, lc=lc: E.copy(
                    ostage.ap()[:, lc, 4:8, :].rearrange("p a b -> p (a b)"),
                    b5[:, 0:256]))
                pr.wait("tensor", ve, v_ev4)
                pr.wait("tensor", ac, v_ev5)
                pr.wait("tensor", ve, wkdw_done[tr])
                for h in range(H):
                    bk = b4 if h < 4 else b5
                    pr.op("tensor", lambda E, h=h, bk=bk, tr=tr, lc=lc: E.matmul(
                        bk[:, 128 * (h % 4):128 * (h % 4) + 128],
                        wkdw[64 * lc:64 * lc + 64, tr, h, :],
                        v_new8[64 * lc:64 * lc + 64, h, :],
                        start=True, stop=True),
                        inc=pe if h % 4 == 3 else None)
                v_U = pe.n
                pr.wait("vector", pe, v_U)
                pr.wait("vector", ve, pr.pend["vector"])
                for h in range(H):
                    bk = b4 if h < 4 else b5
                    pr.op("vector", lambda E, h=h, bk=bk, gc=gc:
                          E.scalar_tensor_tensor(
                              S32[:, h, :], S32[:, h, :],
                              eglb[:, h, gc:gc + 1],
                              bk[:, 128 * (h % 4):128 * (h % 4) + 128],
                              ALU.mult, ALU.add),
                          inc=ve if h == 7 else None)
                pr.pend["vector"] = ve.n

            vpath_done = [0, 0]
            nT_done_st = {"v": 0}
            nT0_pe = {"v": 0}
            for q in range(NQ):
                qs = q * QT
                xT_bf = xT_bf2[q % 2]
                if q > 0:
                    pr.wait("vector", ac, nT_done_st["v"])
                    pr.wait("vector", pe, nT0_pe["v"])
                    halo["n"] = pr.sop("vector", lambda E: E.tensor_copy(
                        nT[:, :, 0:1], nT[:, :, QT:QT + 1]))
                    halo["xv"] = pr.sop("vector", lambda E: E.tensor_copy(
                        xvT[:, :, 0:3], xvT[:, :, QT:QT + 3]))
                pr.wait("gpsimd", pe, vpath_done[q % 2])
                v_xTbf = dma("gpsimd", lg, xT_bf[:],
                             dram["xT"].rearrange("(dt p) t -> p dt t", p=128)
                             [:, :, qs:qs + QT])

                pr.wait("tensor", lg, v_xTbf)
                pr.wait("tensor", lg, v_lg_wv)
                pr.wait("tensor", ve, b3_done["v"])
                for dv in range(8):
                    for i5 in range(QT // 512):
                        for dt in range(8):
                            pr.op("tensor", lambda E, dv=dv, dt=dt, i5=i5,
                                  xT_bf=xT_bf: E.matmul(
                                b3[:], wv_sb[:, dt, dv * 128:(dv + 1) * 128],
                                xT_bf[:, dt, i5 * 512:(i5 + 1) * 512],
                                start=(dt == 0), stop=(dt == 7)),
                                inc=pe if dt == 7 else None)
                        v_mm = pe.n
                        pr.wait("vector", pe, v_mm)
                        if q > 0 and dv == 0 and i5 == 0:
                            pr.wait("vector", ve, halo["xv"])
                        v_evac = pr.sop("vector", lambda E, dv=dv, i5=i5:
                                        E.tensor_copy(
                                            xvT[:, dv, 3 + 512 * i5:515 + 512 * i5],
                                            b3[:]))
                        pr.wait("tensor", ve, v_evac)
                        rr = 0
                        pr.wait("vector", ac, pr.pend["scalar"])
                        pr.sop("vector", lambda E, dv=dv, i5=i5, rr=rr: E.tensor_scalar(
                            cv_r[:, rr, :], xvT[:, dv, 512 * i5:512 * i5 + 512],
                            cw_sb[:, dv, 0:1], cb_sb[:, dv:dv + 1],
                            ALU.mult, ALU.add))
                        for j in (1, 2, 3):
                            pr.sop("vector", lambda E, dv=dv, i5=i5, rr=rr, j=j:
                                   E.scalar_tensor_tensor(
                                       cv_r[:, rr, :],
                                       xvT[:, dv, j + 512 * i5:j + 512 * i5 + 512],
                                       cw_sb[:, dv, j:j + 1], cv_r[:, rr, :],
                                       ALU.mult, ALU.add))
                        v_cv = pr.pend["vector"]
                        pr.wait("scalar", ve, v_cv)
                        pr.sop("scalar", lambda E, dv=dv, i5=i5, rr=rr: E.activation(
                            v_fm[:, dv, 512 * i5:512 * i5 + 512], cv_r[:, rr, :],
                            AF.Silu))
                vpath_done[q % 2] = pe.n

                for tq in range(TTQ):
                    r = tq % 2
                    ts0 = qs + tq * 128
                    pr.wait("sync", ve, prep_guard[r]["ve"])
                    pr.wait("sync", ac, prep_guard[r]["ac"])
                    pr.wait("sync", pe, prep_guard[r]["pe"])
                    v_x = dma("sync", ld, x_tm_r[:, r, :], xtm[ts0:ts0 + 128, :])
                    v_xT = dma("sync", ld, xT32_r[:, r, :, :],
                               dram["xT"].rearrange("(dt p) t -> p dt t", p=128)
                               [:, :, ts0:ts0 + 128])
                    pr.wait("scalar", ld, v_x)
                    pr.wait("scalar", ve, pr.pend["vector"])
                    va = pr.sop("scalar", lambda E, r=r: E.activation(
                        scrA[:, 0, :], x_tm_r[:, r, :], AF.Square))
                    v_sq = va
                    pr.wait("vector", ac, va)
                    pr.sop("vector", lambda E, r=r: E.reduce_sum(
                        ss8[:, r, :],
                        scrA.ap()[:, 0, :].rearrange("p (h k) -> p h k", h=H),
                        axis=AX.X))
                    v1 = pr.sop("vector", lambda E, r=r: E.tensor_scalar(
                        ss8[:, r, :], ss8[:, r, :], EPS_L2, None, ALU.add))
                    pr.wait("scalar", ve, v1)
                    pr.sop("scalar", lambda E, r=r: E.activation(
                        rstd8[:, r, :], ss8[:, r, :], AF.Ln))
                    va = pr.sop("scalar", lambda E, r=r: E.activation(
                        rstd8[:, r, :], rstd8[:, r, :], AF.Exp, scale=-0.5))
                    pr.wait("vector", ac, va)
                    v_ntm = pr.sop("vector", lambda E, r=r: E.tensor_mul(
                        n_tm_r.ap()[:, r, :].rearrange("p (h k) -> p h k", h=H),
                        x_tm_r.ap()[:, r, :].rearrange("p (h k) -> p h k", h=H),
                        rstd8.ap()[:, r, :].unsqueeze(2).broadcast_to([128, H, 128])))
                    pr.wait("tensor", ve, v_ntm)
                    pr.wait("tensor", ac, pr.pend["scalar"])
                    v_tp = [0, 0]
                    for h in range(H):
                        bk = b4 if h < 4 else b5
                        bsl = bk[:, 64 * (h % 4):64 * (h % 4) + 64].bitcast(BF16)
                        pr.op("tensor", lambda E, r=r, h=h, bsl=bsl: E.transpose(
                            bsl, n_tm_r[:, r, h * 128:(h + 1) * 128], id_bf[:]),
                            inc=pe if h % 4 == 3 else None)
                        if h == 3:
                            v_tp[0] = pe.n
                    v_tp[1] = pe.n
                    if q > 0 and tq == 0:
                        pr.wait("scalar", ve, halo["n"])
                    pr.wait("scalar", pe, v_tp[0])
                    pr.sop("scalar", lambda E, tq=tq: E.copy(
                        nT.ap()[:, 0:4, tq * 128 + 1:tq * 128 + 129],
                        b4.ap()[:, 0:256].bitcast(BF16).rearrange(
                            "p (a b) -> p a b", a=4)))
                    pr.wait("scalar", pe, v_tp[1])
                    pr.sop("scalar", lambda E, tq=tq: E.copy(
                        nT.ap()[:, 4:8, tq * 128 + 1:tq * 128 + 129],
                        b5.ap()[:, 0:256].bitcast(BF16).rearrange(
                            "p (a b) -> p a b", a=4)))
                    v_nT_done = pr.pend["scalar"]
                    pr.wait("tensor", ac, v_nT_done)
                    pr.wait("tensor", ld, v_xT)
                    pr.wait("tensor", ld, v_cst)
                    pr.wait("tensor", ve, v_b7_free)
                    pr.wait("tensor", ve, pr.pend["vector"])
                    pr.wait("tensor", ac, pr.pend["scalar"])
                    for dt in range(8):
                        pr.op("tensor", lambda E, r=r, dt=dt: E.matmul(
                            b7[:, 0:16], xT32_r[:, r, dt, :], wab_sb[:, dt, :],
                            start=(dt == 0), stop=(dt == 7)),
                            inc=pe if dt == 7 else None)
                    v_xab = pe.n
                    pr.wait("vector", pe, v_xab)
                    r2 = r
                    v_b7_free = pr.sop("vector", lambda E, r2=r2: E.tensor_copy(
                        sp16[:, r2, :], b7[:, 0:16]))
                    pr.sop("vector", lambda E, r2=r2: E.tensor_add(
                        spw[:, r2, 0:8], sp16[:, r2, 0:8], dtb_sb[:]))
                    v1 = pr.sop("vector", lambda E, r2=r2: E.tensor_scalar(
                        spw[:, r2, 8:16], spw[:, r2, 0:8], 0.0, None, ALU.max))
                    pr.wait("scalar", ve, v1)
                    pr.sop("scalar", lambda E, r2=r2: E.activation(
                        spw[:, r2, 16:24], spw[:, r2, 0:8], AF.Abs))
                    pr.sop("scalar", lambda E, r2=r2: E.activation(
                        spw[:, r2, 16:24], spw[:, r2, 16:24], AF.Exp, scale=-1.0))
                    va = pr.sop("scalar", lambda E, r2=r2: E.activation(
                        spw[:, r2, 16:24], spw[:, r2, 16:24], AF.Ln, bias=1.0))
                    pr.wait("vector", ac, va)
                    pr.sop("vector", lambda E, r2=r2: E.tensor_add(
                        spw[:, r2, 0:8], spw[:, r2, 8:16], spw[:, r2, 16:24]))
                    v1 = pr.sop("vector", lambda E, r2=r2: E.tensor_mul(
                        spw[:, r2, 0:8], spw[:, r2, 0:8], nexpA[:]))
                    va = pr.sop("scalar", lambda E, r2=r2, tq=tq: E.activation(
                        CS[:, tq, 8:16], sp16[:, r2, 8:16], AF.Sigmoid))
                    pr.wait("tensor", ve, v1)
                    pr.wait("tensor", ac, va)
                    pr.op("tensor", lambda E, r2=r2: E.matmul(
                        b6[:, 0:8], bd_trilT[:], spw[:, r2, 0:8],
                        start=True, stop=True))
                    vp = pr.op("tensor", lambda E, r2=r2: E.matmul(
                        b6[:, 8:16], bd_ones[:], spw[:, r2, 0:8],
                        start=True, stop=True), inc=pe)
                    pr.wait("vector", pe, vp)
                    pr.sop("vector", lambda E, tq=tq: E.tensor_copy(
                        CS[:, tq, 0:8], b6[:, 0:8]))
                    v1 = pr.sop("vector", lambda E, r2=r2, tq=tq: E.tensor_sub(
                        spw[:, r2, 8:16], b6[:, 8:16], CS[:, tq, 0:8]))
                    pr.wait("scalar", ve, v1)
                    pr.sop("scalar", lambda E, r2=r2, tq=tq: E.activation(
                        CS[:, tq, 24:32], spw[:, r2, 8:16], AF.Exp))
                    pr.sop("scalar", lambda E, r2=r2, tq=tq: E.activation(
                        spw[:, r2, 16:24], CS[:, tq, 0:8], AF.Exp))
                    va = pr.sop("scalar", lambda E, r2=r2: E.activation(
                        RS[:, r2, 24:32], b6[:, 8:16], AF.Exp))
                    v_b6_free = va
                    pr.wait("vector", ac, va)
                    pr.sop("vector", lambda E, r2=r2, tq=tq: E.tensor_mul(
                        CS[:, tq, 16:24], CS[:, tq, 8:16], spw[:, r2, 16:24]))
                    pr.sop("vector", lambda E, r2=r2, tq=tq: E.tensor_copy(
                        RS[:, r2, 0:8], CS[:, tq, 0:8]))
                    pr.sop("vector", lambda E, r2=r2, tq=tq: E.tensor_scalar(
                        RS[:, r2, 8:16], CS[:, tq, 8:16], -1.0, None, ALU.mult))
                    v_rs = pr.sop("vector", lambda E, r2=r2: E.tensor_scalar(
                        RS[:, r2, 16:24], spw[:, r2, 16:24], RSQK, None, ALU.mult))
                    pr.wait("tensor", ve, v_rs)
                    pr.wait("tensor", ac, v_b6_free)
                    rows4 = [(grow, 0), (nbrow, 8), (egvrow, 16), (eglrow, 24)]
                    for gi, (rowt, off) in enumerate(rows4):
                        pr.op("tensor", lambda E, r2=r2, off=off, gi=gi: E.transpose(
                            b6[0:8, 128 * gi:128 * gi + 128],
                            RS[:, r2, off:off + 8], id_f32[:]),
                            inc=pe if gi == 3 else None)
                    v_rt = pe.n
                    pr.wait("scalar", pe, v_rt)
                    ts0q = ts0 - qs
                    for gi, (rowt, off) in enumerate(rows4):
                        pr.sop("scalar", lambda E, rowt=rowt, gi=gi, ts0q=ts0q: E.copy(
                            rowt[0:8, ts0q:ts0q + 128],
                            b6[0:8, 128 * gi:128 * gi + 128]))
                    v_rows = pr.pend["scalar"]
                    pr.wait("tensor", ac, v_rows)
                    pr.wait("tensor", ve, pr.pend["vector"])
                    pr.op("tensor", lambda E, tq=tq: E.matmul(
                        b6[:, 0:32], dup64[:, 0:128], CS[:, tq, :],
                        start=True, stop=True))
                    vdp = pr.op("tensor", lambda E, tq=tq: E.matmul(
                        b6[:, 32:64], dup64[:, 128:256], CS[:, tq, :],
                        start=True, stop=True), inc=pe)
                    pr.wait("vector", pe, vdp)
                    pr.sop("vector", lambda E, tq=tq: E.tensor_copy(
                        CSd[:, tq, 0, :], b6[:, 0:32]))
                    v_csd = pr.sop("vector", lambda E, tq=tq: E.tensor_copy(
                        CSd[:, tq, 1, :], b6[:, 32:64]))
                    pr.wait("tensor", ve, v_csd)
                    prep_guard[r] = {"ac": v_sq, "ve": v_ntm, "pe": v_xab}
                nT_done_st["v"] = v_nT_done

                ch0 = qs // C
                nchq = QT // C
                for h in range(H):
                    vp = pr.op("tensor", lambda E, h=h, nchq=nchq: E.matmul(
                        b6[:, 0:nchq], colsel[:, h * 128:(h + 1) * 128],
                        eglrow[0:8, 0:QT:64], start=True, stop=True), inc=pe)
                    pr.wait("vector", pe, vp)
                    vv = pr.sop("vector", lambda E, h=h, ch0=ch0, nchq=nchq:
                                E.tensor_copy(
                                    eglb[:, h, ch0:ch0 + nchq], b6[:, 0:nchq]))
                    pr.wait("tensor", ve, vv)

                if q == 0:
                    dbg_dump("eglb2", eglb[:])
                v_vfm = pr.pend["scalar"]
                pr.wait("tensor", ac, v_vfm)
                pr.wait("tensor", ve, pr.pend["vector"])
                for tq in range(TTQ):
                    v_tpv = [0, 0]
                    for h in range(H):
                        bk = b4 if h < 4 else b5
                        bsl = bk[:, 64 * (h % 4):64 * (h % 4) + 64].bitcast(BF16)
                        pr.op("tensor", lambda E, h=h, tq=tq, bsl=bsl:
                              E.transpose(bsl,
                                          v_fm[:, h, tq * 128:(tq + 1) * 128],
                                          id_bf[:]),
                              inc=pe if h % 4 == 3 else None)
                        if h == 3:
                            v_tpv[0] = pe.n
                    v_tpv[1] = pe.n
                    for h in range(H):
                        bk = b4 if h < 4 else b5
                        bsl = bk[:, 64 * (h % 4):64 * (h % 4) + 64].bitcast(BF16)
                        if h % 4 == 0:
                            pr.wait("vector", pe, v_tpv[h // 4])
                        pr.sop("vector", lambda E, h=h, tq=tq, bsl=bsl:
                               E.tensor_scalar(
                                   vb_tm[:, tq, h, :], bsl,
                                   CS[:, tq, 8 + h:9 + h], None, ALU.mult))
                    pr.wait("tensor", ve, pr.pend["vector"])

                if q == 0:
                    dbg_dump("eglrow", eglrow[:])
                    dbg_dump("egvrow", egvrow[:])
                    dbg_dump("grow", grow[:])
                    dbg_dump("nT", nT[:])
                    dbg_dump("CS", CS[:])
                    dbg_dump("vb", vb_tm[:])
                    dbg_dump("vfm", v_fm[:])

                b6_guard["v"] = pr.pend["vector"]
                for tq in range(TTQ):
                    if stop_after == "prep":
                        break
                    solve_AB(q, tq)
                    if stop_after in ("solveA", "solveB"):
                        continue
                    if tq > 0:
                        for lc in (0, 1):
                            scan_chunk(q, tq - 1, lc)
                        rms_tile(q, tq - 1)
                    solve_C(q, tq)
                    if q == 0 and tq == 0:
                        dbg_dump("mdiff", mdiff[:])
                        dbg_dump("e1t", e1t[:])
                        dbg_dump("t2m", t2m[:])
                        dbg_dump("Abd", Abd[:])
                        dbg_dump("ATbd", ATbd[:])
                        dbg_dump("Xf0", Xf[:])
                        dbg_dump("attnT0", attnT_r[:])
                        dbg_dump("wkcT0", wkcT[:])
                        dbg_dump("wkdw0", wkdw[:])
                    if stop_after in ("solve", "solveA", "solveB", "solveC0",
                                      "solveCJ"):
                        continue
                    if tq == TTQ - 1:
                        for lc in (0, 1):
                            scan_chunk(q, tq, lc)
                        rms_tile(q, tq)
                    if q == 0 and tq == 0:
                        dbg_dump("S0", S32[:])
                        dbg_dump("vnew0", v_new8[:])
                        dbg_dump("eglb", eglb[:])
                    if "Strace" in dbg_out:
                        gti = q * TTQ + tq
                        dbg_dump_noname = None
                        barrier_all("sync")
                        dma("sync", st, dbg_out["Strace"][:, gti, :, :], S32[:])

            dbg_dump("onrm", o_normT[:])
            if stop_after in ("prep", "solve", "scan", "solveA", "solveB",
                              "solveC0", "solveCJ"):
                barrier_all("vector")
                vz = pr.op("vector", lambda E: E.memset(scrA[:, 0, :], 0.0),
                           inc=ve)
                pr.wait("sync", ve, vz)
                for tt in range(NTT):
                    dma("sync", st,
                        out_y.rearrange("(tt p) d -> p tt d", p=128)[:, tt, :],
                        scrA[:, 0, :])
                pr.wait("sync", st, st.n)
                pr.flush()
                sA2.close()
                sA1.close()
                return nc
            pr.flush()
            sA2.close()

            yacc = sb("yacc", [128, NTT, D], F32)
            y1T = sb("y1T", [128, 8, TOWN], BF16)
            wga_sb = sb1("wga_sb", [128, 8, D], BF16)
            wo_sb = sb1("wo_sb", [128, 8, D], BF16)
            xTc_r = sb1("xTc_r", [128, 8, 512], BF16)
            og_st = sb1("og_st", [128, 8, 512], BF16)
            xo_r = sb1("xo_r", [128, 2, D], F32)
            gsl_r = sb1("gsl_r", [128, 2, 512], BF16)
            sq_c = sb1("sq_c", [128, 512], F32)

            barrier_all("gpsimd")
            barrier_all("sync")
            v_wga = dma("gpsimd", lg, wga_sb[:],
                        dram["wgaT"].rearrange("(dt p) dd -> p dt dd", p=128))
            v_wo = dma("gpsimd", lg, wo_sb[:],
                       dram["woT"].rearrange("(dt p) dd -> p dt dd", p=128))
            barrier_all("tensor")
            pr.wait("tensor", lg, v_wo)
            gsl_g = [0, 0]
            res_g = [0, 0]
            for t5 in range(NT512):
                pr.wait("gpsimd", ve, pr.pend["vector"])
                v_xTc = dma("gpsimd", lg, xTc_r[:, :, 0:T5W],
                            dram["xT_own"].rearrange("(dt p) t -> p dt t", p=128)
                            [:, :, t5 * T5W:(t5 + 1) * T5W])
                pr.wait("tensor", lg, v_xTc)
                for hd in range(8):
                    gb = pbanks[hd % 2]
                    pr.wait("tensor", ve, gsl_g[hd % 2])
                    for dt in range(8):
                        pr.op("tensor", lambda E, hd=hd, dt=dt, gb=gb: E.matmul(
                            gb[:, 0:T5W], wga_sb[:, dt, hd * 128:(hd + 1) * 128],
                            xTc_r[:, dt, 0:T5W], start=(dt == 0), stop=(dt == 7)),
                            inc=pe if dt == 7 else None)
                    v_g = pe.n
                    pr.wait("scalar", pe, v_g)
                    va = pr.sop("scalar", lambda E, hd=hd, gb=gb: E.activation(
                        gsl_r[:, hd % 2, 0:T5W], gb[:, 0:T5W], AF.Silu))
                    pr.wait("vector", ac, va)
                    vv = pr.sop("vector", lambda E, hd=hd, t5=t5: E.tensor_mul(
                        og_st[:, hd, 0:T5W], gsl_r[:, hd % 2, 0:T5W],
                        o_normT[:, hd, t5 * T5W:(t5 + 1) * T5W]))
                    gsl_g[hd % 2] = vv
                v_og = pr.pend["vector"]
                if t5 == 0:
                    dbg_dump("og0", og_st[:])
                pr.wait("tensor", ve, v_og)
                for sub in range(SUBS):
                    tt = t5 * SUBS + sub
                    rr = sub % 2
                    pr.wait("sync", ve, res_g[rr])
                    v_xo = dma("sync", ld, xo_r[:, rr, :],
                               x_own[tt * 128:(tt + 1) * 128, :])
                    for half in range(2):
                        bk = pbanks[2 + half]
                        pr.wait("tensor", ve, res_g[half])
                        for f in range(8):
                            pr.op("tensor", lambda E, f=f, sub=sub, half=half, bk=bk:
                                  E.matmul(bk[:],
                                           og_st[:, f, sub * 128:(sub + 1) * 128],
                                           wo_sb[:, f, half * 512:(half + 1) * 512],
                                           start=(f == 0), stop=(f == 7)),
                                  inc=pe if f == 7 else None)
                        v_wo_mm = pe.n
                        pr.wait("vector", pe, v_wo_mm)
                        pr.wait("vector", ld, v_xo)
                        res_g[half] = pr.sop(
                            "vector", lambda E, tt=tt, half=half, rr=rr, bk=bk:
                            E.tensor_add(yacc[:, tt, half * 512:(half + 1) * 512],
                                         xo_r[:, rr, half * 512:(half + 1) * 512],
                                         bk[:]))
                    ln_tile(yacc[:, tt, :], yacc[:, tt, :], sq_c[:])
            dbg_dump("yacc1", yacc[:])
            yacc_ready = (ve, pr.pend["vector"])
            pr.flush()
            sA1.close()
        else:
            yacc = sb("yacc", [128, NTT, D], F32)
            y1T = sb("y1T", [128, 8, TOWN], BF16)
            v_x = dma("sync", ld, yacc[:],
                      x_own.rearrange("(tt p) d -> p tt d", p=128))
            yacc_ready = (ld, v_x)

        scr1 = sb("scr1", [128, 512], F32)
        wgq = [sb("wgq_a", [128, 8, 1024], BF16), sb("wgq_b", [128, 8, 1024], BF16)]
        wuq = [sb("wuq_a", [128, 8, 1024], BF16), sb("wuq_b", [128, 8, 1024], BF16)]
        wdq = sb("wdq", [128, 8, 1024], BF16)
        hstage = sb("hstage", [128, 8, 512], BF16)
        gsil = [sb("gsil_a", [128, 512], BF16), sb("gsil_b", [128, 512], BF16)]

        def issue_wgu(e, hq, buf):
            dma("gpsimd", lg, wgq[buf][:],
                wgateT[e].rearrange("(dt p) Hh -> p dt Hh", p=128)
                [:, :, hq * 1024:(hq + 1) * 1024])
            dma("gpsimd", lg, wuq[buf][:],
                wupT[e].rearrange("(dt p) Hh -> p dt Hh", p=128)
                [:, :, hq * 1024:(hq + 1) * 1024])
            return lg.n

        def issue_wd(e, hq):
            dma("gpsimd", lg, wdq[:],
                wdownT[e][hq * 1024:(hq + 1) * 1024, :]
                .rearrange("(ht p) do -> p ht do", p=128))
            return lg.n

        pr.wait("gpsimd", pe, pe.n)
        pr.wait("gpsimd", ve, pr.pend["vector"])
        pr.wait("gpsimd", ac, pr.pend["scalar"])
        v_wgu0 = issue_wgu(0, 0, 0)
        pr.wait("tensor", yacc_ready[0], yacc_ready[1])
        pr.wait("tensor", ld, v_cst)
        pr.wait("tensor", lg, v_wgt)

        for tt in range(NTT):
            pr.wait("tensor", ac, pr.pend["scalar"])
            v6 = 0
            for dt in range(8):
                bk = b6 if dt < 4 else b2
                pr.op("tensor", lambda E, tt=tt, dt=dt, bk=bk: E.transpose(
                    bk[:, 128 * (dt % 4):128 * (dt % 4) + 128],
                    yacc[:, tt, dt * 128:(dt + 1) * 128], id_f32[:]),
                    inc=pe if dt % 4 == 3 else None)
                if dt == 3:
                    v6 = pe.n
            v7 = pe.n
            pr.wait("scalar", pe, v6)
            pr.sop("scalar", lambda E, tt=tt: E.copy(
                y1T.ap()[:, 0:4, tt * 128:(tt + 1) * 128],
                b6.ap()[:, 0:512].rearrange("p (a b) -> p a b", a=4)))
            pr.wait("scalar", pe, v7)
            vac = pr.sop("scalar", lambda E, tt=tt: E.copy(
                y1T.ap()[:, 4:8, tt * 128:(tt + 1) * 128],
                b2.ap()[:, 0:512].rearrange("p (a b) -> p a b", a=4)))
            pr.wait("tensor", ac, vac)
            for dt in range(8):
                pr.op("tensor",
                      lambda E, tt=tt, dt=dt: E.matmul(
                          b7[:, 0:NE],
                          y1T[:, dt, tt * 128:(tt + 1) * 128],
                          wgT_sb[:, dt, :],
                          start=(dt == 0), stop=(dt == 7)),
                      inc=pe if dt == 7 else None)
            v_log = pe.n
            pr.wait("vector", pe, v_log)
            pr.sop("vector", lambda E: E.tensor_add(l_sb[:], b7[:, 0:NE], bg_sb[:]))
            pr.sop("vector", lambda E: E.reduce_max(lncol[:, 0:1], l_sb[:], axis=AX.X))
            pr.sop("vector", lambda E: E.tensor_scalar(
                t8a[:], l_sb[:], lncol[:, 0:1], None, ALU.is_equal))
            pr.sop("vector", lambda E: E.scalar_tensor_tensor(
                t8b[:], t8a[:], -1e30, l_sb[:], ALU.mult, ALU.add))
            pr.sop("vector", lambda E: E.reduce_max(lncol[:, 1:2], t8b[:], axis=AX.X))
            pr.sop("vector", lambda E: E.tensor_scalar(
                t8a[:], l_sb[:], lncol[:, 1:2], None, ALU.is_ge))
            pr.sop("vector", lambda E: E.tensor_scalar(
                lncol[:, 2:3], lncol[:, 1:2], lncol[:, 0:1], None, ALU.subtract))
            v1 = pr.sop("vector", lambda E: E.tensor_scalar(
                t8b[:], l_sb[:], lncol[:, 0:1], None, ALU.subtract))
            pr.wait("scalar", ve, v1)
            pr.sop("scalar", lambda E: E.activation(lncol[:, 3:4], lncol[:, 2:3], AF.Exp))
            v2 = pr.sop("scalar", lambda E: E.activation(t8b[:], t8b[:], AF.Exp))
            pr.wait("vector", ac, v2)
            pr.sop("vector", lambda E: E.tensor_scalar(
                lncol[:, 4:5], lncol[:, 3:4], 1.0, None, ALU.add))
            pr.sop("vector", lambda E: E.reciprocal(lncol[:, 5:6], lncol[:, 4:5]))
            v3 = pr.sop("vector", lambda E, tt=tt: E.scalar_tensor_tensor(
                wstage[:, tt, :], t8b[:], lncol[:, 5:6], t8a[:], ALU.mult, ALU.mult))
            pr.wait("tensor", ve, v3)

        dbg_dump("wstage", wstage[:])
        v_y1T_done = pr.pend["scalar"]
        v_w_done = pr.pend["vector"]
        v_pe_mark = pe.n
        pr.flush()

        steps = [(e, hq) for e in range(NE) for hq in range(HQ)]
        v_wgu = [None] * len(steps)
        v_wd = [None] * len(steps)
        v_wgu[0] = v_wgu0

        v_step_done = [0] * (len(steps) + 1)
        dacc_guard = [0, 0]
        gsil_guard = [0, 0]
        gbank_guard = [0, 0]
        ubank_guard = [0, 0]
        for si, (e, hq) in enumerate(steps):
            buf = si % 2
            pr.wait("gpsimd", pe, v_step_done[si])
            v_wd[si] = issue_wd(e, hq)
            if si + 1 < len(steps):
                en, hn = steps[si + 1]
                v_wgu[si + 1] = issue_wgu(en, hn, (si + 1) % 2)
            pr.wait("tensor", lg, v_wgu[si])
            if si == 0:
                pr.wait("tensor", ac, v_y1T_done)
            for t5 in range(NT512):
                tok = t5 * T5W
                for ht in range(8):
                    par = ht % 2
                    gb, ub = pbanks[2 * par], pbanks[2 * par + 1]
                    gs = gsil[par]
                    pr.wait("tensor", ac, gbank_guard[par])
                    pr.wait("tensor", ve, ubank_guard[par])
                    for dt in range(8):
                        pr.op("tensor", lambda E, buf=buf, dt=dt, ht=ht, tok=tok, gb=gb:
                              E.matmul(gb[:, 0:T5W],
                                       wgq[buf][:, dt, ht * 128:(ht + 1) * 128],
                                       y1T[:, dt, tok:tok + T5W],
                                       start=(dt == 0), stop=(dt == 7)),
                              inc=pe if dt == 7 else None)
                    vg_pe = pe.n
                    for dt in range(8):
                        pr.op("tensor", lambda E, buf=buf, dt=dt, ht=ht, tok=tok, ub=ub:
                              E.matmul(ub[:, 0:T5W],
                                       wuq[buf][:, dt, ht * 128:(ht + 1) * 128],
                                       y1T[:, dt, tok:tok + T5W],
                                       start=(dt == 0), stop=(dt == 7)),
                              inc=pe if dt == 7 else None)
                    vu_pe = pe.n
                    pr.wait("scalar", pe, vg_pe)
                    pr.wait("scalar", ve, gsil_guard[par])
                    va = pr.sop("scalar", lambda E, gb=gb, gs=gs: E.activation(
                        gs[:, 0:T5W], gb[:, 0:T5W], AF.Silu))
                    gbank_guard[par] = va
                    pr.wait("vector", ac, va)
                    pr.wait("vector", pe, vu_pe)
                    vv = pr.sop("vector", lambda E, ub=ub, gs=gs, ht=ht: E.tensor_mul(
                        hstage[:, ht, 0:T5W], gs[:, 0:T5W], ub[:, 0:T5W]))
                    ubank_guard[par] = vv
                    gsil_guard[par] = vv
                v_h_ready = pr.pend["vector"]
                pr.wait("tensor", ve, v_h_ready)
                pr.wait("tensor", lg, v_wd[si])
                for grp in range(2 * SUBS):
                    sub, half = grp // 2, grp % 2
                    bk = pbanks[4 + grp % 2]
                    pr.wait("tensor", ve, dacc_guard[grp % 2])
                    for ht in range(8):
                        pr.op("tensor", lambda E, ht=ht, sub=sub, half=half, bk=bk:
                              E.matmul(bk[:], hstage[:, ht, sub * 128:(sub + 1) * 128],
                                       wdq[:, ht, half * 512:(half + 1) * 512],
                                       start=(ht == 0), stop=(ht == 7)),
                              inc=pe if ht == 7 else None)
                    vdn = pe.n
                    pr.wait("vector", pe, vdn)
                    tt = t5 * SUBS + sub
                    vv = pr.sop("vector", lambda E, tt=tt, half=half, e=e, bk=bk:
                                E.scalar_tensor_tensor(
                                    yacc[:, tt, half * 512:(half + 1) * 512],
                                    bk[:], wstage[:, tt, e:e + 1],
                                    yacc[:, tt, half * 512:(half + 1) * 512],
                                    ALU.mult, ALU.add))
                    dacc_guard[grp % 2] = vv
            v_step_done[si + 1] = pe.n

        v_ffn_done = pr.pend["vector"]
        pr.wait("scalar", ve, v_ffn_done)
        for tt in range(NTT):
            v_ln = ln_tile(yacc[:, tt, :], yacc[:, tt, :], scr1[:])
            pr.wait("sync", ve, v_ln)
            dma("sync", st, out_y.rearrange("(tt p) d -> p tt d", p=128)[:, tt, :],
                yacc[:, tt, :])
        pr.wait("sync", st, st.n)

        pr.flush()

    return nc



def prep_inputs(inp, T=4096):
    f32 = np.float32
    x = np.asarray(inp["L0_attn_x"], f32)
    Wv = np.asarray(inp["L0_attn_Wv"], f32)
    Wg = np.asarray(inp["L0_attn_Wg"], f32)
    Wo = np.asarray(inp["L0_attn_Wo"], f32)
    Wb = np.asarray(inp["L0_attn_Wb"], f32)
    Wa = np.asarray(inp["L0_attn_Wa"], f32)
    dt_bias = np.asarray(inp["L0_attn_dt_bias"], f32)
    A_log = np.asarray(inp["L0_attn_A_log"], f32)
    norm_w = np.asarray(inp["L0_attn_norm_w"], f32)
    conv_w = np.asarray(inp["L0_attn_conv_w"], f32)
    conv_b = np.asarray(inp["L0_attn_conv_b"], f32)
    wg = np.asarray(inp["L0_ffn_wg"], f32)
    bg = np.asarray(inp["L0_ffn_bg"], f32)
    Wgate = np.asarray(inp["L0_ffn_Wgate"], f32)
    Wup = np.asarray(inp["L0_ffn_Wup"], f32)
    Wdown = np.asarray(inp["L0_ffn_Wdown"], f32)

    TOWN = T // 2
    NCH = T // C
    shared = dict(host_constants())
    shared["wgateT"] = np.ascontiguousarray(Wgate.transpose(0, 2, 1))
    shared["wupT"] = np.ascontiguousarray(Wup.transpose(0, 2, 1))
    shared["wdownT"] = np.ascontiguousarray(Wdown.transpose(0, 2, 1))
    shared["wgT"] = np.ascontiguousarray(wg.T)
    shared["bg_bc"] = np.tile(bg, (128, 1)).astype(f32)
    shared["wvT"] = np.ascontiguousarray(Wv.T)
    shared["wgaT"] = np.ascontiguousarray(Wg.T)
    shared["woT"] = np.ascontiguousarray(Wo.T)
    shared["wabT"] = np.ascontiguousarray(
        np.concatenate([Wa.T, Wb.T], axis=1))
    shared["conv_wD"] = np.ascontiguousarray(
        conv_w.reshape(8, 128, 4).transpose(1, 0, 2).reshape(128, 32))
    shared["conv_bD"] = np.ascontiguousarray(conv_b.reshape(8, 128).T)
    shared["dtb_bc"] = np.tile(dt_bias, (128, 1)).astype(f32)
    shared["alog_row"] = np.ascontiguousarray(A_log[None, :])

    B = x.shape[0]
    in_maps = []
    for core in range(2 * B):
        b = core // 2
        half = core % 2
        m = dict(shared)
        xb = np.ascontiguousarray(x[b])
        m["xtm"] = xb
        m["xT"] = np.ascontiguousarray(xb.T)
        m["x_own"] = np.ascontiguousarray(xb[half * TOWN:(half + 1) * TOWN])
        m["xT_own"] = np.ascontiguousarray(xb.T[:, half * TOWN:(half + 1) * TOWN])
        nm = np.zeros((128, NCH), f32)
        own = np.arange(NCH) // (NCH // 2) == half
        nm[:, own] = norm_w[:, None]
        m["nwm"] = nm
        in_maps.append(m)
    return in_maps


def kernel(**inputs):
    T = inputs["L0_attn_x"].shape[1]
    in_maps = prep_inputs(inputs, T=T)
    nc = build(T=T, skip_attn=False)
    res = run_bass_kernel_spmd(nc, in_maps, core_ids=list(range(NCORES)))
    B = inputs["L0_attn_x"].shape[0]
    TOWN = T // 2
    out = np.empty((B, T, D), np.float32)
    for core in range(NCORES):
        b, half = core // 2, core % 2
        out[b, half * TOWN:(half + 1) * TOWN] = res.results[core]["out_y"]
    return out


# version 16
# speedup vs baseline: 1.1311x; 1.1311x over previous
import numpy as np
import ml_dtypes
import concourse.bass as bass
import concourse.mybir as mybir
from concourse.bass_utils import run_bass_kernel_spmd

F32 = mybir.dt.float32
BF16 = mybir.dt.bfloat16
AF = mybir.ActivationFunctionType
ALU = mybir.AluOpType
AX = mybir.AxisListType

D = 1024
H = 8
K = 128
C = 64
HID = 4096
NE = 8
EPS_L2 = 1e-6
EPS_RMS = 1e-5
EPS_LN = 1e-5
RSQK = float(1.0 / np.sqrt(K))
NCORES = 8
QT = 512


class Sem:
    def __init__(self, handle):
        self.h = handle
        self.n = 0

    def inc(self, k=1):
        self.n += k
        return self.n


class Prog:
    ENGS = ("tensor", "vector", "scalar", "gpsimd", "sync")

    def __init__(self, block, own_sems):
        self.block = block
        self.q = {e: [] for e in self.ENGS}
        self.own = own_sems
        self.pend = {e: 0 for e in self.ENGS}

    def wait(self, eng, sem, val):
        if val > 0:
            self.q[eng].append(lambda E, s=sem, v=val: E.wait_ge(s.h, v))

    def op(self, eng, fn, inc=None, inc_n=1):
        if inc is None:
            self.q[eng].append(fn)
            return None
        v = inc.inc(inc_n)
        def run(E, fn=fn, s=inc, n=inc_n):
            fn(E).then_inc(s.h, n)
        self.q[eng].append(run)
        return v

    def sop(self, eng, fn):
        s = self.own[eng]
        self.wait(eng, s, self.pend[eng])
        v = self.op(eng, fn, inc=s, inc_n=1)
        self.pend[eng] = v
        return v

    def flush(self):
        for name in self.ENGS:
            lst = self.q[name]
            if lst:
                def body(E, lst=lst):
                    for f in lst:
                        f(E)
                getattr(self.block, name)(body)
        self.q = {e: [] for e in self.ENGS}


def host_constants():
    cst = {}
    eye = np.eye(128)
    cst["c_id_bf"] = eye.astype(ml_dtypes.bfloat16)
    cst["c_id_f32"] = eye.astype(np.float32)
    cst["c_ones_1x128"] = np.ones((1, 128), np.float32)
    cst["c_ones_128x1"] = np.ones((128, 1), np.float32)
    tril = np.tril(np.ones((C, C)))
    bd = np.zeros((128, 128)); bd[:C, :C] = tril.T; bd[C:, C:] = tril.T
    cst["c_bd_trilT"] = bd.astype(np.float32)
    bd2 = np.zeros((128, 128)); bd2[:C, :C] = 1.0; bd2[C:, C:] = 1.0
    cst["c_bd_ones"] = bd2.astype(np.float32)
    cst["c_negtril_i"] = np.tile(-np.tril(np.ones((C, C))), (2, 1)).astype(np.float32)
    cst["c_negtril_s"] = np.tile(-np.tril(np.ones((C, C)), -1), (2, 1)).astype(np.float32)
    cst["c_triu_i"] = np.tile(np.triu(np.ones((C, C))), (2, 1)).astype(np.float32)
    cst["c_triu_s"] = np.tile(np.triu(np.ones((C, C)), 1), (2, 1)).astype(np.float32)
    sel = np.zeros((H, 4 * 128))
    for hp in range(4):
        sel[hp, hp * 128:hp * 128 + 64] = 1.0
        sel[hp + 4, hp * 128 + 64:hp * 128 + 128] = 1.0
    cst["c_sel_hh4"] = sel.astype(np.float32)
    dup = np.zeros((128, 256))
    for p in range(128):
        dup[p % 64, p] = 1.0
        dup[64 + p % 64, 128 + p] = 1.0
    cst["c_dup64"] = dup.astype(np.float32)
    colsel = np.zeros((H, H * 128))
    for h in range(H):
        colsel[h, h * 128:(h + 1) * 128] = 1.0
    cst["c_colsel"] = colsel.astype(np.float32)
    return cst


def build(T=4096, skip_attn=False, dbg=(), stop_after=None):
    from contextlib import ExitStack
    nc = bass.Bass("TRN2", target_bir_lowering=False)
    TOWN = T // 2
    NTT = TOWN // 128
    T5W = min(512, TOWN)
    SUBS = T5W // 128
    NT512 = TOWN // T5W
    HQ = HID // 1024
    NQ = T // QT
    TTQ = QT // 128
    NCH = T // C

    dram = {}

    def din(name, shape, dt=F32):
        dram[name] = nc.dram_tensor(name, list(shape), dt, kind="ExternalInput").ap()
        return dram[name]

    for nm, arr in host_constants().items():
        din(nm, arr.shape, BF16 if arr.dtype == ml_dtypes.bfloat16 else F32)
    x_own = din("x_own", (TOWN, D))
    wgateT = din("wgateT", (NE, D, HID))
    wupT = din("wupT", (NE, D, HID))
    wdownT = din("wdownT", (NE, HID, D))
    wgT = din("wgT", (D, NE))
    bg_bc = din("bg_bc", (128, NE))
    if not skip_attn:
        xtm = din("xtm", (T, D))
        xT = din("xT", (D, T))
        xT_own = din("xT_own", (D, TOWN))
        wvT = din("wvT", (D, D))
        wgaT = din("wgaT", (D, D))
        woT = din("woT", (D, D))
        wabT = din("wabT", (D, 16))
        conv_wD = din("conv_wD", (128, 32))
        conv_bD = din("conv_bD", (128, 8))
        dtb_bc = din("dtb_bc", (128, H))
        alog_row = din("alog_row", (1, H))
        nwm = din("nwm", (128, NCH))
    out_y = nc.dram_tensor("out_y", [TOWN, D], F32, kind="ExternalOutput").ap()

    dbg_out = {}
    for name, shape, ddt in dbg:
        dbg_out[name] = nc.dram_tensor("dbg_" + name, list(shape), ddt,
                                       kind="ExternalOutput").ap()

    with ExitStack() as ctx:
        def mk_sb(stack, side="left"):
            def f(name, shape, dt):
                return stack.enter_context(
                    nc.sbuf_tensor(name, shape, dt, side=side))
            return f

        sb = mk_sb(ctx)
        id_bf = sb("id_bf", [128, 128], BF16)
        id_f32 = sb("id_f32", [128, 128], F32)
        ones_1x128 = sb("ones_1x128", [1, 128], F32)
        ones_128x1 = sb("ones_128x1", [128, 1], F32)
        bd_trilT = sb("bd_trilT", [128, 128], F32)
        bd_ones = sb("bd_ones", [128, 128], F32)
        negtril_i = sb("negtril_i", [128, C], F32)
        negtril_s = sb("negtril_s", [128, C], F32)
        triu_i = sb("triu_i", [128, C], F32)
        triu_s = sb("triu_s", [128, C], F32)
        sel_hh4 = sb("sel_hh4", [H, 4 * 128], F32)
        dup64 = sb("dup64", [128, 256], F32)
        colsel = sb("colsel", [H, H * 128], F32)
        wgT_sb = sb("wgT_sb", [128, 8, NE], BF16)
        bg_sb = sb("bg_sb", [128, NE], F32)
        wstage = sb("wstage", [128, NTT, NE], F32)
        lncol = sb("lncol", [128, 8], F32)
        l_sb = sb("l_sb", [128, NE], F32)
        t8a = sb("t8a", [128, NE], F32)
        t8b = sb("t8b", [128, NE], F32)
        pbanks = [ctx.enter_context(nc.psum_tensor(f"pb{i}", [128, 512], F32))
                  for i in range(8)]
        b0, b1, b2, b3, b4, b5, b6, b7 = pbanks
        h_ld = ctx.enter_context(nc.semaphore("s_ld"))
        h_lg = ctx.enter_context(nc.semaphore("s_lg"))
        h_pe = ctx.enter_context(nc.semaphore("s_pe"))
        h_ve = ctx.enter_context(nc.semaphore("s_ve"))
        h_ac = ctx.enter_context(nc.semaphore("s_ac"))
        h_gp = ctx.enter_context(nc.semaphore("s_gp"))
        h_st = ctx.enter_context(nc.semaphore("s_st"))
        block = ctx.enter_context(nc.Block())
        ld, lg, pe, ve, ac, gp, st = (Sem(h) for h in
                                      (h_ld, h_lg, h_pe, h_ve, h_ac, h_gp, h_st))
        pr = Prog(block, {"tensor": pe, "vector": ve, "scalar": ac,
                          "gpsimd": gp, "sync": st})

        def dma(eng, sem, dst, src):
            return pr.op(eng, lambda E, dd=dst, ss=src: E.dma_start(out=dd, in_=ss),
                         inc=sem, inc_n=16)

        def barrier_all(eng):
            pr.wait(eng, pe, pe.n)
            pr.wait(eng, ve, ve.n)
            pr.wait(eng, ac, ac.n)
            pr.wait(eng, gp, gp.n)

        def dbg_dump(name, src_ap):
            if name not in dbg_out:
                return
            barrier_all("sync")
            dma("sync", st, dbg_out[name], src_ap)

        for t_sb, nm in [
            (id_bf, "c_id_bf"), (id_f32, "c_id_f32"),
            (ones_1x128, "c_ones_1x128"), (ones_128x1, "c_ones_128x1"),
            (bd_trilT, "c_bd_trilT"), (bd_ones, "c_bd_ones"),
            (negtril_i, "c_negtril_i"), (negtril_s, "c_negtril_s"),
            (triu_i, "c_triu_i"), (triu_s, "c_triu_s"),
            (sel_hh4, "c_sel_hh4"), (colsel, "c_colsel"),
            (dup64, "c_dup64"),
        ]:
            dma("sync", ld, t_sb[:], dram[nm])
        dma("sync", ld, bg_sb[:], dram["bg_bc"])
        dma("gpsimd", lg, wgT_sb[:],
            dram["wgT"].rearrange("(dt p) e -> p dt e", p=128))
        v_cst = ld.n
        v_wgt = lg.n

        def ln_tile(dst_ap, src_ap, scr_half):
            pr.wait("scalar", ve, pr.pend["vector"])
            pr.sop("scalar", lambda E: E.activation(
                scr_half, src_ap[:, 0:512], AF.Square, accum_out=lncol[:, 0:1]))
            vb_ = pr.sop("scalar", lambda E: E.activation(
                scr_half, src_ap[:, 512:1024], AF.Square, accum_out=lncol[:, 1:2]))
            pr.sop("vector", lambda E: E.reduce_sum(
                lncol[:, 2:3], src_ap, axis=AX.X))
            pr.wait("vector", ac, vb_)
            pr.sop("vector", lambda E: E.tensor_add(
                lncol[:, 1:2], lncol[:, 0:1], lncol[:, 1:2]))
            pr.sop("vector", lambda E: E.tensor_scalar(
                lncol[:, 3:4], lncol[:, 2:3], 1.0 / D, None, ALU.mult))
            pr.sop("vector", lambda E: E.tensor_scalar(
                lncol[:, 4:5], lncol[:, 1:2], 1.0 / D, None, ALU.mult))
            pr.sop("vector", lambda E: E.tensor_mul(
                lncol[:, 5:6], lncol[:, 3:4], lncol[:, 3:4]))
            pr.sop("vector", lambda E: E.tensor_sub(
                lncol[:, 5:6], lncol[:, 4:5], lncol[:, 5:6]))
            v2 = pr.sop("vector", lambda E: E.tensor_scalar(
                lncol[:, 5:6], lncol[:, 5:6], EPS_LN, None, ALU.add))
            pr.wait("scalar", ve, v2)
            pr.sop("scalar", lambda E: E.activation(
                lncol[:, 6:7], lncol[:, 5:6], AF.Ln))
            va_ = pr.sop("scalar", lambda E: E.activation(
                lncol[:, 6:7], lncol[:, 6:7], AF.Exp, scale=-0.5))
            pr.sop("vector", lambda E: E.tensor_scalar(
                lncol[:, 7:8], lncol[:, 3:4], -1.0, None, ALU.mult))
            pr.wait("vector", ac, va_)
            return pr.sop("vector", lambda E: E.tensor_scalar(
                dst_ap, src_ap, lncol[:, 7:8], lncol[:, 6:7], ALU.add, ALU.mult))

        if not skip_attn:
            sA1 = ExitStack()
            sb1 = mk_sb(sA1, side="right")
            o_normT = sb1("o_normT", [128, H, TOWN], BF16)

            sA2 = ExitStack()
            sb2 = mk_sb(sA2, side="right")
            grow = sb2("grow", [H, QT], F32)
            nbrow = sb2("nbrow", [H, QT], F32)
            egvrow = sb2("egvrow", [H, QT], F32)
            eglrow = sb2("eglrow", [H, QT], F32)
            eglb = sb2("eglb", [128, H, NCH], F32)
            S32 = sb2("S32", [128, H, 128], F32)
            wab_sb = sb2("wab_sb", [128, 8, 16], F32)
            cw_sb = sb2("cw_sb", [128, 8, 4], F32)
            cb_sb = sb2("cb_sb", [128, 8], F32)
            dtb_sb = sb2("dtb_sb", [128, H], F32)
            alog_sb = sb2("alog_sb", [1, H], F32)
            nexpA = sb2("nexpA", [128, H], F32)
            nwm_sb = sb2("nwm_sb", [128, NCH], F32)
            wv_sb = sb2("wv_sb", [128, 8, D], BF16)
            xT_bf2 = [sb2("xT_bf_a", [128, 8, QT], BF16),
                      sb2("xT_bf_b", [128, 8, QT], BF16)]
            nT = sb2("nT", [128, H, QT + 1], BF16)
            xvT = sb2("xvT", [128, 8, QT + 3], BF16)
            v_fm = sb2("v_fm", [128, 8, QT], BF16)
            vb_tm = sb2("vb_tm", [128, TTQ, H, 128], BF16)
            CS = sb2("CS", [128, TTQ, 32], F32)
            CSd = sb2("CSd", [128, TTQ, 2, 32], F32)
            x_tm_r = sb2("x_tm_r", [128, 2, D], F32)
            xT32_r = sb2("xT32_r", [128, 2, 8, 128], F32)
            n_tm_r = sb2("n_tm_r", [128, 2, D], BF16)
            scrA = sb2("scrA", [128, 1, D], F32)
            ss8 = sb2("ss8", [128, 2, 8], F32)
            rstd8 = sb2("rstd8", [128, 2, 8], F32)
            sp16 = sb2("sp16", [128, 2, 16], F32)
            spw = sb2("spw", [128, 2, 24], F32)
            RS = sb2("RS", [128, 2, 32], F32)
            cv_r = sb2("cv_r", [128, 1, 512], F32)
            mdiff = sb2("mdiff", [128, 4, C], F32)
            e1t = sb2("e1t", [128, 4, C], F32)
            e2t = sb2("e2t", [128, 4, C], F32)
            e1sneg = sb2("e1sneg", [128, 4, C], BF16)
            e2s = sb2("e2s", [128, 4, C], BF16)
            e2zi = sb2("e2zi", [128, 4, C], BF16)
            t2m = sb2("t2m", [128, 4, C], BF16)
            Abd = sb2("Abd", [128, H, 128], BF16)
            ATbd = sb2("ATbd", [128, H, 128], BF16)
            PQ_sb = [sb2("PQ_sb0", [128, H, 256], BF16),
                     sb2("PQ_sb1", [128, H, 256], BF16)]
            X_sb = [sb2("X_sb0", [128, H, 256], BF16),
                    sb2("X_sb1", [128, H, 256], BF16)]
            Xf = sb2("Xf", [128, H, 256], BF16)
            wkcT = sb2("wkcT", [128, H, 128], BF16)
            wkbeg = sb2("wkbeg", [128, H, 128], BF16)
            wkdw = sb2("wkdw", [128, 2, H, 128], BF16)
            attnT_r = sb2("attnT_r", [128, 2, H, C], BF16)
            rk_eT8 = sb2("rk_eT8", [128, H, C], BF16)
            v_new8 = sb2("v_new8", [128, H, 128], BF16)
            o2_sb = sb2("o2_sb", [128, 8, C], F32)
            ostage = sb2("ostage", [128, 2, 8, C], F32)
            rst_row = sb2("rst_row", [1, 512], F32)
            rstdbc = sb2("rstdbc", [128, 512], F32)
            onrm_t = sb2("onrm_t", [128, 8, C], BF16)
            Sbf8 = sb2("Sbf8", [128, H, 128], BF16)

            dma("gpsimd", lg, wv_sb[:],
                dram["wvT"].rearrange("(dt p) dv -> p dt dv", p=128))
            v_lg_wv = lg.n
            dma("sync", ld, wab_sb[:],
                dram["wabT"].rearrange("(dt p) e -> p dt e", p=128))
            dma("sync", ld, cw_sb.ap().rearrange("p a b -> p (a b)"),
                dram["conv_wD"])
            dma("sync", ld, cb_sb[:], dram["conv_bD"])
            dma("sync", ld, dtb_sb[:], dram["dtb_bc"])
            dma("sync", ld, alog_sb[:], dram["alog_row"])
            v_nwm = dma("sync", ld, nwm_sb[:], dram["nwm"])
            pr.wait("scalar", ld, v_nwm)
            va = pr.sop("scalar", lambda E: E.activation(
                alog_sb[:], alog_sb[:], AF.Exp))
            pr.wait("tensor", ac, va)
            vp = pr.op("tensor", lambda E: E.matmul(
                b7[:, 0:8], ones_1x128[:], alog_sb[:], start=True, stop=True),
                inc=pe)
            pr.wait("vector", pe, vp)
            v_b7_free = pr.sop("vector", lambda E: E.tensor_scalar(
                nexpA[:], b7[:, 0:8], -1.0, None, ALU.mult))
            pr.op("vector", lambda E: E.memset(Abd[:], 0.0))
            pr.op("vector", lambda E: E.memset(ATbd[:], 0.0))
            pr.op("vector", lambda E: E.memset(S32[:], 0.0))
            pr.op("vector", lambda E: E.memset(nT[:, :, 0:1], 0.0))
            pr.op("vector", lambda E: E.memset(
                Sbf8.ap()[:, :, :].rearrange("p a b -> p (a b)"), 0.0))
            pr.sop("vector", lambda E: E.memset(xvT[:, :, 0:3], 0.0))
            sbf_st = {"v": pr.pend["vector"]}
            stageB_last = {"v": 0}

            halo = {"n": 0, "xv": 0}
            wkdw_done = [0, 0]
            b6_guard = {"v": 0}
            prep_guard = [{"ac": 0, "ve": 0, "pe": 0} for _ in range(2)]
            b3_done = {"v": 0}

            def evac(use_ac, dst, src):
                if use_ac:
                    return (ac, pr.sop("scalar",
                                       lambda E, dst=dst, src=src: E.copy(dst, src)))
                return (ve, pr.sop("vector",
                                   lambda E, dst=dst, src=src: E.tensor_copy(dst, src)))

            def solve_AB(q, tq):
                gt = q * TTQ + tq
                tr = gt % 2
                lcq0 = 2 * tq
                pr.wait("vector", pe, pe.n)
                pr.wait("scalar", pe, pe.n)
                pr.wait("gpsimd", pe, pe.n)
                pr.wait("scalar", ve, pr.pend["vector"])
                for lc in (0, 1):
                    gc = 2 * gt + lc
                    lcq = lcq0 + lc
                    pr.wait("tensor", ve, pr.pend["vector"])
                    pr.wait("tensor", ac, pr.pend["scalar"])
                    pr.wait("tensor", gp, pr.pend["gpsimd"])
                    if lc == 1:
                        pr.wait("vector", gp, pr.pend["gpsimd"])
                        pr.wait("scalar", gp, pr.pend["gpsimd"])
                        pr.wait("gpsimd", ve, pr.pend["vector"])
                        pr.wait("scalar", ve, pr.pend["vector"])
                    for hp in range(4):
                        for sub in (0, 1):
                            h = hp + 4 * sub
                            pp = 64 * sub
                            wk1 = nT.ap()[:, h, 64 * lcq:64 * lcq + 64]
                            rk1 = nT.ap()[:, h, 64 * lcq + 1:64 * lcq + 65]
                            tp = (0, pp)
                            pr.op("tensor", lambda E, hp=hp, pp=pp, wk1=wk1, tp=tp:
                                  E.matmul(b0[pp:pp + 64,
                                              128 * hp + pp:128 * hp + pp + 64],
                                           wk1, wk1, start=True, stop=True,
                                           tile_position=tp))
                            pr.op("tensor", lambda E, hp=hp, pp=pp, wk1=wk1,
                                  rk1=rk1, tp=tp:
                                  E.matmul(b1[pp:pp + 64,
                                              128 * hp + pp:128 * hp + pp + 64],
                                           wk1, rk1, start=True, stop=True,
                                           tile_position=tp))
                        pr.op("tensor", lambda E, hp=hp, lcq=lcq: E.matmul(
                            b3[:, 64 * hp:64 * hp + 64],
                            sel_hh4[:, hp * 128:(hp + 1) * 128],
                            grow[0:8, 64 * lcq:64 * lcq + 64],
                            start=True, stop=True))
                        pr.op("tensor", lambda E, hp=hp, lcq=lcq: E.matmul(
                            b3[:, 256 + 64 * hp:256 + 64 * hp + 64],
                            sel_hh4[:, hp * 128:(hp + 1) * 128],
                            nbrow[0:8, 64 * lcq:64 * lcq + 64],
                            start=True, stop=True),
                            inc=pe if hp == 3 else None)
                    v_A_mm = pe.n
                    pr.wait("vector", pe, v_A_mm)
                    pr.wait("gpsimd", pe, v_A_mm)
                    pr.wait("scalar", pe, v_A_mm)
                    p0 = 64 * lc
                    for hp in range(4):
                        pr.op("vector", lambda E, hp=hp, p0=p0, tq=tq:
                              E.tensor_scalar(
                                  mdiff[0:64, hp, :], b3[0:64, 64 * hp:64 * hp + 64],
                                  CS[p0:p0 + 64, tq, hp:hp + 1], None,
                                  ALU.subtract))
                        pr.op("vector", lambda E, hp=hp, p0=p0, tq=tq:
                              E.tensor_scalar(
                                  mdiff[64:128, hp, :],
                                  b3[64:128, 64 * hp:64 * hp + 64],
                                  CS[p0:p0 + 64, tq, 4 + hp:5 + hp], None,
                                  ALU.subtract),
                              inc=ve if hp == 3 else None)
                    v_md = ve.n
                    m3 = mdiff.ap()[:, :, :]
                    pr.wait("vector", ve, v_md)
                    pr.op("vector", lambda E: E.tensor_mul(
                        e1t.ap()[:, :, :], m3,
                        negtril_i.ap()[:, :].unsqueeze(1).broadcast_to([128, 4, C])))
                    v_m = pr.op("vector", lambda E: E.tensor_mul(
                        e2t.ap()[:, :, :], m3,
                        triu_i.ap()[:, :].unsqueeze(1).broadcast_to([128, 4, C])),
                        inc=ve)
                    pr.wait("scalar", ve, v_m)
                    pr.op("scalar", lambda E: E.activation(
                        e1t.ap()[:, :, :].rearrange("p a b -> p (a b)"),
                        e1t.ap()[:, :, :].rearrange("p a b -> p (a b)"), AF.Exp))
                    v_e = pr.op("scalar", lambda E: E.activation(
                        e2t.ap()[:, :, :].rearrange("p a b -> p (a b)"),
                        e2t.ap()[:, :, :].rearrange("p a b -> p (a b)"), AF.Exp),
                        inc=ac)
                    pr.wait("gpsimd", ac, v_e)
                    pr.op("gpsimd", lambda E: E.tensor_mul(
                        e1sneg.ap()[:, :, :], e1t.ap()[:, :, :],
                        negtril_s.ap()[:, :].unsqueeze(1).broadcast_to([128, 4, C])))
                    pr.op("gpsimd", lambda E: E.tensor_mul(
                        e2s.ap()[:, :, :], e2t.ap()[:, :, :],
                        triu_s.ap()[:, :].unsqueeze(1).broadcast_to([128, 4, C])))
                    v_g = pr.op("gpsimd", lambda E: E.tensor_mul(
                        e2zi.ap()[:, :, :], e2t.ap()[:, :, :],
                        triu_i.ap()[:, :].unsqueeze(1).broadcast_to([128, 4, C])),
                        inc=gp)
                    pr.wait("vector", gp, v_g)
                    v_t2 = pr.op("vector", lambda E: E.tensor_mul(
                        t2m.ap()[:, :, :], e2s.ap()[:, :, :],
                        b3.ap()[:, 256:512].rearrange("p (a b) -> p a b", a=4)),
                        inc=ve)
                    pr.wait("vector", ve, v_t2)
                    pr.op("vector", lambda E, lc=lc, tr=tr: E.scalar_tensor_tensor(
                        attnT_r.ap()[64 * lc:64 * lc + 64, tr, 0:4, :],
                        b1.ap()[0:64, 0:512].rearrange("p (a b) -> p a b", b=128)
                        [:, :, 0:64],
                        RSQK, e2zi.ap()[0:64, :, :], ALU.mult, ALU.mult))
                    pr.op("vector", lambda E, lc=lc, tr=tr: E.scalar_tensor_tensor(
                        attnT_r.ap()[64 * lc:64 * lc + 64, tr, 4:8, :],
                        b1.ap()[64:128, 0:512].rearrange("p (a b) -> p a b", b=128)
                        [:, :, 64:128],
                        RSQK, e2zi.ap()[64:128, :, :], ALU.mult, ALU.mult))
                    for hp in range(4):
                        for sub, h in ((0, hp), (1, hp + 4)):
                            pp = 64 * sub
                            pr.op("vector", lambda E, hp=hp, h=h, lc=lc,
                                  pp=pp, tq=tq: E.scalar_tensor_tensor(
                                      Abd[64 * lc:64 * lc + 64, h,
                                          64 * lc:64 * lc + 64],
                                      b0[pp:pp + 64, 128 * hp + pp:128 * hp + pp + 64],
                                      CSd[pp:pp + 64, tq, lc, 8 + h:9 + h],
                                      e1sneg[pp:pp + 64, hp, :],
                                      ALU.mult, ALU.mult))
                            pr.op("vector", lambda E, hp=hp, h=h, lc=lc,
                                  pp=pp: E.tensor_mul(
                                      ATbd[64 * lc:64 * lc + 64, h,
                                           64 * lc:64 * lc + 64],
                                      b0[pp:pp + 64, 128 * hp + pp:128 * hp + pp + 64],
                                      t2m[pp:pp + 64, hp, :]),
                                  inc=ve if hp == 3 and sub == 1 else None)
                    pr.pend["vector"] = ve.n
                b3_done["v"] = pr.pend["vector"]
                pr.wait("tensor", ve, pr.pend["vector"])
                v_tpB = [0, 0]
                for h in range(H):
                    bk = (b4 if h < 4 else b5)
                    bsl = bk[:, 64 * (h % 4):64 * (h % 4) + 64].bitcast(BF16)
                    pr.op("tensor", lambda E, h=h, bsl=bsl, lcq0=lcq0:
                          E.transpose(bsl,
                                      nT[:, h, 64 * lcq0:64 * lcq0 + 128],
                                      id_bf[:]),
                          inc=pe if h % 4 == 3 else None)
                    if h == 3:
                        v_tpB[0] = pe.n
                v_tpB[1] = pe.n
                if tq == 0:
                    nT0_pe["v"] = v_tpB[1]
                for h in range(H):
                    bk = (b4 if h < 4 else b5)
                    bsl = bk[:, 64 * (h % 4):64 * (h % 4) + 64].bitcast(BF16)
                    if h % 4 == 0:
                        pr.wait("vector", pe, v_tpB[h // 4])
                    pr.sop("vector", lambda E, h=h, bsl=bsl, tq=tq:
                           E.tensor_scalar(
                               wkbeg[:, h, :], bsl, CS[:, tq, 16 + h:17 + h],
                               None, ALU.mult))
                    pr.sop("vector", lambda E, h=h, bsl=bsl, tq=tq, tr=tr:
                           E.tensor_scalar(
                               wkdw[:, tr, h, :], bsl, CS[:, tq, 24 + h:25 + h],
                               None, ALU.mult))
                wkdw_done[tr] = pr.pend["vector"]
                stageB_last["v"] = pr.pend["vector"]
            def solve_C(q, tq):
                gt = q * TTQ + tq
                tr = gt % 2
                pr.wait("scalar", ve, pr.pend["vector"])
                def xreg(h):
                    bk = pbanks[h // 2]
                    return bk[:, 256 * (h % 2):256 * (h % 2) + 256]

                pr.wait("tensor", ve, pr.pend["vector"])
                v_mm = [0] * H
                vev = [(ve, 0)] * 4

                def evac_round(bank_dsts):
                    for use_ac in (False, True):
                        eng = "scalar" if use_ac else "vector"
                        sem = ac if use_ac else ve
                        pr.wait(eng, sem, pr.pend[eng])
                        banks = [k for k in range(4) if ((k % 2 == 1) == use_ac)]
                        n_ops = sum(len(bank_dsts(k)) for k in banks)
                        k_op = 0
                        for k in banks:
                            pr.wait(eng, pe, v_mm[k * 2 + 1])
                            for dst, srcv in bank_dsts(k):
                                k_op += 1
                                last = (k_op == n_ops)
                                if use_ac:
                                    pr.op(eng, lambda E, dst=dst, srcv=srcv:
                                          E.copy(dst, srcv),
                                          inc=sem if last else None)
                                else:
                                    pr.op(eng, lambda E, dst=dst, srcv=srcv:
                                          E.tensor_copy(dst, srcv),
                                          inc=sem if last else None)
                        pr.pend[eng] = sem.n
                        for k in range(4):
                            if ((k % 2 == 1) == use_ac):
                                vev[k] = (sem, sem.n)

                for h in range(H):
                    reg = xreg(h)
                    pr.op("tensor", lambda E, h=h, reg=reg, tq=tq: E.matmul(
                        reg[:, 0:128], ATbd[:, h, :], vb_tm[:, tq, h, :],
                        start=True, stop=False))
                    pr.op("tensor", lambda E, h=h, reg=reg, tq=tq: E.matmul(
                        reg[:, 0:128], id_bf[:], vb_tm[:, tq, h, :],
                        start=False, stop=True))
                    pr.op("tensor", lambda E, h=h, reg=reg: E.matmul(
                        reg[:, 128:256], ATbd[:, h, :], wkbeg[:, h, :],
                        start=True, stop=False))
                    v_mm[h] = pr.op("tensor", lambda E, h=h, reg=reg: E.matmul(
                        reg[:, 128:256], id_bf[:], wkbeg[:, h, :],
                        start=False, stop=True), inc=pe)
                evac_round(lambda k: [(X_sb[0].ap()[:, 2 * k:2 * k + 2, :]
                                       .rearrange("p a b -> p (a b)"),
                                       pbanks[k][:, 0:512])])
                if stop_after == "solveC0":
                    return
                for j in range(1, 6):
                    for h in range(H):
                        reg = xreg(h)
                        Pp = Abd[:, h, :] if j == 1 else \
                            PQ_sb[(j - 1) % 2][:, h, 0:128]
                        Qp = ATbd[:, h, :] if j == 1 else \
                            PQ_sb[(j - 1) % 2][:, h, 128:256]
                        pr.wait("tensor", vev[h // 2][0], vev[h // 2][1])
                        if j < 5:
                            pr.op("tensor", lambda E, reg=reg, Pp=Pp, Qp=Qp:
                                  E.matmul(reg[:, 0:128], Qp, Pp,
                                           start=True, stop=True))
                        v_mm[h] = pr.op("tensor", lambda E, reg=reg, Pp=Pp, Qp=Qp:
                                        E.matmul(reg[:, 128:256], Pp, Qp,
                                                 start=True, stop=True), inc=pe)
                    evac_round(lambda k, j=j: (
                        [(PQ_sb[j % 2].ap()[:, 2 * k:2 * k + 2, :]
                          .rearrange("p a b -> p (a b)"),
                          pbanks[k][:, 0:512])] if j < 5 else
                        [(PQ_sb[j % 2][:, 2 * k, 128:256],
                          pbanks[k][:, 128:256]),
                         (PQ_sb[j % 2][:, 2 * k + 1, 128:256],
                          pbanks[k][:, 384:512])]))
                    for h in range(H):
                        reg = xreg(h)
                        pr.wait("tensor", vev[h // 2][0], vev[h // 2][1])
                        pr.op("tensor", lambda E, h=h, reg=reg, j=j: E.matmul(
                            reg[:], PQ_sb[j % 2][:, h, 128:256],
                            X_sb[(j - 1) % 2][:, h, :],
                            start=True, stop=False))
                        v_mm[h] = pr.op("tensor", lambda E, h=h, reg=reg, j=j:
                                        E.matmul(reg[:], id_bf[:],
                                                 X_sb[(j - 1) % 2][:, h, :],
                                                 start=False, stop=True), inc=pe)
                    if j < 5:
                        evac_round(lambda k, j=j: [
                            (X_sb[j % 2].ap()[:, 2 * k:2 * k + 2, :]
                             .rearrange("p a b -> p (a b)"),
                             pbanks[k][:, 0:512])])
                    else:
                        evac_round(lambda k: [
                            (Xf.ap()[:, 2 * k:2 * k + 2, :]
                             .rearrange("p a b -> p (a b)"),
                             pbanks[k][:, 0:512])])
                if stop_after == "solveCJ":
                    return
                pr.wait("tensor", ve, pr.pend["vector"])
                pr.wait("tensor", ac, pr.pend["scalar"])
                v_tpW = [0, 0]
                for h in range(H):
                    bk = (b4 if h < 4 else b5)
                    bsl = bk[:, 64 * (h % 4):64 * (h % 4) + 64].bitcast(BF16)
                    pr.op("tensor", lambda E, h=h, bsl=bsl: E.transpose(
                        bsl, Xf[:, h, 128:256], id_bf[:]),
                        inc=pe if h % 4 == 3 else None)
                    if h == 3:
                        v_tpW[0] = pe.n
                v_tpW[1] = pe.n
                pr.wait("vector", pe, v_tpW[0])
                pr.sop("vector", lambda E: E.tensor_copy(
                    wkcT.ap()[:, 0:4, :].rearrange("p a b -> p (a b)"),
                    b4.ap()[:, 0:256].bitcast(BF16)))
                pr.wait("vector", pe, v_tpW[1])
                pr.sop("vector", lambda E: E.tensor_copy(
                    wkcT.ap()[:, 4:8, :].rearrange("p a b -> p (a b)"),
                    b5.ap()[:, 0:256].bitcast(BF16)))

            def rms_tile(q, tq):
                gt = q * TTQ + tq
                for lc in (0, 1):
                    gc = 2 * gt + lc
                    oc = (C * gc) % TOWN
                    second = (C * gc) >= TOWN
                    va = pr.sop("scalar", lambda E, lc=lc: E.activation(
                        o2_sb.ap()[:, :, :].rearrange("p a b -> p (a b)"),
                        ostage.ap()[:, lc, :, :].rearrange("p a b -> p (a b)"),
                        AF.Square))
                    pr.wait("tensor", ac, va)
                    pr.wait("tensor", ve, pr.pend["vector"])
                    pr.op("tensor", lambda E: E.matmul(
                        b7[0:1, 0:256], ones_128x1[:],
                        o2_sb.ap()[:, 0:4, :].rearrange("p a b -> p (a b)"),
                        start=True, stop=True))
                    v_ss = pr.op("tensor", lambda E: E.matmul(
                        b7[0:1, 256:512], ones_128x1[:],
                        o2_sb.ap()[:, 4:8, :].rearrange("p a b -> p (a b)"),
                        start=True, stop=True), inc=pe)
                    pr.wait("vector", pe, v_ss)
                    v9 = pr.sop("vector", lambda E: E.tensor_scalar(
                        rst_row[:], b7[0:1, 0:512], 1.0 / K, EPS_RMS,
                        ALU.mult, ALU.add))
                    pr.wait("scalar", ve, v9)
                    pr.sop("scalar", lambda E: E.activation(
                        rst_row[:], rst_row[:], AF.Ln))
                    v11 = pr.sop("scalar", lambda E: E.activation(
                        rst_row[:], rst_row[:], AF.Exp, scale=-0.5))
                    pr.wait("tensor", ve, v9)
                    pr.wait("tensor", ac, v11)
                    v12 = pr.op("tensor", lambda E: E.matmul(
                        b7[:], ones_1x128[:], rst_row[:], start=True, stop=True),
                        inc=pe)
                    pr.wait("vector", pe, v12)
                    pr.sop("vector", lambda E: E.tensor_copy(rstdbc[:], b7[:]))
                    if not second:
                        pr.sop("vector", lambda E, lc=lc, gc=gc, oc=oc:
                               E.scalar_tensor_tensor(
                                   o_normT.ap()[:, :, oc:oc + C],
                                   ostage.ap()[:, lc, :, :],
                                   nwm_sb[:, gc:gc + 1],
                                   rstdbc.ap()[:, :].rearrange("p (a b) -> p a b", a=8),
                                   ALU.mult, ALU.mult))
                    else:
                        pr.sop("vector", lambda E, lc=lc, gc=gc:
                               E.scalar_tensor_tensor(
                                   onrm_t.ap()[:, :, :],
                                   ostage.ap()[:, lc, :, :],
                                   nwm_sb[:, gc:gc + 1],
                                   rstdbc.ap()[:, :].rearrange("p (a b) -> p a b", a=8),
                                   ALU.mult, ALU.mult))
                        pr.sop("vector", lambda E, oc=oc: E.tensor_add(
                            o_normT.ap()[:, :, oc:oc + C],
                            o_normT.ap()[:, :, oc:oc + C],
                            onrm_t.ap()[:, :, :]))

            def scan_chunk(q, tq, lc):
                gt = q * TTQ + tq
                tr = gt % 2
                gc = 2 * gt + lc
                lcq = 2 * tq + lc
                oc = (C * gc) % TOWN
                second = (C * gc) >= TOWN
                pr.wait("vector", pe, pe.n)
                pr.wait("scalar", pe, pe.n)
                pr.wait("tensor", ve, max(sbf_st["v"], pr.pend["vector"]))
                for h in range(H):
                    bk = b4 if h < 4 else b5
                    pr.op("tensor", lambda E, h=h, bk=bk, tr=tr, lc=lc: E.matmul(
                        bk[64 * lc:64 * lc + 64, 128 * (h % 4):128 * (h % 4) + 128],
                        wkcT[:, h, 64 * lc:64 * lc + 64], Sbf8[:, h, :],
                        start=True, stop=True, tile_position=(0, 64 * lc)),
                        inc=pe if h % 4 == 3 else None)
                v_vn_mm = pe.n
                pr.wait("vector", pe, v_vn_mm)
                pr.wait("vector", ve, pr.pend["vector"])
                pr.op("vector", lambda E, lc=lc: E.tensor_sub(
                    v_new8.ap()[64 * lc:64 * lc + 64, 0:4, :],
                    Xf.ap()[64 * lc:64 * lc + 64, 0:4, 0:128],
                    b4.ap()[64 * lc:64 * lc + 64, :].rearrange(
                        "p (a b) -> p a b", a=4)))
                v_vnew = pr.op("vector", lambda E, lc=lc: E.tensor_sub(
                    v_new8.ap()[64 * lc:64 * lc + 64, 4:8, :],
                    Xf.ap()[64 * lc:64 * lc + 64, 4:8, 0:128],
                    b5.ap()[64 * lc:64 * lc + 64, :].rearrange(
                        "p (a b) -> p a b", a=4)), inc=ve)
                pr.pend["vector"] = v_vnew
                pr.wait("tensor", ve, b6_guard["v"])
                for h in range(H):
                    pr.op("tensor", lambda E, h=h, lcq=lcq: E.matmul(
                        b6[:, 64 * h:64 * h + 64],
                        colsel[:, h * 128:(h + 1) * 128],
                        egvrow[0:8, 64 * lcq:64 * lcq + 64],
                        start=True, stop=True),
                        inc=pe if h == 7 else None)
                v_egb = pe.n
                pr.wait("vector", pe, v_egb)
                pr.wait("vector", ve, pr.pend["vector"])
                v_rke = pr.op("vector", lambda E, lcq=lcq: E.tensor_mul(
                    rk_eT8.ap()[:, :, :],
                    nT.ap()[:, :, 64 * lcq + 1:64 * lcq + 65],
                    b6.ap()[:, :].rearrange("p (a b) -> p a b", a=8)), inc=ve)
                pr.pend["vector"] = v_rke
                b6_guard["v"] = v_rke
                pr.wait("tensor", ve, v_rke)
                for h in range(H):
                    bk = b4 if h < 4 else b5
                    sl = bk[:, 64 * (h % 4):64 * (h % 4) + 64]
                    pr.op("tensor", lambda E, h=h, sl=sl: E.matmul(
                        sl, Sbf8[:, h, :], rk_eT8[:, h, :],
                        start=True, stop=False))
                    pr.op("tensor", lambda E, h=h, sl=sl, tr=tr, lc=lc: E.matmul(
                        sl, v_new8[64 * lc:64 * lc + 64, h, :],
                        attnT_r[64 * lc:64 * lc + 64, tr, h, :],
                        start=False, stop=True),
                        inc=pe if h % 4 == 3 else None)
                v_oT = pe.n
                pr.wait("vector", pe, v_oT)
                pr.wait("scalar", pe, v_oT)
                v_ev4 = pr.sop("vector", lambda E, lc=lc: E.tensor_copy(
                    ostage.ap()[:, lc, 0:4, :].rearrange("p a b -> p (a b)"),
                    b4[:, 0:256]))
                v_ev5 = pr.sop("scalar", lambda E, lc=lc: E.copy(
                    ostage.ap()[:, lc, 4:8, :].rearrange("p a b -> p (a b)"),
                    b5[:, 0:256]))
                pr.wait("tensor", ve, v_ev4)
                pr.wait("tensor", ac, v_ev5)
                pr.wait("tensor", ve, wkdw_done[tr])
                for h in range(H):
                    bk = b4 if h < 4 else b5
                    pr.op("tensor", lambda E, h=h, bk=bk, tr=tr, lc=lc: E.matmul(
                        bk[:, 128 * (h % 4):128 * (h % 4) + 128],
                        wkdw[64 * lc:64 * lc + 64, tr, h, :],
                        v_new8[64 * lc:64 * lc + 64, h, :],
                        start=True, stop=True),
                        inc=pe if h % 4 == 3 else None)
                v_U = pe.n
                pr.wait("vector", pe, v_U)
                pr.wait("vector", ve, pr.pend["vector"])
                for h in range(H):
                    bk = b4 if h < 4 else b5
                    pr.op("vector", lambda E, h=h, bk=bk, gc=gc:
                          E.scalar_tensor_tensor(
                              S32[:, h, :], S32[:, h, :],
                              eglb[:, h, gc:gc + 1],
                              bk[:, 128 * (h % 4):128 * (h % 4) + 128],
                              ALU.mult, ALU.add),
                          inc=ve if h == 7 else None)
                pr.pend["vector"] = ve.n
                sbf_st["v"] = pr.op("vector", lambda E: E.tensor_copy(
                    Sbf8.ap()[:, :, :].rearrange("p a b -> p (a b)"),
                    S32.ap()[:, :, :].rearrange("p a b -> p (a b)")), inc=ve)
                pr.pend["vector"] = sbf_st["v"]

            vpath_done = [0, 0]
            nT_done_st = {"v": 0}
            nT0_pe = {"v": 0}
            for q in range(NQ):
                qs = q * QT
                xT_bf = xT_bf2[q % 2]
                if q > 0:
                    pr.wait("vector", ac, nT_done_st["v"])
                    pr.wait("vector", pe, nT0_pe["v"])
                    halo["n"] = pr.sop("vector", lambda E: E.tensor_copy(
                        nT[:, :, 0:1], nT[:, :, QT:QT + 1]))
                    halo["xv"] = pr.sop("vector", lambda E: E.tensor_copy(
                        xvT[:, :, 0:3], xvT[:, :, QT:QT + 3]))
                pr.wait("gpsimd", pe, vpath_done[q % 2])
                v_xTbf = dma("gpsimd", lg, xT_bf[:],
                             dram["xT"].rearrange("(dt p) t -> p dt t", p=128)
                             [:, :, qs:qs + QT])

                pr.wait("tensor", lg, v_xTbf)
                pr.wait("tensor", lg, v_lg_wv)
                pr.wait("tensor", ve, b3_done["v"])
                for dv in range(8):
                    for i5 in range(QT // 512):
                        for dt in range(8):
                            pr.op("tensor", lambda E, dv=dv, dt=dt, i5=i5,
                                  xT_bf=xT_bf: E.matmul(
                                b3[:], wv_sb[:, dt, dv * 128:(dv + 1) * 128],
                                xT_bf[:, dt, i5 * 512:(i5 + 1) * 512],
                                start=(dt == 0), stop=(dt == 7)),
                                inc=pe if dt == 7 else None)
                        v_mm = pe.n
                        pr.wait("vector", pe, v_mm)
                        if q > 0 and dv == 0 and i5 == 0:
                            pr.wait("vector", ve, halo["xv"])
                        v_evac = pr.sop("vector", lambda E, dv=dv, i5=i5:
                                        E.tensor_copy(
                                            xvT[:, dv, 3 + 512 * i5:515 + 512 * i5],
                                            b3[:]))
                        pr.wait("tensor", ve, v_evac)
                        rr = 0
                        pr.wait("vector", ac, pr.pend["scalar"])
                        pr.sop("vector", lambda E, dv=dv, i5=i5, rr=rr: E.tensor_scalar(
                            cv_r[:, rr, :], xvT[:, dv, 512 * i5:512 * i5 + 512],
                            cw_sb[:, dv, 0:1], cb_sb[:, dv:dv + 1],
                            ALU.mult, ALU.add))
                        for j in (1, 2, 3):
                            pr.sop("vector", lambda E, dv=dv, i5=i5, rr=rr, j=j:
                                   E.scalar_tensor_tensor(
                                       cv_r[:, rr, :],
                                       xvT[:, dv, j + 512 * i5:j + 512 * i5 + 512],
                                       cw_sb[:, dv, j:j + 1], cv_r[:, rr, :],
                                       ALU.mult, ALU.add))
                        v_cv = pr.pend["vector"]
                        pr.wait("scalar", ve, v_cv)
                        pr.sop("scalar", lambda E, dv=dv, i5=i5, rr=rr: E.activation(
                            v_fm[:, dv, 512 * i5:512 * i5 + 512], cv_r[:, rr, :],
                            AF.Silu))
                vpath_done[q % 2] = pe.n

                for tq in range(TTQ):
                    r = tq % 2
                    ts0 = qs + tq * 128
                    pr.wait("sync", ve, prep_guard[r]["ve"])
                    pr.wait("sync", ac, prep_guard[r]["ac"])
                    pr.wait("sync", pe, prep_guard[r]["pe"])
                    v_x = dma("sync", ld, x_tm_r[:, r, :], xtm[ts0:ts0 + 128, :])
                    v_xT = dma("sync", ld, xT32_r[:, r, :, :],
                               dram["xT"].rearrange("(dt p) t -> p dt t", p=128)
                               [:, :, ts0:ts0 + 128])
                    pr.wait("scalar", ld, v_x)
                    pr.wait("scalar", ve, pr.pend["vector"])
                    va = pr.sop("scalar", lambda E, r=r: E.activation(
                        scrA[:, 0, :], x_tm_r[:, r, :], AF.Square))
                    v_sq = va
                    pr.wait("vector", ac, va)
                    pr.sop("vector", lambda E, r=r: E.reduce_sum(
                        ss8[:, r, :],
                        scrA.ap()[:, 0, :].rearrange("p (h k) -> p h k", h=H),
                        axis=AX.X))
                    v1 = pr.sop("vector", lambda E, r=r: E.tensor_scalar(
                        ss8[:, r, :], ss8[:, r, :], EPS_L2, None, ALU.add))
                    pr.wait("scalar", ve, v1)
                    pr.sop("scalar", lambda E, r=r: E.activation(
                        rstd8[:, r, :], ss8[:, r, :], AF.Ln))
                    va = pr.sop("scalar", lambda E, r=r: E.activation(
                        rstd8[:, r, :], rstd8[:, r, :], AF.Exp, scale=-0.5))
                    pr.wait("vector", ac, va)
                    v_ntm = pr.sop("vector", lambda E, r=r: E.tensor_mul(
                        n_tm_r.ap()[:, r, :].rearrange("p (h k) -> p h k", h=H),
                        x_tm_r.ap()[:, r, :].rearrange("p (h k) -> p h k", h=H),
                        rstd8.ap()[:, r, :].unsqueeze(2).broadcast_to([128, H, 128])))
                    pr.wait("tensor", ve, v_ntm)
                    pr.wait("tensor", ac, pr.pend["scalar"])
                    v_tp = [0, 0]
                    for h in range(H):
                        bk = b4 if h < 4 else b5
                        bsl = bk[:, 64 * (h % 4):64 * (h % 4) + 64].bitcast(BF16)
                        pr.op("tensor", lambda E, r=r, h=h, bsl=bsl: E.transpose(
                            bsl, n_tm_r[:, r, h * 128:(h + 1) * 128], id_bf[:]),
                            inc=pe if h % 4 == 3 else None)
                        if h == 3:
                            v_tp[0] = pe.n
                    v_tp[1] = pe.n
                    if q > 0 and tq == 0:
                        pr.wait("scalar", ve, halo["n"])
                    pr.wait("scalar", pe, v_tp[0])
                    pr.sop("scalar", lambda E, tq=tq: E.copy(
                        nT.ap()[:, 0:4, tq * 128 + 1:tq * 128 + 129],
                        b4.ap()[:, 0:256].bitcast(BF16).rearrange(
                            "p (a b) -> p a b", a=4)))
                    pr.wait("scalar", pe, v_tp[1])
                    pr.sop("scalar", lambda E, tq=tq: E.copy(
                        nT.ap()[:, 4:8, tq * 128 + 1:tq * 128 + 129],
                        b5.ap()[:, 0:256].bitcast(BF16).rearrange(
                            "p (a b) -> p a b", a=4)))
                    v_nT_done = pr.pend["scalar"]
                    pr.wait("tensor", ac, v_nT_done)
                    pr.wait("tensor", ld, v_xT)
                    pr.wait("tensor", ld, v_cst)
                    pr.wait("tensor", ve, v_b7_free)
                    pr.wait("tensor", ve, pr.pend["vector"])
                    pr.wait("tensor", ac, pr.pend["scalar"])
                    for dt in range(8):
                        pr.op("tensor", lambda E, r=r, dt=dt: E.matmul(
                            b7[:, 0:16], xT32_r[:, r, dt, :], wab_sb[:, dt, :],
                            start=(dt == 0), stop=(dt == 7)),
                            inc=pe if dt == 7 else None)
                    v_xab = pe.n
                    pr.wait("vector", pe, v_xab)
                    r2 = r
                    v_b7_free = pr.sop("vector", lambda E, r2=r2: E.tensor_copy(
                        sp16[:, r2, :], b7[:, 0:16]))
                    pr.sop("vector", lambda E, r2=r2: E.tensor_add(
                        spw[:, r2, 0:8], sp16[:, r2, 0:8], dtb_sb[:]))
                    v1 = pr.sop("vector", lambda E, r2=r2: E.tensor_scalar(
                        spw[:, r2, 8:16], spw[:, r2, 0:8], 0.0, None, ALU.max))
                    pr.wait("scalar", ve, v1)
                    pr.sop("scalar", lambda E, r2=r2: E.activation(
                        spw[:, r2, 16:24], spw[:, r2, 0:8], AF.Abs))
                    pr.sop("scalar", lambda E, r2=r2: E.activation(
                        spw[:, r2, 16:24], spw[:, r2, 16:24], AF.Exp, scale=-1.0))
                    va = pr.sop("scalar", lambda E, r2=r2: E.activation(
                        spw[:, r2, 16:24], spw[:, r2, 16:24], AF.Ln, bias=1.0))
                    pr.wait("vector", ac, va)
                    pr.sop("vector", lambda E, r2=r2: E.tensor_add(
                        spw[:, r2, 0:8], spw[:, r2, 8:16], spw[:, r2, 16:24]))
                    v1 = pr.sop("vector", lambda E, r2=r2: E.tensor_mul(
                        spw[:, r2, 0:8], spw[:, r2, 0:8], nexpA[:]))
                    va = pr.sop("scalar", lambda E, r2=r2, tq=tq: E.activation(
                        CS[:, tq, 8:16], sp16[:, r2, 8:16], AF.Sigmoid))
                    pr.wait("tensor", ve, v1)
                    pr.wait("tensor", ac, va)
                    pr.op("tensor", lambda E, r2=r2: E.matmul(
                        b6[:, 0:8], bd_trilT[:], spw[:, r2, 0:8],
                        start=True, stop=True))
                    vp = pr.op("tensor", lambda E, r2=r2: E.matmul(
                        b6[:, 8:16], bd_ones[:], spw[:, r2, 0:8],
                        start=True, stop=True), inc=pe)
                    pr.wait("vector", pe, vp)
                    pr.sop("vector", lambda E, tq=tq: E.tensor_copy(
                        CS[:, tq, 0:8], b6[:, 0:8]))
                    v1 = pr.sop("vector", lambda E, r2=r2, tq=tq: E.tensor_sub(
                        spw[:, r2, 8:16], b6[:, 8:16], CS[:, tq, 0:8]))
                    pr.wait("scalar", ve, v1)
                    pr.sop("scalar", lambda E, r2=r2, tq=tq: E.activation(
                        CS[:, tq, 24:32], spw[:, r2, 8:16], AF.Exp))
                    pr.sop("scalar", lambda E, r2=r2, tq=tq: E.activation(
                        spw[:, r2, 16:24], CS[:, tq, 0:8], AF.Exp))
                    va = pr.sop("scalar", lambda E, r2=r2: E.activation(
                        RS[:, r2, 24:32], b6[:, 8:16], AF.Exp))
                    v_b6_free = va
                    pr.wait("vector", ac, va)
                    pr.sop("vector", lambda E, r2=r2, tq=tq: E.tensor_mul(
                        CS[:, tq, 16:24], CS[:, tq, 8:16], spw[:, r2, 16:24]))
                    pr.sop("vector", lambda E, r2=r2, tq=tq: E.tensor_copy(
                        RS[:, r2, 0:8], CS[:, tq, 0:8]))
                    pr.sop("vector", lambda E, r2=r2, tq=tq: E.tensor_scalar(
                        RS[:, r2, 8:16], CS[:, tq, 8:16], -1.0, None, ALU.mult))
                    v_rs = pr.sop("vector", lambda E, r2=r2: E.tensor_scalar(
                        RS[:, r2, 16:24], spw[:, r2, 16:24], RSQK, None, ALU.mult))
                    pr.wait("tensor", ve, v_rs)
                    pr.wait("tensor", ac, v_b6_free)
                    rows4 = [(grow, 0), (nbrow, 8), (egvrow, 16), (eglrow, 24)]
                    for gi, (rowt, off) in enumerate(rows4):
                        pr.op("tensor", lambda E, r2=r2, off=off, gi=gi: E.transpose(
                            b6[0:8, 128 * gi:128 * gi + 128],
                            RS[:, r2, off:off + 8], id_f32[:]),
                            inc=pe if gi == 3 else None)
                    v_rt = pe.n
                    pr.wait("scalar", pe, v_rt)
                    ts0q = ts0 - qs
                    for gi, (rowt, off) in enumerate(rows4):
                        pr.sop("scalar", lambda E, rowt=rowt, gi=gi, ts0q=ts0q: E.copy(
                            rowt[0:8, ts0q:ts0q + 128],
                            b6[0:8, 128 * gi:128 * gi + 128]))
                    v_rows = pr.pend["scalar"]
                    pr.wait("tensor", ac, v_rows)
                    pr.wait("tensor", ve, pr.pend["vector"])
                    pr.op("tensor", lambda E, tq=tq: E.matmul(
                        b6[:, 0:32], dup64[:, 0:128], CS[:, tq, :],
                        start=True, stop=True))
                    vdp = pr.op("tensor", lambda E, tq=tq: E.matmul(
                        b6[:, 32:64], dup64[:, 128:256], CS[:, tq, :],
                        start=True, stop=True), inc=pe)
                    pr.wait("vector", pe, vdp)
                    pr.sop("vector", lambda E, tq=tq: E.tensor_copy(
                        CSd[:, tq, 0, :], b6[:, 0:32]))
                    v_csd = pr.sop("vector", lambda E, tq=tq: E.tensor_copy(
                        CSd[:, tq, 1, :], b6[:, 32:64]))
                    pr.wait("tensor", ve, v_csd)
                    prep_guard[r] = {"ac": v_sq, "ve": v_ntm, "pe": v_xab}
                nT_done_st["v"] = v_nT_done

                ch0 = qs // C
                nchq = QT // C
                for h in range(H):
                    vp = pr.op("tensor", lambda E, h=h, nchq=nchq: E.matmul(
                        b6[:, 0:nchq], colsel[:, h * 128:(h + 1) * 128],
                        eglrow[0:8, 0:QT:64], start=True, stop=True), inc=pe)
                    pr.wait("vector", pe, vp)
                    vv = pr.sop("vector", lambda E, h=h, ch0=ch0, nchq=nchq:
                                E.tensor_copy(
                                    eglb[:, h, ch0:ch0 + nchq], b6[:, 0:nchq]))
                    pr.wait("tensor", ve, vv)

                if q == 0:
                    dbg_dump("eglb2", eglb[:])
                v_vfm = pr.pend["scalar"]
                pr.wait("tensor", ac, v_vfm)
                pr.wait("tensor", ve, pr.pend["vector"])
                for tq in range(TTQ):
                    v_tpv = [0, 0]
                    for h in range(H):
                        bk = b4 if h < 4 else b5
                        bsl = bk[:, 64 * (h % 4):64 * (h % 4) + 64].bitcast(BF16)
                        pr.op("tensor", lambda E, h=h, tq=tq, bsl=bsl:
                              E.transpose(bsl,
                                          v_fm[:, h, tq * 128:(tq + 1) * 128],
                                          id_bf[:]),
                              inc=pe if h % 4 == 3 else None)
                        if h == 3:
                            v_tpv[0] = pe.n
                    v_tpv[1] = pe.n
                    for h in range(H):
                        bk = b4 if h < 4 else b5
                        bsl = bk[:, 64 * (h % 4):64 * (h % 4) + 64].bitcast(BF16)
                        if h % 4 == 0:
                            pr.wait("vector", pe, v_tpv[h // 4])
                        pr.sop("vector", lambda E, h=h, tq=tq, bsl=bsl:
                               E.tensor_scalar(
                                   vb_tm[:, tq, h, :], bsl,
                                   CS[:, tq, 8 + h:9 + h], None, ALU.mult))
                    pr.wait("tensor", ve, pr.pend["vector"])

                if q == 0:
                    dbg_dump("eglrow", eglrow[:])
                    dbg_dump("egvrow", egvrow[:])
                    dbg_dump("grow", grow[:])
                    dbg_dump("nT", nT[:])
                    dbg_dump("CS", CS[:])
                    dbg_dump("vb", vb_tm[:])
                    dbg_dump("vfm", v_fm[:])

                b6_guard["v"] = pr.pend["vector"]
                for tq in range(TTQ):
                    if stop_after == "prep":
                        break
                    solve_AB(q, tq)
                    if stop_after in ("solveA", "solveB"):
                        continue
                    if tq > 0:
                        for lc in (0, 1):
                            scan_chunk(q, tq - 1, lc)
                        rms_tile(q, tq - 1)
                    solve_C(q, tq)
                    if q == 0 and tq == 0:
                        dbg_dump("mdiff", mdiff[:])
                        dbg_dump("e1t", e1t[:])
                        dbg_dump("t2m", t2m[:])
                        dbg_dump("Abd", Abd[:])
                        dbg_dump("ATbd", ATbd[:])
                        dbg_dump("Xf0", Xf[:])
                        dbg_dump("attnT0", attnT_r[:])
                        dbg_dump("wkcT0", wkcT[:])
                        dbg_dump("wkdw0", wkdw[:])
                    if stop_after in ("solve", "solveA", "solveB", "solveC0",
                                      "solveCJ"):
                        continue
                    if tq == TTQ - 1:
                        for lc in (0, 1):
                            scan_chunk(q, tq, lc)
                        rms_tile(q, tq)
                    if q == 0 and tq == 0:
                        dbg_dump("S0", S32[:])
                        dbg_dump("vnew0", v_new8[:])
                        dbg_dump("eglb", eglb[:])
                    if "Strace" in dbg_out:
                        gti = q * TTQ + tq
                        dbg_dump_noname = None
                        barrier_all("sync")
                        dma("sync", st, dbg_out["Strace"][:, gti, :, :], S32[:])

            dbg_dump("onrm", o_normT[:])
            if stop_after in ("prep", "solve", "scan", "solveA", "solveB",
                              "solveC0", "solveCJ"):
                barrier_all("vector")
                vz = pr.op("vector", lambda E: E.memset(scrA[:, 0, :], 0.0),
                           inc=ve)
                pr.wait("sync", ve, vz)
                for tt in range(NTT):
                    dma("sync", st,
                        out_y.rearrange("(tt p) d -> p tt d", p=128)[:, tt, :],
                        scrA[:, 0, :])
                pr.wait("sync", st, st.n)
                pr.flush()
                sA2.close()
                sA1.close()
                return nc
            pr.flush()
            sA2.close()

            yacc = sb("yacc", [128, NTT, D], F32)
            y1T = sb("y1T", [128, 8, TOWN], BF16)
            wga_sb = sb1("wga_sb", [128, 8, D], BF16)
            wo_sb = sb1("wo_sb", [128, 8, D], BF16)
            xTc_r = sb1("xTc_r", [128, 8, 512], BF16)
            og_st = sb1("og_st", [128, 8, 512], BF16)
            xo_r = sb1("xo_r", [128, 2, D], F32)
            gsl_r = sb1("gsl_r", [128, 2, 512], BF16)
            sq_c = sb1("sq_c", [128, 512], F32)

            barrier_all("gpsimd")
            barrier_all("sync")
            v_wga = dma("gpsimd", lg, wga_sb[:],
                        dram["wgaT"].rearrange("(dt p) dd -> p dt dd", p=128))
            v_wo = dma("gpsimd", lg, wo_sb[:],
                       dram["woT"].rearrange("(dt p) dd -> p dt dd", p=128))
            barrier_all("tensor")
            pr.wait("tensor", lg, v_wo)
            gsl_g = [0, 0]
            res_g = [0, 0]
            for t5 in range(NT512):
                pr.wait("gpsimd", ve, pr.pend["vector"])
                v_xTc = dma("gpsimd", lg, xTc_r[:, :, 0:T5W],
                            dram["xT_own"].rearrange("(dt p) t -> p dt t", p=128)
                            [:, :, t5 * T5W:(t5 + 1) * T5W])
                pr.wait("tensor", lg, v_xTc)
                for hd in range(8):
                    gb = pbanks[hd % 2]
                    pr.wait("tensor", ve, gsl_g[hd % 2])
                    for dt in range(8):
                        pr.op("tensor", lambda E, hd=hd, dt=dt, gb=gb: E.matmul(
                            gb[:, 0:T5W], wga_sb[:, dt, hd * 128:(hd + 1) * 128],
                            xTc_r[:, dt, 0:T5W], start=(dt == 0), stop=(dt == 7)),
                            inc=pe if dt == 7 else None)
                    v_g = pe.n
                    pr.wait("scalar", pe, v_g)
                    va = pr.sop("scalar", lambda E, hd=hd, gb=gb: E.activation(
                        gsl_r[:, hd % 2, 0:T5W], gb[:, 0:T5W], AF.Silu))
                    pr.wait("vector", ac, va)
                    vv = pr.sop("vector", lambda E, hd=hd, t5=t5: E.tensor_mul(
                        og_st[:, hd, 0:T5W], gsl_r[:, hd % 2, 0:T5W],
                        o_normT[:, hd, t5 * T5W:(t5 + 1) * T5W]))
                    gsl_g[hd % 2] = vv
                v_og = pr.pend["vector"]
                if t5 == 0:
                    dbg_dump("og0", og_st[:])
                pr.wait("tensor", ve, v_og)
                for sub in range(SUBS):
                    tt = t5 * SUBS + sub
                    rr = sub % 2
                    pr.wait("sync", ve, res_g[rr])
                    v_xo = dma("sync", ld, xo_r[:, rr, :],
                               x_own[tt * 128:(tt + 1) * 128, :])
                    for half in range(2):
                        bk = pbanks[2 + half]
                        pr.wait("tensor", ve, res_g[half])
                        for f in range(8):
                            pr.op("tensor", lambda E, f=f, sub=sub, half=half, bk=bk:
                                  E.matmul(bk[:],
                                           og_st[:, f, sub * 128:(sub + 1) * 128],
                                           wo_sb[:, f, half * 512:(half + 1) * 512],
                                           start=(f == 0), stop=(f == 7)),
                                  inc=pe if f == 7 else None)
                        v_wo_mm = pe.n
                        pr.wait("vector", pe, v_wo_mm)
                        pr.wait("vector", ld, v_xo)
                        res_g[half] = pr.sop(
                            "vector", lambda E, tt=tt, half=half, rr=rr, bk=bk:
                            E.tensor_add(yacc[:, tt, half * 512:(half + 1) * 512],
                                         xo_r[:, rr, half * 512:(half + 1) * 512],
                                         bk[:]))
                    ln_tile(yacc[:, tt, :], yacc[:, tt, :], sq_c[:])
            dbg_dump("yacc1", yacc[:])
            yacc_ready = (ve, pr.pend["vector"])
            pr.flush()
            sA1.close()
        else:
            yacc = sb("yacc", [128, NTT, D], F32)
            y1T = sb("y1T", [128, 8, TOWN], BF16)
            v_x = dma("sync", ld, yacc[:],
                      x_own.rearrange("(tt p) d -> p tt d", p=128))
            yacc_ready = (ld, v_x)

        scr1 = sb("scr1", [128, 512], F32)
        wgq = [sb("wgq_a", [128, 8, 1024], BF16), sb("wgq_b", [128, 8, 1024], BF16)]
        wuq = [sb("wuq_a", [128, 8, 1024], BF16), sb("wuq_b", [128, 8, 1024], BF16)]
        wdq = sb("wdq", [128, 8, 1024], BF16)
        hstage = sb("hstage", [128, 8, 512], BF16)
        gsil = [sb("gsil_a", [128, 512], BF16), sb("gsil_b", [128, 512], BF16)]

        def issue_wgu(e, hq, buf):
            dma("gpsimd", lg, wgq[buf][:],
                wgateT[e].rearrange("(dt p) Hh -> p dt Hh", p=128)
                [:, :, hq * 1024:(hq + 1) * 1024])
            dma("gpsimd", lg, wuq[buf][:],
                wupT[e].rearrange("(dt p) Hh -> p dt Hh", p=128)
                [:, :, hq * 1024:(hq + 1) * 1024])
            return lg.n

        def issue_wd(e, hq):
            dma("gpsimd", lg, wdq[:],
                wdownT[e][hq * 1024:(hq + 1) * 1024, :]
                .rearrange("(ht p) do -> p ht do", p=128))
            return lg.n

        pr.wait("gpsimd", pe, pe.n)
        pr.wait("gpsimd", ve, pr.pend["vector"])
        pr.wait("gpsimd", ac, pr.pend["scalar"])
        v_wgu0 = issue_wgu(0, 0, 0)
        pr.wait("tensor", yacc_ready[0], yacc_ready[1])
        pr.wait("tensor", ld, v_cst)
        pr.wait("tensor", lg, v_wgt)

        for tt in range(NTT):
            pr.wait("tensor", ac, pr.pend["scalar"])
            v6 = 0
            for dt in range(8):
                bk = b6 if dt < 4 else b2
                pr.op("tensor", lambda E, tt=tt, dt=dt, bk=bk: E.transpose(
                    bk[:, 128 * (dt % 4):128 * (dt % 4) + 128],
                    yacc[:, tt, dt * 128:(dt + 1) * 128], id_f32[:]),
                    inc=pe if dt % 4 == 3 else None)
                if dt == 3:
                    v6 = pe.n
            v7 = pe.n
            pr.wait("scalar", pe, v6)
            pr.sop("scalar", lambda E, tt=tt: E.copy(
                y1T.ap()[:, 0:4, tt * 128:(tt + 1) * 128],
                b6.ap()[:, 0:512].rearrange("p (a b) -> p a b", a=4)))
            pr.wait("scalar", pe, v7)
            vac = pr.sop("scalar", lambda E, tt=tt: E.copy(
                y1T.ap()[:, 4:8, tt * 128:(tt + 1) * 128],
                b2.ap()[:, 0:512].rearrange("p (a b) -> p a b", a=4)))
            pr.wait("tensor", ac, vac)
            for dt in range(8):
                pr.op("tensor",
                      lambda E, tt=tt, dt=dt: E.matmul(
                          b7[:, 0:NE],
                          y1T[:, dt, tt * 128:(tt + 1) * 128],
                          wgT_sb[:, dt, :],
                          start=(dt == 0), stop=(dt == 7)),
                      inc=pe if dt == 7 else None)
            v_log = pe.n
            pr.wait("vector", pe, v_log)
            pr.sop("vector", lambda E: E.tensor_add(l_sb[:], b7[:, 0:NE], bg_sb[:]))
            pr.sop("vector", lambda E: E.reduce_max(lncol[:, 0:1], l_sb[:], axis=AX.X))
            pr.sop("vector", lambda E: E.tensor_scalar(
                t8a[:], l_sb[:], lncol[:, 0:1], None, ALU.is_equal))
            pr.sop("vector", lambda E: E.scalar_tensor_tensor(
                t8b[:], t8a[:], -1e30, l_sb[:], ALU.mult, ALU.add))
            pr.sop("vector", lambda E: E.reduce_max(lncol[:, 1:2], t8b[:], axis=AX.X))
            pr.sop("vector", lambda E: E.tensor_scalar(
                t8a[:], l_sb[:], lncol[:, 1:2], None, ALU.is_ge))
            pr.sop("vector", lambda E: E.tensor_scalar(
                lncol[:, 2:3], lncol[:, 1:2], lncol[:, 0:1], None, ALU.subtract))
            v1 = pr.sop("vector", lambda E: E.tensor_scalar(
                t8b[:], l_sb[:], lncol[:, 0:1], None, ALU.subtract))
            pr.wait("scalar", ve, v1)
            pr.sop("scalar", lambda E: E.activation(lncol[:, 3:4], lncol[:, 2:3], AF.Exp))
            v2 = pr.sop("scalar", lambda E: E.activation(t8b[:], t8b[:], AF.Exp))
            pr.wait("vector", ac, v2)
            pr.sop("vector", lambda E: E.tensor_scalar(
                lncol[:, 4:5], lncol[:, 3:4], 1.0, None, ALU.add))
            pr.sop("vector", lambda E: E.reciprocal(lncol[:, 5:6], lncol[:, 4:5]))
            v3 = pr.sop("vector", lambda E, tt=tt: E.scalar_tensor_tensor(
                wstage[:, tt, :], t8b[:], lncol[:, 5:6], t8a[:], ALU.mult, ALU.mult))
            pr.wait("tensor", ve, v3)

        dbg_dump("wstage", wstage[:])
        v_y1T_done = pr.pend["scalar"]
        v_w_done = pr.pend["vector"]
        v_pe_mark = pe.n
        pr.flush()

        steps = [(e, hq) for e in range(NE) for hq in range(HQ)]
        v_wgu = [None] * len(steps)
        v_wd = [None] * len(steps)
        v_wgu[0] = v_wgu0

        v_step_done = [0] * (len(steps) + 1)
        dacc_guard = [0, 0]
        gsil_guard = [0, 0]
        gbank_guard = [0, 0]
        ubank_guard = [0, 0]
        for si, (e, hq) in enumerate(steps):
            buf = si % 2
            pr.wait("gpsimd", pe, v_step_done[si])
            v_wd[si] = issue_wd(e, hq)
            if si + 1 < len(steps):
                en, hn = steps[si + 1]
                v_wgu[si + 1] = issue_wgu(en, hn, (si + 1) % 2)
            pr.wait("tensor", lg, v_wgu[si])
            if si == 0:
                pr.wait("tensor", ac, v_y1T_done)
            for t5 in range(NT512):
                tok = t5 * T5W
                for ht in range(8):
                    par = ht % 2
                    gb, ub = pbanks[2 * par], pbanks[2 * par + 1]
                    gs = gsil[par]
                    pr.wait("tensor", ac, gbank_guard[par])
                    pr.wait("tensor", ve, ubank_guard[par])
                    for dt in range(8):
                        pr.op("tensor", lambda E, buf=buf, dt=dt, ht=ht, tok=tok, gb=gb:
                              E.matmul(gb[:, 0:T5W],
                                       wgq[buf][:, dt, ht * 128:(ht + 1) * 128],
                                       y1T[:, dt, tok:tok + T5W],
                                       start=(dt == 0), stop=(dt == 7)),
                              inc=pe if dt == 7 else None)
                    vg_pe = pe.n
                    for dt in range(8):
                        pr.op("tensor", lambda E, buf=buf, dt=dt, ht=ht, tok=tok, ub=ub:
                              E.matmul(ub[:, 0:T5W],
                                       wuq[buf][:, dt, ht * 128:(ht + 1) * 128],
                                       y1T[:, dt, tok:tok + T5W],
                                       start=(dt == 0), stop=(dt == 7)),
                              inc=pe if dt == 7 else None)
                    vu_pe = pe.n
                    pr.wait("scalar", pe, vg_pe)
                    pr.wait("scalar", ve, gsil_guard[par])
                    va = pr.sop("scalar", lambda E, gb=gb, gs=gs: E.activation(
                        gs[:, 0:T5W], gb[:, 0:T5W], AF.Silu))
                    gbank_guard[par] = va
                    pr.wait("vector", ac, va)
                    pr.wait("vector", pe, vu_pe)
                    vv = pr.sop("vector", lambda E, ub=ub, gs=gs, ht=ht: E.tensor_mul(
                        hstage[:, ht, 0:T5W], gs[:, 0:T5W], ub[:, 0:T5W]))
                    ubank_guard[par] = vv
                    gsil_guard[par] = vv
                v_h_ready = pr.pend["vector"]
                pr.wait("tensor", ve, v_h_ready)
                pr.wait("tensor", lg, v_wd[si])
                for grp in range(2 * SUBS):
                    sub, half = grp // 2, grp % 2
                    bk = pbanks[4 + grp % 2]
                    pr.wait("tensor", ve, dacc_guard[grp % 2])
                    for ht in range(8):
                        pr.op("tensor", lambda E, ht=ht, sub=sub, half=half, bk=bk:
                              E.matmul(bk[:], hstage[:, ht, sub * 128:(sub + 1) * 128],
                                       wdq[:, ht, half * 512:(half + 1) * 512],
                                       start=(ht == 0), stop=(ht == 7)),
                              inc=pe if ht == 7 else None)
                    vdn = pe.n
                    pr.wait("vector", pe, vdn)
                    tt = t5 * SUBS + sub
                    vv = pr.sop("vector", lambda E, tt=tt, half=half, e=e, bk=bk:
                                E.scalar_tensor_tensor(
                                    yacc[:, tt, half * 512:(half + 1) * 512],
                                    bk[:], wstage[:, tt, e:e + 1],
                                    yacc[:, tt, half * 512:(half + 1) * 512],
                                    ALU.mult, ALU.add))
                    dacc_guard[grp % 2] = vv
            v_step_done[si + 1] = pe.n

        v_ffn_done = pr.pend["vector"]
        pr.wait("scalar", ve, v_ffn_done)
        for tt in range(NTT):
            v_ln = ln_tile(yacc[:, tt, :], yacc[:, tt, :], scr1[:])
            pr.wait("sync", ve, v_ln)
            dma("sync", st, out_y.rearrange("(tt p) d -> p tt d", p=128)[:, tt, :],
                yacc[:, tt, :])
        pr.wait("sync", st, st.n)

        pr.flush()

    return nc



def prep_inputs(inp, T=4096):
    f32 = np.float32
    x = np.asarray(inp["L0_attn_x"], f32)
    Wv = np.asarray(inp["L0_attn_Wv"], f32)
    Wg = np.asarray(inp["L0_attn_Wg"], f32)
    Wo = np.asarray(inp["L0_attn_Wo"], f32)
    Wb = np.asarray(inp["L0_attn_Wb"], f32)
    Wa = np.asarray(inp["L0_attn_Wa"], f32)
    dt_bias = np.asarray(inp["L0_attn_dt_bias"], f32)
    A_log = np.asarray(inp["L0_attn_A_log"], f32)
    norm_w = np.asarray(inp["L0_attn_norm_w"], f32)
    conv_w = np.asarray(inp["L0_attn_conv_w"], f32)
    conv_b = np.asarray(inp["L0_attn_conv_b"], f32)
    wg = np.asarray(inp["L0_ffn_wg"], f32)
    bg = np.asarray(inp["L0_ffn_bg"], f32)
    Wgate = np.asarray(inp["L0_ffn_Wgate"], f32)
    Wup = np.asarray(inp["L0_ffn_Wup"], f32)
    Wdown = np.asarray(inp["L0_ffn_Wdown"], f32)

    TOWN = T // 2
    NCH = T // C
    shared = dict(host_constants())
    shared["wgateT"] = np.ascontiguousarray(Wgate.transpose(0, 2, 1))
    shared["wupT"] = np.ascontiguousarray(Wup.transpose(0, 2, 1))
    shared["wdownT"] = np.ascontiguousarray(Wdown.transpose(0, 2, 1))
    shared["wgT"] = np.ascontiguousarray(wg.T)
    shared["bg_bc"] = np.tile(bg, (128, 1)).astype(f32)
    shared["wvT"] = np.ascontiguousarray(Wv.T)
    shared["wgaT"] = np.ascontiguousarray(Wg.T)
    shared["woT"] = np.ascontiguousarray(Wo.T)
    shared["wabT"] = np.ascontiguousarray(
        np.concatenate([Wa.T, Wb.T], axis=1))
    shared["conv_wD"] = np.ascontiguousarray(
        conv_w.reshape(8, 128, 4).transpose(1, 0, 2).reshape(128, 32))
    shared["conv_bD"] = np.ascontiguousarray(conv_b.reshape(8, 128).T)
    shared["dtb_bc"] = np.tile(dt_bias, (128, 1)).astype(f32)
    shared["alog_row"] = np.ascontiguousarray(A_log[None, :])

    B = x.shape[0]
    in_maps = []
    for core in range(2 * B):
        b = core // 2
        half = core % 2
        m = dict(shared)
        xb = np.ascontiguousarray(x[b])
        m["xtm"] = xb
        m["xT"] = np.ascontiguousarray(xb.T)
        m["x_own"] = np.ascontiguousarray(xb[half * TOWN:(half + 1) * TOWN])
        m["xT_own"] = np.ascontiguousarray(xb.T[:, half * TOWN:(half + 1) * TOWN])
        nm = np.zeros((128, NCH), f32)
        own = np.arange(NCH) // (NCH // 2) == half
        nm[:, own] = norm_w[:, None]
        m["nwm"] = nm
        in_maps.append(m)
    return in_maps


def kernel(**inputs):
    T = inputs["L0_attn_x"].shape[1]
    in_maps = prep_inputs(inputs, T=T)
    nc = build(T=T, skip_attn=False)
    res = run_bass_kernel_spmd(nc, in_maps, core_ids=list(range(NCORES)))
    B = inputs["L0_attn_x"].shape[0]
    TOWN = T // 2
    out = np.empty((B, T, D), np.float32)
    for core in range(NCORES):
        b, half = core // 2, core % 2
        out[b, half * TOWN:(half + 1) * TOWN] = res.results[core]["out_y"]
    return out


# version 17
# speedup vs baseline: 1.2080x; 1.0680x over previous
import numpy as np
import ml_dtypes
import concourse.bass as bass
import concourse.mybir as mybir
from concourse.bass_utils import run_bass_kernel_spmd

F32 = mybir.dt.float32
BF16 = mybir.dt.bfloat16
AF = mybir.ActivationFunctionType
ALU = mybir.AluOpType
AX = mybir.AxisListType

D = 1024
H = 8
K = 128
C = 64
HID = 4096
NE = 8
EPS_L2 = 1e-6
EPS_RMS = 1e-5
EPS_LN = 1e-5
RSQK = float(1.0 / np.sqrt(K))
NCORES = 8
QT = 512


class Sem:
    def __init__(self, handle):
        self.h = handle
        self.n = 0

    def inc(self, k=1):
        self.n += k
        return self.n


class Prog:
    ENGS = ("tensor", "vector", "scalar", "gpsimd", "sync")

    def __init__(self, block, own_sems):
        self.block = block
        self.q = {e: [] for e in self.ENGS}
        self.own = own_sems
        self.pend = {e: 0 for e in self.ENGS}

    def wait(self, eng, sem, val):
        if val > 0:
            self.q[eng].append(lambda E, s=sem, v=val: E.wait_ge(s.h, v))

    def op(self, eng, fn, inc=None, inc_n=1):
        if inc is None:
            self.q[eng].append(fn)
            return None
        v = inc.inc(inc_n)
        def run(E, fn=fn, s=inc, n=inc_n):
            fn(E).then_inc(s.h, n)
        self.q[eng].append(run)
        return v

    def sop(self, eng, fn):
        s = self.own[eng]
        self.wait(eng, s, self.pend[eng])
        v = self.op(eng, fn, inc=s, inc_n=1)
        self.pend[eng] = v
        return v

    def flush(self):
        for name in self.ENGS:
            lst = self.q[name]
            if lst:
                def body(E, lst=lst):
                    for f in lst:
                        f(E)
                getattr(self.block, name)(body)
        self.q = {e: [] for e in self.ENGS}


def host_constants():
    cst = {}
    eye = np.eye(128)
    cst["c_id_bf"] = eye.astype(ml_dtypes.bfloat16)
    cst["c_id_f32"] = eye.astype(np.float32)
    cst["c_ones_1x128"] = np.ones((1, 128), np.float32)
    cst["c_ones_128x1"] = np.ones((128, 1), np.float32)
    tril = np.tril(np.ones((C, C)))
    bd = np.zeros((128, 128)); bd[:C, :C] = tril.T; bd[C:, C:] = tril.T
    cst["c_bd_trilT"] = bd.astype(np.float32)
    bd2 = np.zeros((128, 128)); bd2[:C, :C] = 1.0; bd2[C:, C:] = 1.0
    cst["c_bd_ones"] = bd2.astype(np.float32)
    cst["c_negtril_i"] = np.tile(-np.tril(np.ones((C, C))), (2, 1)).astype(np.float32)
    cst["c_negtril_s"] = np.tile(-np.tril(np.ones((C, C)), -1), (2, 1)).astype(np.float32)
    cst["c_triu_i"] = np.tile(np.triu(np.ones((C, C))), (2, 1)).astype(np.float32)
    cst["c_triu_s"] = np.tile(np.triu(np.ones((C, C)), 1), (2, 1)).astype(np.float32)
    sel = np.zeros((H, 4 * 128))
    for hp in range(4):
        sel[hp, hp * 128:hp * 128 + 64] = 1.0
        sel[hp + 4, hp * 128 + 64:hp * 128 + 128] = 1.0
    cst["c_sel_hh4"] = sel.astype(np.float32)
    dup = np.zeros((128, 256))
    for p in range(128):
        dup[p % 64, p] = 1.0
        dup[64 + p % 64, 128 + p] = 1.0
    cst["c_dup64"] = dup.astype(np.float32)
    colsel = np.zeros((H, H * 128))
    for h in range(H):
        colsel[h, h * 128:(h + 1) * 128] = 1.0
    cst["c_colsel"] = colsel.astype(np.float32)
    return cst


def build(T=4096, skip_attn=False, dbg=(), stop_after=None):
    from contextlib import ExitStack
    nc = bass.Bass("TRN2", target_bir_lowering=False)
    TOWN = T // 2
    NTT = TOWN // 128
    T5W = min(512, TOWN)
    SUBS = T5W // 128
    NT512 = TOWN // T5W
    HQ = HID // 1024
    NQ = T // QT
    TTQ = QT // 128
    NCH = T // C

    dram = {}

    def din(name, shape, dt=F32):
        dram[name] = nc.dram_tensor(name, list(shape), dt, kind="ExternalInput").ap()
        return dram[name]

    for nm, arr in host_constants().items():
        din(nm, arr.shape, BF16 if arr.dtype == ml_dtypes.bfloat16 else F32)
    x_own = din("x_own", (TOWN, D))
    wgateT = din("wgateT", (NE, D, HID))
    wupT = din("wupT", (NE, D, HID))
    wdownT = din("wdownT", (NE, HID, D))
    wgT = din("wgT", (D, NE))
    bg_bc = din("bg_bc", (128, NE))
    if not skip_attn:
        xtm = din("xtm", (T, D))
        xT = din("xT", (D, T))
        xT_own = din("xT_own", (D, TOWN))
        wvT = din("wvT", (D, D))
        wgaT = din("wgaT", (D, D))
        woT = din("woT", (D, D))
        wabT = din("wabT", (D, 16))
        conv_wD = din("conv_wD", (128, 32))
        conv_bD = din("conv_bD", (128, 8))
        dtb_bc = din("dtb_bc", (128, H))
        alog_row = din("alog_row", (1, H))
        nwm = din("nwm", (128, NCH))
    out_y = nc.dram_tensor("out_y", [TOWN, D], F32, kind="ExternalOutput").ap()

    dbg_out = {}
    for name, shape, ddt in dbg:
        dbg_out[name] = nc.dram_tensor("dbg_" + name, list(shape), ddt,
                                       kind="ExternalOutput").ap()

    with ExitStack() as ctx:
        def mk_sb(stack, side="left"):
            def f(name, shape, dt):
                return stack.enter_context(
                    nc.sbuf_tensor(name, shape, dt, side=side))
            return f

        sb = mk_sb(ctx)
        id_bf = sb("id_bf", [128, 128], BF16)
        id_f32 = sb("id_f32", [128, 128], F32)
        ones_1x128 = sb("ones_1x128", [1, 128], F32)
        ones_128x1 = sb("ones_128x1", [128, 1], F32)
        bd_trilT = sb("bd_trilT", [128, 128], F32)
        bd_ones = sb("bd_ones", [128, 128], F32)
        negtril_i = sb("negtril_i", [128, C], F32)
        negtril_s = sb("negtril_s", [128, C], F32)
        triu_i = sb("triu_i", [128, C], F32)
        triu_s = sb("triu_s", [128, C], F32)
        sel_hh4 = sb("sel_hh4", [H, 4 * 128], F32)
        dup64 = sb("dup64", [128, 256], F32)
        colsel = sb("colsel", [H, H * 128], F32)
        wgT_sb = sb("wgT_sb", [128, 8, NE], BF16)
        bg_sb = sb("bg_sb", [128, NE], F32)
        wstage = sb("wstage", [128, NTT, NE], F32)
        lncol = sb("lncol", [128, 8], F32)
        l_sb = sb("l_sb", [128, NE], F32)
        t8a = sb("t8a", [128, NE], F32)
        t8b = sb("t8b", [128, NE], F32)
        pbanks = [ctx.enter_context(nc.psum_tensor(f"pb{i}", [128, 512], F32))
                  for i in range(8)]
        b0, b1, b2, b3, b4, b5, b6, b7 = pbanks
        h_ld = ctx.enter_context(nc.semaphore("s_ld"))
        h_lg = ctx.enter_context(nc.semaphore("s_lg"))
        h_pe = ctx.enter_context(nc.semaphore("s_pe"))
        h_ve = ctx.enter_context(nc.semaphore("s_ve"))
        h_ac = ctx.enter_context(nc.semaphore("s_ac"))
        h_gp = ctx.enter_context(nc.semaphore("s_gp"))
        h_st = ctx.enter_context(nc.semaphore("s_st"))
        block = ctx.enter_context(nc.Block())
        ld, lg, pe, ve, ac, gp, st = (Sem(h) for h in
                                      (h_ld, h_lg, h_pe, h_ve, h_ac, h_gp, h_st))
        pr = Prog(block, {"tensor": pe, "vector": ve, "scalar": ac,
                          "gpsimd": gp, "sync": st})

        def dma(eng, sem, dst, src):
            return pr.op(eng, lambda E, dd=dst, ss=src: E.dma_start(out=dd, in_=ss),
                         inc=sem, inc_n=16)

        def barrier_all(eng):
            pr.wait(eng, pe, pe.n)
            pr.wait(eng, ve, ve.n)
            pr.wait(eng, ac, ac.n)
            pr.wait(eng, gp, gp.n)

        def dbg_dump(name, src_ap):
            if name not in dbg_out:
                return
            barrier_all("sync")
            dma("sync", st, dbg_out[name], src_ap)

        for t_sb, nm in [
            (id_bf, "c_id_bf"), (id_f32, "c_id_f32"),
            (ones_1x128, "c_ones_1x128"), (ones_128x1, "c_ones_128x1"),
            (bd_trilT, "c_bd_trilT"), (bd_ones, "c_bd_ones"),
            (negtril_i, "c_negtril_i"), (negtril_s, "c_negtril_s"),
            (triu_i, "c_triu_i"), (triu_s, "c_triu_s"),
            (sel_hh4, "c_sel_hh4"), (colsel, "c_colsel"),
            (dup64, "c_dup64"),
        ]:
            dma("sync", ld, t_sb[:], dram[nm])
        dma("sync", ld, bg_sb[:], dram["bg_bc"])
        dma("gpsimd", lg, wgT_sb[:],
            dram["wgT"].rearrange("(dt p) e -> p dt e", p=128))
        v_cst = ld.n
        v_wgt = lg.n

        def ln_tile(dst_ap, src_ap, scr_half):
            pr.wait("scalar", ve, pr.pend["vector"])
            pr.sop("scalar", lambda E: E.activation(
                scr_half, src_ap[:, 0:512], AF.Square, accum_out=lncol[:, 0:1]))
            vb_ = pr.sop("scalar", lambda E: E.activation(
                scr_half, src_ap[:, 512:1024], AF.Square, accum_out=lncol[:, 1:2]))
            pr.sop("vector", lambda E: E.reduce_sum(
                lncol[:, 2:3], src_ap, axis=AX.X))
            pr.wait("vector", ac, vb_)
            pr.sop("vector", lambda E: E.tensor_add(
                lncol[:, 1:2], lncol[:, 0:1], lncol[:, 1:2]))
            pr.sop("vector", lambda E: E.tensor_scalar(
                lncol[:, 3:4], lncol[:, 2:3], 1.0 / D, None, ALU.mult))
            pr.sop("vector", lambda E: E.tensor_scalar(
                lncol[:, 4:5], lncol[:, 1:2], 1.0 / D, None, ALU.mult))
            pr.sop("vector", lambda E: E.tensor_mul(
                lncol[:, 5:6], lncol[:, 3:4], lncol[:, 3:4]))
            pr.sop("vector", lambda E: E.tensor_sub(
                lncol[:, 5:6], lncol[:, 4:5], lncol[:, 5:6]))
            v2 = pr.sop("vector", lambda E: E.tensor_scalar(
                lncol[:, 5:6], lncol[:, 5:6], EPS_LN, None, ALU.add))
            pr.wait("scalar", ve, v2)
            pr.sop("scalar", lambda E: E.activation(
                lncol[:, 6:7], lncol[:, 5:6], AF.Ln))
            va_ = pr.sop("scalar", lambda E: E.activation(
                lncol[:, 6:7], lncol[:, 6:7], AF.Exp, scale=-0.5))
            pr.sop("vector", lambda E: E.tensor_scalar(
                lncol[:, 7:8], lncol[:, 3:4], -1.0, None, ALU.mult))
            pr.wait("vector", ac, va_)
            return pr.sop("vector", lambda E: E.tensor_scalar(
                dst_ap, src_ap, lncol[:, 7:8], lncol[:, 6:7], ALU.add, ALU.mult))

        if not skip_attn:
            sA1 = ExitStack()
            sb1 = mk_sb(sA1, side="right")
            o_normT = sb1("o_normT", [128, H, TOWN], BF16)

            sA2 = ExitStack()
            sb2 = mk_sb(sA2, side="right")
            grow = sb2("grow", [H, QT], F32)
            nbrow = sb2("nbrow", [H, QT], F32)
            egvrow = sb2("egvrow", [H, QT], F32)
            eglrow = sb2("eglrow", [H, QT], F32)
            eglb = sb2("eglb", [128, H, NCH], F32)
            S32 = sb2("S32", [128, H, 128], F32)
            wab_sb = sb2("wab_sb", [128, 8, 16], F32)
            cw_sb = sb2("cw_sb", [128, 8, 4], F32)
            cb_sb = sb2("cb_sb", [128, 8], F32)
            dtb_sb = sb2("dtb_sb", [128, H], F32)
            alog_sb = sb2("alog_sb", [1, H], F32)
            nexpA = sb2("nexpA", [128, H], F32)
            nwm_sb = sb2("nwm_sb", [128, NCH], F32)
            wv_sb = sb2("wv_sb", [128, 8, D], BF16)
            xT_bf2 = [sb2("xT_bf_a", [128, 8, QT], BF16),
                      sb2("xT_bf_b", [128, 8, QT], BF16)]
            nT = sb2("nT", [128, H, QT + 1], BF16)
            xvT = sb2("xvT", [128, 8, QT + 3], BF16)
            v_fm = sb2("v_fm", [128, 8, QT], BF16)
            vb_tm = sb2("vb_tm", [128, TTQ, H, 128], BF16)
            CS = sb2("CS", [128, TTQ, 32], F32)
            CSd = sb2("CSd", [128, TTQ, 2, 32], F32)
            x_tm_r = sb2("x_tm_r", [128, 2, D], F32)
            xT32_r = sb2("xT32_r", [128, 2, 8, 128], F32)
            n_tm_r = sb2("n_tm_r", [128, 2, D], BF16)
            scrA = sb2("scrA", [128, 1, D], F32)
            ss8 = sb2("ss8", [128, 2, 8], F32)
            rstd8 = sb2("rstd8", [128, 2, 8], F32)
            sp16 = sb2("sp16", [128, 2, 16], F32)
            spw = sb2("spw", [128, 2, 24], F32)
            RS = sb2("RS", [128, 2, 32], F32)
            cv_r = sb2("cv_r", [128, 1, 512], F32)
            mdiff = sb2("mdiff", [128, 4, C], F32)
            e1t = sb2("e1t", [128, 4, C], F32)
            e2t = sb2("e2t", [128, 4, C], F32)
            e1sneg = sb2("e1sneg", [128, 4, C], BF16)
            e2s = sb2("e2s", [128, 4, C], BF16)
            e2zi = sb2("e2zi", [128, 4, C], BF16)
            t2m = sb2("t2m", [128, 4, C], BF16)
            Abd = sb2("Abd", [128, H, 128], BF16)
            ATbd = sb2("ATbd", [128, H, 128], BF16)
            PQ_sb = [sb2("PQ_sb0", [128, H, 256], BF16),
                     sb2("PQ_sb1", [128, H, 256], BF16)]
            X_sb = [sb2("X_sb0", [128, H, 256], BF16),
                    sb2("X_sb1", [128, H, 256], BF16)]
            Xf = sb2("Xf", [128, H, 256], BF16)
            wkcT = sb2("wkcT", [128, H, 128], BF16)
            wkbeg = sb2("wkbeg", [128, H, 128], BF16)
            wkdw = sb2("wkdw", [128, 2, H, 128], BF16)
            attnT_r = sb2("attnT_r", [128, 2, H, C], BF16)
            rk_eT8 = sb2("rk_eT8", [128, H, C], BF16)
            v_new8 = sb2("v_new8", [128, H, 128], BF16)
            o2b = sb2("o2b", [128, 2, 8, C], F32)
            rst2 = sb2("rst2", [1, 1024], F32)
            ostage = sb2("ostage", [128, 2, 8, C], F32)
            onrm_t = sb2("onrm_t", [128, 8, C], BF16)
            Sbf8 = sb2("Sbf8", [128, H, 128], BF16)

            dma("gpsimd", lg, wv_sb[:],
                dram["wvT"].rearrange("(dt p) dv -> p dt dv", p=128))
            v_lg_wv = lg.n
            dma("sync", ld, wab_sb[:],
                dram["wabT"].rearrange("(dt p) e -> p dt e", p=128))
            dma("sync", ld, cw_sb.ap().rearrange("p a b -> p (a b)"),
                dram["conv_wD"])
            dma("sync", ld, cb_sb[:], dram["conv_bD"])
            dma("sync", ld, dtb_sb[:], dram["dtb_bc"])
            dma("sync", ld, alog_sb[:], dram["alog_row"])
            v_nwm = dma("sync", ld, nwm_sb[:], dram["nwm"])
            pr.wait("scalar", ld, v_nwm)
            va = pr.sop("scalar", lambda E: E.activation(
                alog_sb[:], alog_sb[:], AF.Exp))
            pr.wait("tensor", ac, va)
            vp = pr.op("tensor", lambda E: E.matmul(
                b7[:, 0:8], ones_1x128[:], alog_sb[:], start=True, stop=True),
                inc=pe)
            pr.wait("vector", pe, vp)
            v_b7_free = pr.sop("vector", lambda E: E.tensor_scalar(
                nexpA[:], b7[:, 0:8], -1.0, None, ALU.mult))
            pr.op("vector", lambda E: E.memset(Abd[:], 0.0))
            pr.op("vector", lambda E: E.memset(ATbd[:], 0.0))
            pr.op("vector", lambda E: E.memset(S32[:], 0.0))
            pr.op("vector", lambda E: E.memset(nT[:, :, 0:1], 0.0))
            pr.op("vector", lambda E: E.memset(
                Sbf8.ap()[:, :, :].rearrange("p a b -> p (a b)"), 0.0))
            pr.sop("vector", lambda E: E.memset(xvT[:, :, 0:3], 0.0))
            sbf_st = {"v": pr.pend["vector"]}
            stageB_last = {"v": 0}

            halo = {"n": 0, "xv": 0}
            wkdw_done = [0, 0]
            b6_guard = {"v": 0}
            ostage_ev = {"v": 0}
            prep_guard = [{"ac": 0, "ve": 0, "pe": 0} for _ in range(2)]
            b3_done = {"v": 0}

            def evac(use_ac, dst, src):
                if use_ac:
                    return (ac, pr.sop("scalar",
                                       lambda E, dst=dst, src=src: E.copy(dst, src)))
                return (ve, pr.sop("vector",
                                   lambda E, dst=dst, src=src: E.tensor_copy(dst, src)))

            def solve_AB(q, tq):
                gt = q * TTQ + tq
                tr = gt % 2
                lcq0 = 2 * tq
                pr.wait("vector", pe, pe.n)
                pr.wait("scalar", pe, pe.n)
                pr.wait("gpsimd", pe, pe.n)
                pr.wait("scalar", ve, pr.pend["vector"])
                for lc in (0, 1):
                    gc = 2 * gt + lc
                    lcq = lcq0 + lc
                    pr.wait("tensor", ve, pr.pend["vector"])
                    pr.wait("tensor", ac, pr.pend["scalar"])
                    pr.wait("tensor", gp, pr.pend["gpsimd"])
                    if lc == 1:
                        pr.wait("vector", gp, pr.pend["gpsimd"])
                        pr.wait("scalar", gp, pr.pend["gpsimd"])
                        pr.wait("gpsimd", ve, pr.pend["vector"])
                        pr.wait("scalar", ve, pr.pend["vector"])
                    for hp in range(4):
                        for sub in (0, 1):
                            h = hp + 4 * sub
                            pp = 64 * sub
                            wk1 = nT.ap()[:, h, 64 * lcq:64 * lcq + 64]
                            rk1 = nT.ap()[:, h, 64 * lcq + 1:64 * lcq + 65]
                            tp = (0, pp)
                            pr.op("tensor", lambda E, hp=hp, pp=pp, wk1=wk1, tp=tp:
                                  E.matmul(b0[pp:pp + 64,
                                              128 * hp + pp:128 * hp + pp + 64],
                                           wk1, wk1, start=True, stop=True,
                                           tile_position=tp))
                            pr.op("tensor", lambda E, hp=hp, pp=pp, wk1=wk1,
                                  rk1=rk1, tp=tp:
                                  E.matmul(b1[pp:pp + 64,
                                              128 * hp + pp:128 * hp + pp + 64],
                                           wk1, rk1, start=True, stop=True,
                                           tile_position=tp))
                        pr.op("tensor", lambda E, hp=hp, lcq=lcq: E.matmul(
                            b3[:, 64 * hp:64 * hp + 64],
                            sel_hh4[:, hp * 128:(hp + 1) * 128],
                            grow[0:8, 64 * lcq:64 * lcq + 64],
                            start=True, stop=True))
                        pr.op("tensor", lambda E, hp=hp, lcq=lcq: E.matmul(
                            b3[:, 256 + 64 * hp:256 + 64 * hp + 64],
                            sel_hh4[:, hp * 128:(hp + 1) * 128],
                            nbrow[0:8, 64 * lcq:64 * lcq + 64],
                            start=True, stop=True),
                            inc=pe if hp == 3 else None)
                    v_A_mm = pe.n
                    pr.wait("vector", pe, v_A_mm)
                    pr.wait("gpsimd", pe, v_A_mm)
                    pr.wait("scalar", pe, v_A_mm)
                    p0 = 64 * lc
                    for hp in range(4):
                        pr.op("vector", lambda E, hp=hp, p0=p0, tq=tq:
                              E.tensor_scalar(
                                  mdiff[0:64, hp, :], b3[0:64, 64 * hp:64 * hp + 64],
                                  CS[p0:p0 + 64, tq, hp:hp + 1], None,
                                  ALU.subtract))
                        pr.op("vector", lambda E, hp=hp, p0=p0, tq=tq:
                              E.tensor_scalar(
                                  mdiff[64:128, hp, :],
                                  b3[64:128, 64 * hp:64 * hp + 64],
                                  CS[p0:p0 + 64, tq, 4 + hp:5 + hp], None,
                                  ALU.subtract),
                              inc=ve if hp == 3 else None)
                    v_md = ve.n
                    m3 = mdiff.ap()[:, :, :]
                    pr.wait("vector", ve, v_md)
                    pr.op("vector", lambda E: E.tensor_mul(
                        e1t.ap()[:, :, :], m3,
                        negtril_i.ap()[:, :].unsqueeze(1).broadcast_to([128, 4, C])))
                    v_m = pr.op("vector", lambda E: E.tensor_mul(
                        e2t.ap()[:, :, :], m3,
                        triu_i.ap()[:, :].unsqueeze(1).broadcast_to([128, 4, C])),
                        inc=ve)
                    pr.wait("scalar", ve, v_m)
                    pr.op("scalar", lambda E: E.activation(
                        e1t.ap()[:, :, :].rearrange("p a b -> p (a b)"),
                        e1t.ap()[:, :, :].rearrange("p a b -> p (a b)"), AF.Exp))
                    v_e = pr.op("scalar", lambda E: E.activation(
                        e2t.ap()[:, :, :].rearrange("p a b -> p (a b)"),
                        e2t.ap()[:, :, :].rearrange("p a b -> p (a b)"), AF.Exp),
                        inc=ac)
                    pr.wait("gpsimd", ac, v_e)
                    pr.op("gpsimd", lambda E: E.tensor_mul(
                        e1sneg.ap()[:, :, :], e1t.ap()[:, :, :],
                        negtril_s.ap()[:, :].unsqueeze(1).broadcast_to([128, 4, C])))
                    pr.op("gpsimd", lambda E: E.tensor_mul(
                        e2s.ap()[:, :, :], e2t.ap()[:, :, :],
                        triu_s.ap()[:, :].unsqueeze(1).broadcast_to([128, 4, C])))
                    v_g = pr.op("gpsimd", lambda E: E.tensor_mul(
                        e2zi.ap()[:, :, :], e2t.ap()[:, :, :],
                        triu_i.ap()[:, :].unsqueeze(1).broadcast_to([128, 4, C])),
                        inc=gp)
                    pr.wait("vector", gp, v_g)
                    v_t2 = pr.op("vector", lambda E: E.tensor_mul(
                        t2m.ap()[:, :, :], e2s.ap()[:, :, :],
                        b3.ap()[:, 256:512].rearrange("p (a b) -> p a b", a=4)),
                        inc=ve)
                    pr.wait("vector", ve, v_t2)
                    pr.op("vector", lambda E, lc=lc, tr=tr: E.scalar_tensor_tensor(
                        attnT_r.ap()[64 * lc:64 * lc + 64, tr, 0:4, :],
                        b1.ap()[0:64, 0:512].rearrange("p (a b) -> p a b", b=128)
                        [:, :, 0:64],
                        RSQK, e2zi.ap()[0:64, :, :], ALU.mult, ALU.mult))
                    pr.op("vector", lambda E, lc=lc, tr=tr: E.scalar_tensor_tensor(
                        attnT_r.ap()[64 * lc:64 * lc + 64, tr, 4:8, :],
                        b1.ap()[64:128, 0:512].rearrange("p (a b) -> p a b", b=128)
                        [:, :, 64:128],
                        RSQK, e2zi.ap()[64:128, :, :], ALU.mult, ALU.mult))
                    for hp in range(4):
                        for sub, h in ((0, hp), (1, hp + 4)):
                            pp = 64 * sub
                            pr.op("vector", lambda E, hp=hp, h=h, lc=lc,
                                  pp=pp, tq=tq: E.scalar_tensor_tensor(
                                      Abd[64 * lc:64 * lc + 64, h,
                                          64 * lc:64 * lc + 64],
                                      b0[pp:pp + 64, 128 * hp + pp:128 * hp + pp + 64],
                                      CSd[pp:pp + 64, tq, lc, 8 + h:9 + h],
                                      e1sneg[pp:pp + 64, hp, :],
                                      ALU.mult, ALU.mult))
                            pr.op("vector", lambda E, hp=hp, h=h, lc=lc,
                                  pp=pp: E.tensor_mul(
                                      ATbd[64 * lc:64 * lc + 64, h,
                                           64 * lc:64 * lc + 64],
                                      b0[pp:pp + 64, 128 * hp + pp:128 * hp + pp + 64],
                                      t2m[pp:pp + 64, hp, :]),
                                  inc=ve if hp == 3 and sub == 1 else None)
                    pr.pend["vector"] = ve.n
                b3_done["v"] = pr.pend["vector"]
                pr.wait("tensor", ve, pr.pend["vector"])
                v_tpB = [0, 0]
                for h in range(H):
                    bk = (b4 if h < 4 else b5)
                    bsl = bk[:, 64 * (h % 4):64 * (h % 4) + 64].bitcast(BF16)
                    pr.op("tensor", lambda E, h=h, bsl=bsl, lcq0=lcq0:
                          E.transpose(bsl,
                                      nT[:, h, 64 * lcq0:64 * lcq0 + 128],
                                      id_bf[:]),
                          inc=pe if h % 4 == 3 else None)
                    if h == 3:
                        v_tpB[0] = pe.n
                v_tpB[1] = pe.n
                if tq == 0:
                    nT0_pe["v"] = v_tpB[1]
                for h in range(H):
                    bk = (b4 if h < 4 else b5)
                    bsl = bk[:, 64 * (h % 4):64 * (h % 4) + 64].bitcast(BF16)
                    if h % 4 == 0:
                        pr.wait("vector", pe, v_tpB[h // 4])
                    pr.sop("vector", lambda E, h=h, bsl=bsl, tq=tq:
                           E.tensor_scalar(
                               wkbeg[:, h, :], bsl, CS[:, tq, 16 + h:17 + h],
                               None, ALU.mult))
                    pr.sop("vector", lambda E, h=h, bsl=bsl, tq=tq, tr=tr:
                           E.tensor_scalar(
                               wkdw[:, tr, h, :], bsl, CS[:, tq, 24 + h:25 + h],
                               None, ALU.mult))
                wkdw_done[tr] = pr.pend["vector"]
                stageB_last["v"] = pr.pend["vector"]
            def solve_C(q, tq):
                gt = q * TTQ + tq
                tr = gt % 2
                pr.wait("scalar", ve, pr.pend["vector"])
                def xreg(h):
                    bk = pbanks[h // 2]
                    return bk[:, 256 * (h % 2):256 * (h % 2) + 256]

                pr.wait("tensor", ve, pr.pend["vector"])
                v_mm = [0] * H
                vev = [(ve, 0)] * 4

                def evac_round(bank_dsts):
                    for use_ac in (False, True):
                        eng = "scalar" if use_ac else "vector"
                        sem = ac if use_ac else ve
                        pr.wait(eng, sem, pr.pend[eng])
                        banks = [k for k in range(4) if ((k % 2 == 1) == use_ac)]
                        n_ops = sum(len(bank_dsts(k)) for k in banks)
                        k_op = 0
                        for k in banks:
                            pr.wait(eng, pe, v_mm[k * 2 + 1])
                            for dst, srcv in bank_dsts(k):
                                k_op += 1
                                last = (k_op == n_ops)
                                if use_ac:
                                    pr.op(eng, lambda E, dst=dst, srcv=srcv:
                                          E.copy(dst, srcv),
                                          inc=sem if last else None)
                                else:
                                    pr.op(eng, lambda E, dst=dst, srcv=srcv:
                                          E.tensor_copy(dst, srcv),
                                          inc=sem if last else None)
                        pr.pend[eng] = sem.n
                        for k in range(4):
                            if ((k % 2 == 1) == use_ac):
                                vev[k] = (sem, sem.n)

                for h in range(H):
                    reg = xreg(h)
                    pr.op("tensor", lambda E, h=h, reg=reg, tq=tq: E.matmul(
                        reg[:, 0:128], ATbd[:, h, :], vb_tm[:, tq, h, :],
                        start=True, stop=False))
                    pr.op("tensor", lambda E, h=h, reg=reg, tq=tq: E.matmul(
                        reg[:, 0:128], id_bf[:], vb_tm[:, tq, h, :],
                        start=False, stop=True))
                    pr.op("tensor", lambda E, h=h, reg=reg: E.matmul(
                        reg[:, 128:256], ATbd[:, h, :], wkbeg[:, h, :],
                        start=True, stop=False))
                    v_mm[h] = pr.op("tensor", lambda E, h=h, reg=reg: E.matmul(
                        reg[:, 128:256], id_bf[:], wkbeg[:, h, :],
                        start=False, stop=True), inc=pe)
                evac_round(lambda k: [(X_sb[0].ap()[:, 2 * k:2 * k + 2, :]
                                       .rearrange("p a b -> p (a b)"),
                                       pbanks[k][:, 0:512])])
                if stop_after == "solveC0":
                    return
                for j in range(1, 6):
                    for h in range(H):
                        reg = xreg(h)
                        Pp = Abd[:, h, :] if j == 1 else \
                            PQ_sb[(j - 1) % 2][:, h, 0:128]
                        Qp = ATbd[:, h, :] if j == 1 else \
                            PQ_sb[(j - 1) % 2][:, h, 128:256]
                        pr.wait("tensor", vev[h // 2][0], vev[h // 2][1])
                        if j < 5:
                            pr.op("tensor", lambda E, reg=reg, Pp=Pp, Qp=Qp:
                                  E.matmul(reg[:, 0:128], Qp, Pp,
                                           start=True, stop=True))
                        v_mm[h] = pr.op("tensor", lambda E, reg=reg, Pp=Pp, Qp=Qp:
                                        E.matmul(reg[:, 128:256], Pp, Qp,
                                                 start=True, stop=True), inc=pe)
                    evac_round(lambda k, j=j: (
                        [(PQ_sb[j % 2].ap()[:, 2 * k:2 * k + 2, :]
                          .rearrange("p a b -> p (a b)"),
                          pbanks[k][:, 0:512])] if j < 5 else
                        [(PQ_sb[j % 2][:, 2 * k, 128:256],
                          pbanks[k][:, 128:256]),
                         (PQ_sb[j % 2][:, 2 * k + 1, 128:256],
                          pbanks[k][:, 384:512])]))
                    for h in range(H):
                        reg = xreg(h)
                        pr.wait("tensor", vev[h // 2][0], vev[h // 2][1])
                        pr.op("tensor", lambda E, h=h, reg=reg, j=j: E.matmul(
                            reg[:], PQ_sb[j % 2][:, h, 128:256],
                            X_sb[(j - 1) % 2][:, h, :],
                            start=True, stop=False))
                        v_mm[h] = pr.op("tensor", lambda E, h=h, reg=reg, j=j:
                                        E.matmul(reg[:], id_bf[:],
                                                 X_sb[(j - 1) % 2][:, h, :],
                                                 start=False, stop=True), inc=pe)
                    if j < 5:
                        evac_round(lambda k, j=j: [
                            (X_sb[j % 2].ap()[:, 2 * k:2 * k + 2, :]
                             .rearrange("p a b -> p (a b)"),
                             pbanks[k][:, 0:512])])
                    else:
                        evac_round(lambda k: [
                            (Xf.ap()[:, 2 * k:2 * k + 2, :]
                             .rearrange("p a b -> p (a b)"),
                             pbanks[k][:, 0:512])])
                if stop_after == "solveCJ":
                    return
                pr.wait("tensor", ve, pr.pend["vector"])
                pr.wait("tensor", ac, pr.pend["scalar"])
                v_tpW = [0, 0]
                for h in range(H):
                    bk = (b4 if h < 4 else b5)
                    bsl = bk[:, 64 * (h % 4):64 * (h % 4) + 64].bitcast(BF16)
                    pr.op("tensor", lambda E, h=h, bsl=bsl: E.transpose(
                        bsl, Xf[:, h, 128:256], id_bf[:]),
                        inc=pe if h % 4 == 3 else None)
                    if h == 3:
                        v_tpW[0] = pe.n
                v_tpW[1] = pe.n
                pr.wait("vector", pe, v_tpW[0])
                pr.sop("vector", lambda E: E.tensor_copy(
                    wkcT.ap()[:, 0:4, :].rearrange("p a b -> p (a b)"),
                    b4.ap()[:, 0:256].bitcast(BF16)))
                pr.wait("vector", pe, v_tpW[1])
                pr.sop("vector", lambda E: E.tensor_copy(
                    wkcT.ap()[:, 4:8, :].rearrange("p a b -> p (a b)"),
                    b5.ap()[:, 0:256].bitcast(BF16)))

            def rms_tile(q, tq):
                gt = q * TTQ + tq
                pr.wait("scalar", ve, ostage_ev["v"])
                va = pr.sop("scalar", lambda E: E.activation(
                    o2b.ap()[:, :, :, :].rearrange("p a b c -> p (a b c)"),
                    ostage.ap()[:, :, :, :].rearrange("p a b c -> p (a b c)"),
                    AF.Square))
                pr.wait("tensor", ac, va)
                pr.wait("tensor", ve, pr.pend["vector"])
                pr.wait("tensor", ve, b6_guard["v"])
                for lc, bk in ((0, b7), (1, b6)):
                    pr.op("tensor", lambda E, lc=lc, bk=bk: E.matmul(
                        bk[0:1, 0:256], ones_128x1[:],
                        o2b.ap()[:, lc, 0:4, :].rearrange("p a b -> p (a b)"),
                        start=True, stop=True))
                    pr.op("tensor", lambda E, lc=lc, bk=bk: E.matmul(
                        bk[0:1, 256:512], ones_128x1[:],
                        o2b.ap()[:, lc, 4:8, :].rearrange("p a b -> p (a b)"),
                        start=True, stop=True),
                        inc=pe if lc == 1 else None)
                v_ss = pe.n
                pr.wait("vector", pe, v_ss)
                pr.sop("vector", lambda E: E.tensor_scalar(
                    rst2[0:1, 0:512], b7[0:1, 0:512], 1.0 / K, EPS_RMS,
                    ALU.mult, ALU.add))
                v9 = pr.sop("vector", lambda E: E.tensor_scalar(
                    rst2[0:1, 512:1024], b6[0:1, 0:512], 1.0 / K, EPS_RMS,
                    ALU.mult, ALU.add))
                pr.wait("scalar", ve, v9)
                pr.sop("scalar", lambda E: E.activation(
                    rst2[:], rst2[:], AF.Ln))
                v11 = pr.sop("scalar", lambda E: E.activation(
                    rst2[:], rst2[:], AF.Exp, scale=-0.5))
                pr.wait("tensor", ve, v9)
                pr.wait("tensor", ac, v11)
                for lc, bk in ((0, b7), (1, b6)):
                    pr.op("tensor", lambda E, lc=lc, bk=bk: E.matmul(
                        bk[:], ones_1x128[:], rst2[0:1, 512 * lc:512 * lc + 512],
                        start=True, stop=True),
                        inc=pe if lc == 1 else None)
                v12 = pe.n
                pr.wait("vector", pe, v12)
                pr.sop("vector", lambda E: E.tensor_copy(
                    o2b.ap()[:, 0, :, :].rearrange("p a b -> p (a b)"), b7[:]))
                pr.sop("vector", lambda E: E.tensor_copy(
                    o2b.ap()[:, 1, :, :].rearrange("p a b -> p (a b)"), b6[:]))
                for lc in (0, 1):
                    gc = 2 * gt + lc
                    oc = (C * gc) % TOWN
                    second = (C * gc) >= TOWN
                    bkv = o2b.ap()[:, lc, :, :]
                    if not second:
                        pr.sop("vector", lambda E, gc=gc, oc=oc, lc=lc, bkv=bkv:
                               E.scalar_tensor_tensor(
                                   o_normT.ap()[:, :, oc:oc + C],
                                   ostage.ap()[:, lc, :, :],
                                   nwm_sb[:, gc:gc + 1], bkv,
                                   ALU.mult, ALU.mult))
                    else:
                        pr.sop("vector", lambda E, gc=gc, lc=lc, bkv=bkv:
                               E.scalar_tensor_tensor(
                                   onrm_t.ap()[:, :, :],
                                   ostage.ap()[:, lc, :, :],
                                   nwm_sb[:, gc:gc + 1], bkv,
                                   ALU.mult, ALU.mult))
                        pr.sop("vector", lambda E, oc=oc: E.tensor_add(
                            o_normT.ap()[:, :, oc:oc + C],
                            o_normT.ap()[:, :, oc:oc + C],
                            onrm_t.ap()[:, :, :]))
                b6_guard["v"] = pr.pend["vector"]

            def scan_chunk(q, tq, lc):
                gt = q * TTQ + tq
                tr = gt % 2
                gc = 2 * gt + lc
                lcq = 2 * tq + lc
                oc = (C * gc) % TOWN
                second = (C * gc) >= TOWN
                pr.wait("vector", pe, pe.n)
                pr.wait("scalar", pe, pe.n)
                pr.wait("tensor", ve, max(sbf_st["v"], pr.pend["vector"]))
                for h in range(H):
                    bk = b4 if h < 4 else b5
                    pr.op("tensor", lambda E, h=h, bk=bk, tr=tr, lc=lc: E.matmul(
                        bk[64 * lc:64 * lc + 64, 128 * (h % 4):128 * (h % 4) + 128],
                        wkcT[:, h, 64 * lc:64 * lc + 64], Sbf8[:, h, :],
                        start=True, stop=True, tile_position=(0, 64 * lc)),
                        inc=pe if h % 4 == 3 else None)
                v_vn_mm = pe.n
                pr.wait("vector", pe, v_vn_mm)
                pr.wait("vector", ve, pr.pend["vector"])
                pr.op("vector", lambda E, lc=lc: E.tensor_sub(
                    v_new8.ap()[64 * lc:64 * lc + 64, 0:4, :],
                    Xf.ap()[64 * lc:64 * lc + 64, 0:4, 0:128],
                    b4.ap()[64 * lc:64 * lc + 64, :].rearrange(
                        "p (a b) -> p a b", a=4)))
                v_vnew = pr.op("vector", lambda E, lc=lc: E.tensor_sub(
                    v_new8.ap()[64 * lc:64 * lc + 64, 4:8, :],
                    Xf.ap()[64 * lc:64 * lc + 64, 4:8, 0:128],
                    b5.ap()[64 * lc:64 * lc + 64, :].rearrange(
                        "p (a b) -> p a b", a=4)), inc=ve)
                pr.pend["vector"] = v_vnew
                pr.wait("tensor", ve, b6_guard["v"])
                for h in range(H):
                    pr.op("tensor", lambda E, h=h, lcq=lcq: E.matmul(
                        b6[:, 64 * h:64 * h + 64],
                        colsel[:, h * 128:(h + 1) * 128],
                        egvrow[0:8, 64 * lcq:64 * lcq + 64],
                        start=True, stop=True),
                        inc=pe if h == 7 else None)
                v_egb = pe.n
                pr.wait("vector", pe, v_egb)
                pr.wait("vector", ve, pr.pend["vector"])
                v_rke = pr.op("vector", lambda E, lcq=lcq: E.tensor_mul(
                    rk_eT8.ap()[:, :, :],
                    nT.ap()[:, :, 64 * lcq + 1:64 * lcq + 65],
                    b6.ap()[:, :].rearrange("p (a b) -> p a b", a=8)), inc=ve)
                pr.pend["vector"] = v_rke
                b6_guard["v"] = v_rke
                pr.wait("tensor", ve, v_rke)
                for h in range(H):
                    bk = b4 if h < 4 else b5
                    sl = bk[:, 64 * (h % 4):64 * (h % 4) + 64]
                    pr.op("tensor", lambda E, h=h, sl=sl: E.matmul(
                        sl, Sbf8[:, h, :], rk_eT8[:, h, :],
                        start=True, stop=False))
                    pr.op("tensor", lambda E, h=h, sl=sl, tr=tr, lc=lc: E.matmul(
                        sl, v_new8[64 * lc:64 * lc + 64, h, :],
                        attnT_r[64 * lc:64 * lc + 64, tr, h, :],
                        start=False, stop=True),
                        inc=pe if h % 4 == 3 else None)
                v_oT = pe.n
                pr.wait("vector", pe, v_oT)
                pr.wait("scalar", pe, v_oT)
                v_ev4 = pr.sop("vector", lambda E, lc=lc: E.tensor_copy(
                    ostage.ap()[:, lc, 0:4, :].rearrange("p a b -> p (a b)"),
                    b4[:, 0:256]))
                v_ev5 = pr.sop("scalar", lambda E, lc=lc: E.copy(
                    ostage.ap()[:, lc, 4:8, :].rearrange("p a b -> p (a b)"),
                    b5[:, 0:256]))
                ostage_ev["v"] = v_ev4
                pr.wait("tensor", ve, v_ev4)
                pr.wait("tensor", ac, v_ev5)
                pr.wait("tensor", ve, wkdw_done[tr])
                for h in range(H):
                    bk = b4 if h < 4 else b5
                    pr.op("tensor", lambda E, h=h, bk=bk, tr=tr, lc=lc: E.matmul(
                        bk[:, 128 * (h % 4):128 * (h % 4) + 128],
                        wkdw[64 * lc:64 * lc + 64, tr, h, :],
                        v_new8[64 * lc:64 * lc + 64, h, :],
                        start=True, stop=True),
                        inc=pe if h % 4 == 3 else None)
                v_U = pe.n
                pr.wait("vector", pe, v_U)
                pr.wait("vector", ve, pr.pend["vector"])
                for h in range(H):
                    bk = b4 if h < 4 else b5
                    pr.op("vector", lambda E, h=h, bk=bk, gc=gc:
                          E.scalar_tensor_tensor(
                              S32[:, h, :], S32[:, h, :],
                              eglb[:, h, gc:gc + 1],
                              bk[:, 128 * (h % 4):128 * (h % 4) + 128],
                              ALU.mult, ALU.add),
                          inc=ve if h == 7 else None)
                pr.pend["vector"] = ve.n
                sbf_st["v"] = pr.op("vector", lambda E: E.tensor_copy(
                    Sbf8.ap()[:, :, :].rearrange("p a b -> p (a b)"),
                    S32.ap()[:, :, :].rearrange("p a b -> p (a b)")), inc=ve)
                pr.pend["vector"] = sbf_st["v"]

            vpath_done = [0, 0]
            nT_done_st = {"v": 0}
            nT0_pe = {"v": 0}
            for q in range(NQ):
                qs = q * QT
                xT_bf = xT_bf2[q % 2]
                if q > 0:
                    pr.wait("vector", ac, nT_done_st["v"])
                    pr.wait("vector", pe, nT0_pe["v"])
                    halo["n"] = pr.sop("vector", lambda E: E.tensor_copy(
                        nT[:, :, 0:1], nT[:, :, QT:QT + 1]))
                    halo["xv"] = pr.sop("vector", lambda E: E.tensor_copy(
                        xvT[:, :, 0:3], xvT[:, :, QT:QT + 3]))
                pr.wait("gpsimd", pe, vpath_done[q % 2])
                v_xTbf = dma("gpsimd", lg, xT_bf[:],
                             dram["xT"].rearrange("(dt p) t -> p dt t", p=128)
                             [:, :, qs:qs + QT])

                pr.wait("tensor", lg, v_xTbf)
                pr.wait("tensor", lg, v_lg_wv)
                pr.wait("tensor", ve, b3_done["v"])
                for dv in range(8):
                    for i5 in range(QT // 512):
                        for dt in range(8):
                            pr.op("tensor", lambda E, dv=dv, dt=dt, i5=i5,
                                  xT_bf=xT_bf: E.matmul(
                                b3[:], wv_sb[:, dt, dv * 128:(dv + 1) * 128],
                                xT_bf[:, dt, i5 * 512:(i5 + 1) * 512],
                                start=(dt == 0), stop=(dt == 7)),
                                inc=pe if dt == 7 else None)
                        v_mm = pe.n
                        pr.wait("vector", pe, v_mm)
                        if q > 0 and dv == 0 and i5 == 0:
                            pr.wait("vector", ve, halo["xv"])
                        v_evac = pr.sop("vector", lambda E, dv=dv, i5=i5:
                                        E.tensor_copy(
                                            xvT[:, dv, 3 + 512 * i5:515 + 512 * i5],
                                            b3[:]))
                        pr.wait("tensor", ve, v_evac)
                        rr = 0
                        pr.wait("vector", ac, pr.pend["scalar"])
                        pr.sop("vector", lambda E, dv=dv, i5=i5, rr=rr: E.tensor_scalar(
                            cv_r[:, rr, :], xvT[:, dv, 512 * i5:512 * i5 + 512],
                            cw_sb[:, dv, 0:1], cb_sb[:, dv:dv + 1],
                            ALU.mult, ALU.add))
                        for j in (1, 2, 3):
                            pr.sop("vector", lambda E, dv=dv, i5=i5, rr=rr, j=j:
                                   E.scalar_tensor_tensor(
                                       cv_r[:, rr, :],
                                       xvT[:, dv, j + 512 * i5:j + 512 * i5 + 512],
                                       cw_sb[:, dv, j:j + 1], cv_r[:, rr, :],
                                       ALU.mult, ALU.add))
                        v_cv = pr.pend["vector"]
                        pr.wait("scalar", ve, v_cv)
                        pr.sop("scalar", lambda E, dv=dv, i5=i5, rr=rr: E.activation(
                            v_fm[:, dv, 512 * i5:512 * i5 + 512], cv_r[:, rr, :],
                            AF.Silu))
                vpath_done[q % 2] = pe.n

                for tq in range(TTQ):
                    r = tq % 2
                    ts0 = qs + tq * 128
                    pr.wait("sync", ve, prep_guard[r]["ve"])
                    pr.wait("sync", ac, prep_guard[r]["ac"])
                    pr.wait("sync", pe, prep_guard[r]["pe"])
                    v_x = dma("sync", ld, x_tm_r[:, r, :], xtm[ts0:ts0 + 128, :])
                    v_xT = dma("sync", ld, xT32_r[:, r, :, :],
                               dram["xT"].rearrange("(dt p) t -> p dt t", p=128)
                               [:, :, ts0:ts0 + 128])
                    pr.wait("scalar", ld, v_x)
                    pr.wait("scalar", ve, pr.pend["vector"])
                    va = pr.sop("scalar", lambda E, r=r: E.activation(
                        scrA[:, 0, :], x_tm_r[:, r, :], AF.Square))
                    v_sq = va
                    pr.wait("vector", ac, va)
                    pr.sop("vector", lambda E, r=r: E.reduce_sum(
                        ss8[:, r, :],
                        scrA.ap()[:, 0, :].rearrange("p (h k) -> p h k", h=H),
                        axis=AX.X))
                    v1 = pr.sop("vector", lambda E, r=r: E.tensor_scalar(
                        ss8[:, r, :], ss8[:, r, :], EPS_L2, None, ALU.add))
                    pr.wait("scalar", ve, v1)
                    pr.sop("scalar", lambda E, r=r: E.activation(
                        rstd8[:, r, :], ss8[:, r, :], AF.Ln))
                    va = pr.sop("scalar", lambda E, r=r: E.activation(
                        rstd8[:, r, :], rstd8[:, r, :], AF.Exp, scale=-0.5))
                    pr.wait("vector", ac, va)
                    v_ntm = pr.sop("vector", lambda E, r=r: E.tensor_mul(
                        n_tm_r.ap()[:, r, :].rearrange("p (h k) -> p h k", h=H),
                        x_tm_r.ap()[:, r, :].rearrange("p (h k) -> p h k", h=H),
                        rstd8.ap()[:, r, :].unsqueeze(2).broadcast_to([128, H, 128])))
                    pr.wait("tensor", ve, v_ntm)
                    pr.wait("tensor", ac, pr.pend["scalar"])
                    v_tp = [0, 0]
                    for h in range(H):
                        bk = b4 if h < 4 else b5
                        bsl = bk[:, 64 * (h % 4):64 * (h % 4) + 64].bitcast(BF16)
                        pr.op("tensor", lambda E, r=r, h=h, bsl=bsl: E.transpose(
                            bsl, n_tm_r[:, r, h * 128:(h + 1) * 128], id_bf[:]),
                            inc=pe if h % 4 == 3 else None)
                        if h == 3:
                            v_tp[0] = pe.n
                    v_tp[1] = pe.n
                    if q > 0 and tq == 0:
                        pr.wait("scalar", ve, halo["n"])
                    pr.wait("scalar", pe, v_tp[0])
                    pr.sop("scalar", lambda E, tq=tq: E.copy(
                        nT.ap()[:, 0:4, tq * 128 + 1:tq * 128 + 129],
                        b4.ap()[:, 0:256].bitcast(BF16).rearrange(
                            "p (a b) -> p a b", a=4)))
                    pr.wait("scalar", pe, v_tp[1])
                    pr.sop("scalar", lambda E, tq=tq: E.copy(
                        nT.ap()[:, 4:8, tq * 128 + 1:tq * 128 + 129],
                        b5.ap()[:, 0:256].bitcast(BF16).rearrange(
                            "p (a b) -> p a b", a=4)))
                    v_nT_done = pr.pend["scalar"]
                    pr.wait("tensor", ac, v_nT_done)
                    pr.wait("tensor", ld, v_xT)
                    pr.wait("tensor", ld, v_cst)
                    pr.wait("tensor", ve, v_b7_free)
                    pr.wait("tensor", ve, pr.pend["vector"])
                    pr.wait("tensor", ac, pr.pend["scalar"])
                    for dt in range(8):
                        pr.op("tensor", lambda E, r=r, dt=dt: E.matmul(
                            b7[:, 0:16], xT32_r[:, r, dt, :], wab_sb[:, dt, :],
                            start=(dt == 0), stop=(dt == 7)),
                            inc=pe if dt == 7 else None)
                    v_xab = pe.n
                    pr.wait("vector", pe, v_xab)
                    r2 = r
                    v_b7_free = pr.sop("vector", lambda E, r2=r2: E.tensor_copy(
                        sp16[:, r2, :], b7[:, 0:16]))
                    pr.sop("vector", lambda E, r2=r2: E.tensor_add(
                        spw[:, r2, 0:8], sp16[:, r2, 0:8], dtb_sb[:]))
                    v1 = pr.sop("vector", lambda E, r2=r2: E.tensor_scalar(
                        spw[:, r2, 8:16], spw[:, r2, 0:8], 0.0, None, ALU.max))
                    pr.wait("scalar", ve, v1)
                    pr.sop("scalar", lambda E, r2=r2: E.activation(
                        spw[:, r2, 16:24], spw[:, r2, 0:8], AF.Abs))
                    pr.sop("scalar", lambda E, r2=r2: E.activation(
                        spw[:, r2, 16:24], spw[:, r2, 16:24], AF.Exp, scale=-1.0))
                    va = pr.sop("scalar", lambda E, r2=r2: E.activation(
                        spw[:, r2, 16:24], spw[:, r2, 16:24], AF.Ln, bias=1.0))
                    pr.wait("vector", ac, va)
                    pr.sop("vector", lambda E, r2=r2: E.tensor_add(
                        spw[:, r2, 0:8], spw[:, r2, 8:16], spw[:, r2, 16:24]))
                    v1 = pr.sop("vector", lambda E, r2=r2: E.tensor_mul(
                        spw[:, r2, 0:8], spw[:, r2, 0:8], nexpA[:]))
                    va = pr.sop("scalar", lambda E, r2=r2, tq=tq: E.activation(
                        CS[:, tq, 8:16], sp16[:, r2, 8:16], AF.Sigmoid))
                    pr.wait("tensor", ve, v1)
                    pr.wait("tensor", ac, va)
                    pr.op("tensor", lambda E, r2=r2: E.matmul(
                        b6[:, 0:8], bd_trilT[:], spw[:, r2, 0:8],
                        start=True, stop=True))
                    vp = pr.op("tensor", lambda E, r2=r2: E.matmul(
                        b6[:, 8:16], bd_ones[:], spw[:, r2, 0:8],
                        start=True, stop=True), inc=pe)
                    pr.wait("vector", pe, vp)
                    pr.sop("vector", lambda E, tq=tq: E.tensor_copy(
                        CS[:, tq, 0:8], b6[:, 0:8]))
                    v1 = pr.sop("vector", lambda E, r2=r2, tq=tq: E.tensor_sub(
                        spw[:, r2, 8:16], b6[:, 8:16], CS[:, tq, 0:8]))
                    pr.wait("scalar", ve, v1)
                    pr.sop("scalar", lambda E, r2=r2, tq=tq: E.activation(
                        CS[:, tq, 24:32], spw[:, r2, 8:16], AF.Exp))
                    pr.sop("scalar", lambda E, r2=r2, tq=tq: E.activation(
                        spw[:, r2, 16:24], CS[:, tq, 0:8], AF.Exp))
                    va = pr.sop("scalar", lambda E, r2=r2: E.activation(
                        RS[:, r2, 24:32], b6[:, 8:16], AF.Exp))
                    v_b6_free = va
                    pr.wait("vector", ac, va)
                    pr.sop("vector", lambda E, r2=r2, tq=tq: E.tensor_mul(
                        CS[:, tq, 16:24], CS[:, tq, 8:16], spw[:, r2, 16:24]))
                    pr.sop("vector", lambda E, r2=r2, tq=tq: E.tensor_copy(
                        RS[:, r2, 0:8], CS[:, tq, 0:8]))
                    pr.sop("vector", lambda E, r2=r2, tq=tq: E.tensor_scalar(
                        RS[:, r2, 8:16], CS[:, tq, 8:16], -1.0, None, ALU.mult))
                    v_rs = pr.sop("vector", lambda E, r2=r2: E.tensor_scalar(
                        RS[:, r2, 16:24], spw[:, r2, 16:24], RSQK, None, ALU.mult))
                    pr.wait("tensor", ve, v_rs)
                    pr.wait("tensor", ac, v_b6_free)
                    rows4 = [(grow, 0), (nbrow, 8), (egvrow, 16), (eglrow, 24)]
                    for gi, (rowt, off) in enumerate(rows4):
                        pr.op("tensor", lambda E, r2=r2, off=off, gi=gi: E.transpose(
                            b6[0:8, 128 * gi:128 * gi + 128],
                            RS[:, r2, off:off + 8], id_f32[:]),
                            inc=pe if gi == 3 else None)
                    v_rt = pe.n
                    pr.wait("scalar", pe, v_rt)
                    ts0q = ts0 - qs
                    for gi, (rowt, off) in enumerate(rows4):
                        pr.sop("scalar", lambda E, rowt=rowt, gi=gi, ts0q=ts0q: E.copy(
                            rowt[0:8, ts0q:ts0q + 128],
                            b6[0:8, 128 * gi:128 * gi + 128]))
                    v_rows = pr.pend["scalar"]
                    pr.wait("tensor", ac, v_rows)
                    pr.wait("tensor", ve, pr.pend["vector"])
                    pr.op("tensor", lambda E, tq=tq: E.matmul(
                        b6[:, 0:32], dup64[:, 0:128], CS[:, tq, :],
                        start=True, stop=True))
                    vdp = pr.op("tensor", lambda E, tq=tq: E.matmul(
                        b6[:, 32:64], dup64[:, 128:256], CS[:, tq, :],
                        start=True, stop=True), inc=pe)
                    pr.wait("vector", pe, vdp)
                    pr.sop("vector", lambda E, tq=tq: E.tensor_copy(
                        CSd[:, tq, 0, :], b6[:, 0:32]))
                    v_csd = pr.sop("vector", lambda E, tq=tq: E.tensor_copy(
                        CSd[:, tq, 1, :], b6[:, 32:64]))
                    pr.wait("tensor", ve, v_csd)
                    prep_guard[r] = {"ac": v_sq, "ve": v_ntm, "pe": v_xab}
                nT_done_st["v"] = v_nT_done

                ch0 = qs // C
                nchq = QT // C
                for h in range(H):
                    vp = pr.op("tensor", lambda E, h=h, nchq=nchq: E.matmul(
                        b6[:, 0:nchq], colsel[:, h * 128:(h + 1) * 128],
                        eglrow[0:8, 0:QT:64], start=True, stop=True), inc=pe)
                    pr.wait("vector", pe, vp)
                    vv = pr.sop("vector", lambda E, h=h, ch0=ch0, nchq=nchq:
                                E.tensor_copy(
                                    eglb[:, h, ch0:ch0 + nchq], b6[:, 0:nchq]))
                    pr.wait("tensor", ve, vv)

                if q == 0:
                    dbg_dump("eglb2", eglb[:])
                v_vfm = pr.pend["scalar"]
                pr.wait("tensor", ac, v_vfm)
                pr.wait("tensor", ve, pr.pend["vector"])
                for tq in range(TTQ):
                    v_tpv = [0, 0]
                    for h in range(H):
                        bk = b4 if h < 4 else b5
                        bsl = bk[:, 64 * (h % 4):64 * (h % 4) + 64].bitcast(BF16)
                        pr.op("tensor", lambda E, h=h, tq=tq, bsl=bsl:
                              E.transpose(bsl,
                                          v_fm[:, h, tq * 128:(tq + 1) * 128],
                                          id_bf[:]),
                              inc=pe if h % 4 == 3 else None)
                        if h == 3:
                            v_tpv[0] = pe.n
                    v_tpv[1] = pe.n
                    for h in range(H):
                        bk = b4 if h < 4 else b5
                        bsl = bk[:, 64 * (h % 4):64 * (h % 4) + 64].bitcast(BF16)
                        if h % 4 == 0:
                            pr.wait("vector", pe, v_tpv[h // 4])
                        pr.sop("vector", lambda E, h=h, tq=tq, bsl=bsl:
                               E.tensor_scalar(
                                   vb_tm[:, tq, h, :], bsl,
                                   CS[:, tq, 8 + h:9 + h], None, ALU.mult))
                    pr.wait("tensor", ve, pr.pend["vector"])

                if q == 0:
                    dbg_dump("eglrow", eglrow[:])
                    dbg_dump("egvrow", egvrow[:])
                    dbg_dump("grow", grow[:])
                    dbg_dump("nT", nT[:])
                    dbg_dump("CS", CS[:])
                    dbg_dump("vb", vb_tm[:])
                    dbg_dump("vfm", v_fm[:])

                b6_guard["v"] = pr.pend["vector"]
                for tq in range(TTQ):
                    if stop_after == "prep":
                        break
                    solve_AB(q, tq)
                    if stop_after in ("solveA", "solveB"):
                        continue
                    if tq > 0:
                        for lc in (0, 1):
                            scan_chunk(q, tq - 1, lc)
                        rms_tile(q, tq - 1)
                    solve_C(q, tq)
                    if q == 0 and tq == 0:
                        dbg_dump("mdiff", mdiff[:])
                        dbg_dump("e1t", e1t[:])
                        dbg_dump("t2m", t2m[:])
                        dbg_dump("Abd", Abd[:])
                        dbg_dump("ATbd", ATbd[:])
                        dbg_dump("Xf0", Xf[:])
                        dbg_dump("attnT0", attnT_r[:])
                        dbg_dump("wkcT0", wkcT[:])
                        dbg_dump("wkdw0", wkdw[:])
                    if stop_after in ("solve", "solveA", "solveB", "solveC0",
                                      "solveCJ"):
                        continue
                    if tq == TTQ - 1:
                        for lc in (0, 1):
                            scan_chunk(q, tq, lc)
                        rms_tile(q, tq)
                    if q == 0 and tq == 0:
                        dbg_dump("S0", S32[:])
                        dbg_dump("vnew0", v_new8[:])
                        dbg_dump("eglb", eglb[:])
                    if "Strace" in dbg_out:
                        gti = q * TTQ + tq
                        dbg_dump_noname = None
                        barrier_all("sync")
                        dma("sync", st, dbg_out["Strace"][:, gti, :, :], S32[:])

            dbg_dump("onrm", o_normT[:])
            if stop_after in ("prep", "solve", "scan", "solveA", "solveB",
                              "solveC0", "solveCJ"):
                barrier_all("vector")
                vz = pr.op("vector", lambda E: E.memset(scrA[:, 0, :], 0.0),
                           inc=ve)
                pr.wait("sync", ve, vz)
                for tt in range(NTT):
                    dma("sync", st,
                        out_y.rearrange("(tt p) d -> p tt d", p=128)[:, tt, :],
                        scrA[:, 0, :])
                pr.wait("sync", st, st.n)
                pr.flush()
                sA2.close()
                sA1.close()
                return nc
            pr.flush()
            sA2.close()

            yacc = sb("yacc", [128, NTT, D], F32)
            y1T = sb("y1T", [128, 8, TOWN], BF16)
            wga_sb = sb1("wga_sb", [128, 8, D], BF16)
            wo_sb = sb1("wo_sb", [128, 8, D], BF16)
            xTc_r = sb1("xTc_r", [128, 8, 512], BF16)
            og_st = sb1("og_st", [128, 8, 512], BF16)
            xo_r = sb1("xo_r", [128, 2, D], F32)
            gsl_r = sb1("gsl_r", [128, 2, 512], BF16)
            sq_c = sb1("sq_c", [128, 512], F32)

            barrier_all("gpsimd")
            barrier_all("sync")
            v_wga = dma("gpsimd", lg, wga_sb[:],
                        dram["wgaT"].rearrange("(dt p) dd -> p dt dd", p=128))
            v_wo = dma("gpsimd", lg, wo_sb[:],
                       dram["woT"].rearrange("(dt p) dd -> p dt dd", p=128))
            barrier_all("tensor")
            pr.wait("tensor", lg, v_wo)
            gsl_g = [0, 0]
            res_g = [0, 0]
            for t5 in range(NT512):
                pr.wait("gpsimd", ve, pr.pend["vector"])
                v_xTc = dma("gpsimd", lg, xTc_r[:, :, 0:T5W],
                            dram["xT_own"].rearrange("(dt p) t -> p dt t", p=128)
                            [:, :, t5 * T5W:(t5 + 1) * T5W])
                pr.wait("tensor", lg, v_xTc)
                for hd in range(8):
                    gb = pbanks[hd % 2]
                    pr.wait("tensor", ve, gsl_g[hd % 2])
                    for dt in range(8):
                        pr.op("tensor", lambda E, hd=hd, dt=dt, gb=gb: E.matmul(
                            gb[:, 0:T5W], wga_sb[:, dt, hd * 128:(hd + 1) * 128],
                            xTc_r[:, dt, 0:T5W], start=(dt == 0), stop=(dt == 7)),
                            inc=pe if dt == 7 else None)
                    v_g = pe.n
                    pr.wait("scalar", pe, v_g)
                    va = pr.sop("scalar", lambda E, hd=hd, gb=gb: E.activation(
                        gsl_r[:, hd % 2, 0:T5W], gb[:, 0:T5W], AF.Silu))
                    pr.wait("vector", ac, va)
                    vv = pr.sop("vector", lambda E, hd=hd, t5=t5: E.tensor_mul(
                        og_st[:, hd, 0:T5W], gsl_r[:, hd % 2, 0:T5W],
                        o_normT[:, hd, t5 * T5W:(t5 + 1) * T5W]))
                    gsl_g[hd % 2] = vv
                v_og = pr.pend["vector"]
                if t5 == 0:
                    dbg_dump("og0", og_st[:])
                pr.wait("tensor", ve, v_og)
                for sub in range(SUBS):
                    tt = t5 * SUBS + sub
                    rr = sub % 2
                    pr.wait("sync", ve, res_g[rr])
                    v_xo = dma("sync", ld, xo_r[:, rr, :],
                               x_own[tt * 128:(tt + 1) * 128, :])
                    for half in range(2):
                        bk = pbanks[2 + half]
                        pr.wait("tensor", ve, res_g[half])
                        for f in range(8):
                            pr.op("tensor", lambda E, f=f, sub=sub, half=half, bk=bk:
                                  E.matmul(bk[:],
                                           og_st[:, f, sub * 128:(sub + 1) * 128],
                                           wo_sb[:, f, half * 512:(half + 1) * 512],
                                           start=(f == 0), stop=(f == 7)),
                                  inc=pe if f == 7 else None)
                        v_wo_mm = pe.n
                        pr.wait("vector", pe, v_wo_mm)
                        pr.wait("vector", ld, v_xo)
                        res_g[half] = pr.sop(
                            "vector", lambda E, tt=tt, half=half, rr=rr, bk=bk:
                            E.tensor_add(yacc[:, tt, half * 512:(half + 1) * 512],
                                         xo_r[:, rr, half * 512:(half + 1) * 512],
                                         bk[:]))
                    ln_tile(yacc[:, tt, :], yacc[:, tt, :], sq_c[:])
            dbg_dump("yacc1", yacc[:])
            yacc_ready = (ve, pr.pend["vector"])
            pr.flush()
            sA1.close()
        else:
            yacc = sb("yacc", [128, NTT, D], F32)
            y1T = sb("y1T", [128, 8, TOWN], BF16)
            v_x = dma("sync", ld, yacc[:],
                      x_own.rearrange("(tt p) d -> p tt d", p=128))
            yacc_ready = (ld, v_x)

        scr1 = sb("scr1", [128, 512], F32)
        wgq = [sb("wgq_a", [128, 8, 1024], BF16), sb("wgq_b", [128, 8, 1024], BF16)]
        wuq = [sb("wuq_a", [128, 8, 1024], BF16), sb("wuq_b", [128, 8, 1024], BF16)]
        wdq = sb("wdq", [128, 8, 1024], BF16)
        hstage = sb("hstage", [128, 8, 512], BF16)
        gsil = [sb("gsil_a", [128, 512], BF16), sb("gsil_b", [128, 512], BF16)]

        def issue_wgu(e, hq, buf):
            dma("gpsimd", lg, wgq[buf][:],
                wgateT[e].rearrange("(dt p) Hh -> p dt Hh", p=128)
                [:, :, hq * 1024:(hq + 1) * 1024])
            dma("gpsimd", lg, wuq[buf][:],
                wupT[e].rearrange("(dt p) Hh -> p dt Hh", p=128)
                [:, :, hq * 1024:(hq + 1) * 1024])
            return lg.n

        def issue_wd(e, hq):
            dma("gpsimd", lg, wdq[:],
                wdownT[e][hq * 1024:(hq + 1) * 1024, :]
                .rearrange("(ht p) do -> p ht do", p=128))
            return lg.n

        pr.wait("gpsimd", pe, pe.n)
        pr.wait("gpsimd", ve, pr.pend["vector"])
        pr.wait("gpsimd", ac, pr.pend["scalar"])
        v_wgu0 = issue_wgu(0, 0, 0)
        pr.wait("tensor", yacc_ready[0], yacc_ready[1])
        pr.wait("tensor", ld, v_cst)
        pr.wait("tensor", lg, v_wgt)

        for tt in range(NTT):
            pr.wait("tensor", ac, pr.pend["scalar"])
            v6 = 0
            for dt in range(8):
                bk = b6 if dt < 4 else b2
                pr.op("tensor", lambda E, tt=tt, dt=dt, bk=bk: E.transpose(
                    bk[:, 128 * (dt % 4):128 * (dt % 4) + 128],
                    yacc[:, tt, dt * 128:(dt + 1) * 128], id_f32[:]),
                    inc=pe if dt % 4 == 3 else None)
                if dt == 3:
                    v6 = pe.n
            v7 = pe.n
            pr.wait("scalar", pe, v6)
            pr.sop("scalar", lambda E, tt=tt: E.copy(
                y1T.ap()[:, 0:4, tt * 128:(tt + 1) * 128],
                b6.ap()[:, 0:512].rearrange("p (a b) -> p a b", a=4)))
            pr.wait("scalar", pe, v7)
            vac = pr.sop("scalar", lambda E, tt=tt: E.copy(
                y1T.ap()[:, 4:8, tt * 128:(tt + 1) * 128],
                b2.ap()[:, 0:512].rearrange("p (a b) -> p a b", a=4)))
            pr.wait("tensor", ac, vac)
            for dt in range(8):
                pr.op("tensor",
                      lambda E, tt=tt, dt=dt: E.matmul(
                          b7[:, 0:NE],
                          y1T[:, dt, tt * 128:(tt + 1) * 128],
                          wgT_sb[:, dt, :],
                          start=(dt == 0), stop=(dt == 7)),
                      inc=pe if dt == 7 else None)
            v_log = pe.n
            pr.wait("vector", pe, v_log)
            pr.sop("vector", lambda E: E.tensor_add(l_sb[:], b7[:, 0:NE], bg_sb[:]))
            pr.sop("vector", lambda E: E.reduce_max(lncol[:, 0:1], l_sb[:], axis=AX.X))
            pr.sop("vector", lambda E: E.tensor_scalar(
                t8a[:], l_sb[:], lncol[:, 0:1], None, ALU.is_equal))
            pr.sop("vector", lambda E: E.scalar_tensor_tensor(
                t8b[:], t8a[:], -1e30, l_sb[:], ALU.mult, ALU.add))
            pr.sop("vector", lambda E: E.reduce_max(lncol[:, 1:2], t8b[:], axis=AX.X))
            pr.sop("vector", lambda E: E.tensor_scalar(
                t8a[:], l_sb[:], lncol[:, 1:2], None, ALU.is_ge))
            pr.sop("vector", lambda E: E.tensor_scalar(
                lncol[:, 2:3], lncol[:, 1:2], lncol[:, 0:1], None, ALU.subtract))
            v1 = pr.sop("vector", lambda E: E.tensor_scalar(
                t8b[:], l_sb[:], lncol[:, 0:1], None, ALU.subtract))
            pr.wait("scalar", ve, v1)
            pr.sop("scalar", lambda E: E.activation(lncol[:, 3:4], lncol[:, 2:3], AF.Exp))
            v2 = pr.sop("scalar", lambda E: E.activation(t8b[:], t8b[:], AF.Exp))
            pr.wait("vector", ac, v2)
            pr.sop("vector", lambda E: E.tensor_scalar(
                lncol[:, 4:5], lncol[:, 3:4], 1.0, None, ALU.add))
            pr.sop("vector", lambda E: E.reciprocal(lncol[:, 5:6], lncol[:, 4:5]))
            v3 = pr.sop("vector", lambda E, tt=tt: E.scalar_tensor_tensor(
                wstage[:, tt, :], t8b[:], lncol[:, 5:6], t8a[:], ALU.mult, ALU.mult))
            pr.wait("tensor", ve, v3)

        dbg_dump("wstage", wstage[:])
        v_y1T_done = pr.pend["scalar"]
        v_w_done = pr.pend["vector"]
        v_pe_mark = pe.n
        pr.flush()

        steps = [(e, hq) for e in range(NE) for hq in range(HQ)]
        v_wgu = [None] * len(steps)
        v_wd = [None] * len(steps)
        v_wgu[0] = v_wgu0

        v_step_done = [0] * (len(steps) + 1)
        dacc_guard = [0, 0]
        gsil_guard = [0, 0]
        gbank_guard = [0, 0]
        ubank_guard = [0, 0]
        for si, (e, hq) in enumerate(steps):
            buf = si % 2
            pr.wait("gpsimd", pe, v_step_done[si])
            v_wd[si] = issue_wd(e, hq)
            if si + 1 < len(steps):
                en, hn = steps[si + 1]
                v_wgu[si + 1] = issue_wgu(en, hn, (si + 1) % 2)
            pr.wait("tensor", lg, v_wgu[si])
            if si == 0:
                pr.wait("tensor", ac, v_y1T_done)
            for t5 in range(NT512):
                tok = t5 * T5W
                for ht in range(8):
                    par = ht % 2
                    gb, ub = pbanks[2 * par], pbanks[2 * par + 1]
                    gs = gsil[par]
                    pr.wait("tensor", ac, gbank_guard[par])
                    pr.wait("tensor", ve, ubank_guard[par])
                    for dt in range(8):
                        pr.op("tensor", lambda E, buf=buf, dt=dt, ht=ht, tok=tok, gb=gb:
                              E.matmul(gb[:, 0:T5W],
                                       wgq[buf][:, dt, ht * 128:(ht + 1) * 128],
                                       y1T[:, dt, tok:tok + T5W],
                                       start=(dt == 0), stop=(dt == 7)),
                              inc=pe if dt == 7 else None)
                    vg_pe = pe.n
                    for dt in range(8):
                        pr.op("tensor", lambda E, buf=buf, dt=dt, ht=ht, tok=tok, ub=ub:
                              E.matmul(ub[:, 0:T5W],
                                       wuq[buf][:, dt, ht * 128:(ht + 1) * 128],
                                       y1T[:, dt, tok:tok + T5W],
                                       start=(dt == 0), stop=(dt == 7)),
                              inc=pe if dt == 7 else None)
                    vu_pe = pe.n
                    pr.wait("scalar", pe, vg_pe)
                    pr.wait("scalar", ve, gsil_guard[par])
                    va = pr.sop("scalar", lambda E, gb=gb, gs=gs: E.activation(
                        gs[:, 0:T5W], gb[:, 0:T5W], AF.Silu))
                    gbank_guard[par] = va
                    pr.wait("vector", ac, va)
                    pr.wait("vector", pe, vu_pe)
                    vv = pr.sop("vector", lambda E, ub=ub, gs=gs, ht=ht: E.tensor_mul(
                        hstage[:, ht, 0:T5W], gs[:, 0:T5W], ub[:, 0:T5W]))
                    ubank_guard[par] = vv
                    gsil_guard[par] = vv
                v_h_ready = pr.pend["vector"]
                pr.wait("tensor", ve, v_h_ready)
                pr.wait("tensor", lg, v_wd[si])
                for grp in range(2 * SUBS):
                    sub, half = grp // 2, grp % 2
                    bk = pbanks[4 + grp % 2]
                    pr.wait("tensor", ve, dacc_guard[grp % 2])
                    for ht in range(8):
                        pr.op("tensor", lambda E, ht=ht, sub=sub, half=half, bk=bk:
                              E.matmul(bk[:], hstage[:, ht, sub * 128:(sub + 1) * 128],
                                       wdq[:, ht, half * 512:(half + 1) * 512],
                                       start=(ht == 0), stop=(ht == 7)),
                              inc=pe if ht == 7 else None)
                    vdn = pe.n
                    pr.wait("vector", pe, vdn)
                    tt = t5 * SUBS + sub
                    vv = pr.sop("vector", lambda E, tt=tt, half=half, e=e, bk=bk:
                                E.scalar_tensor_tensor(
                                    yacc[:, tt, half * 512:(half + 1) * 512],
                                    bk[:], wstage[:, tt, e:e + 1],
                                    yacc[:, tt, half * 512:(half + 1) * 512],
                                    ALU.mult, ALU.add))
                    dacc_guard[grp % 2] = vv
            v_step_done[si + 1] = pe.n

        v_ffn_done = pr.pend["vector"]
        pr.wait("scalar", ve, v_ffn_done)
        for tt in range(NTT):
            v_ln = ln_tile(yacc[:, tt, :], yacc[:, tt, :], scr1[:])
            pr.wait("sync", ve, v_ln)
            dma("sync", st, out_y.rearrange("(tt p) d -> p tt d", p=128)[:, tt, :],
                yacc[:, tt, :])
        pr.wait("sync", st, st.n)

        pr.flush()

    return nc



def prep_inputs(inp, T=4096):
    f32 = np.float32
    x = np.asarray(inp["L0_attn_x"], f32)
    Wv = np.asarray(inp["L0_attn_Wv"], f32)
    Wg = np.asarray(inp["L0_attn_Wg"], f32)
    Wo = np.asarray(inp["L0_attn_Wo"], f32)
    Wb = np.asarray(inp["L0_attn_Wb"], f32)
    Wa = np.asarray(inp["L0_attn_Wa"], f32)
    dt_bias = np.asarray(inp["L0_attn_dt_bias"], f32)
    A_log = np.asarray(inp["L0_attn_A_log"], f32)
    norm_w = np.asarray(inp["L0_attn_norm_w"], f32)
    conv_w = np.asarray(inp["L0_attn_conv_w"], f32)
    conv_b = np.asarray(inp["L0_attn_conv_b"], f32)
    wg = np.asarray(inp["L0_ffn_wg"], f32)
    bg = np.asarray(inp["L0_ffn_bg"], f32)
    Wgate = np.asarray(inp["L0_ffn_Wgate"], f32)
    Wup = np.asarray(inp["L0_ffn_Wup"], f32)
    Wdown = np.asarray(inp["L0_ffn_Wdown"], f32)

    TOWN = T // 2
    NCH = T // C
    shared = dict(host_constants())
    shared["wgateT"] = np.ascontiguousarray(Wgate.transpose(0, 2, 1))
    shared["wupT"] = np.ascontiguousarray(Wup.transpose(0, 2, 1))
    shared["wdownT"] = np.ascontiguousarray(Wdown.transpose(0, 2, 1))
    shared["wgT"] = np.ascontiguousarray(wg.T)
    shared["bg_bc"] = np.tile(bg, (128, 1)).astype(f32)
    shared["wvT"] = np.ascontiguousarray(Wv.T)
    shared["wgaT"] = np.ascontiguousarray(Wg.T)
    shared["woT"] = np.ascontiguousarray(Wo.T)
    shared["wabT"] = np.ascontiguousarray(
        np.concatenate([Wa.T, Wb.T], axis=1))
    shared["conv_wD"] = np.ascontiguousarray(
        conv_w.reshape(8, 128, 4).transpose(1, 0, 2).reshape(128, 32))
    shared["conv_bD"] = np.ascontiguousarray(conv_b.reshape(8, 128).T)
    shared["dtb_bc"] = np.tile(dt_bias, (128, 1)).astype(f32)
    shared["alog_row"] = np.ascontiguousarray(A_log[None, :])

    B = x.shape[0]
    in_maps = []
    for core in range(2 * B):
        b = core // 2
        half = core % 2
        m = dict(shared)
        xb = np.ascontiguousarray(x[b])
        m["xtm"] = xb
        m["xT"] = np.ascontiguousarray(xb.T)
        m["x_own"] = np.ascontiguousarray(xb[half * TOWN:(half + 1) * TOWN])
        m["xT_own"] = np.ascontiguousarray(xb.T[:, half * TOWN:(half + 1) * TOWN])
        nm = np.zeros((128, NCH), f32)
        own = np.arange(NCH) // (NCH // 2) == half
        nm[:, own] = norm_w[:, None]
        m["nwm"] = nm
        in_maps.append(m)
    return in_maps


def kernel(**inputs):
    T = inputs["L0_attn_x"].shape[1]
    in_maps = prep_inputs(inputs, T=T)
    nc = build(T=T, skip_attn=False)
    res = run_bass_kernel_spmd(nc, in_maps, core_ids=list(range(NCORES)))
    B = inputs["L0_attn_x"].shape[0]
    TOWN = T // 2
    out = np.empty((B, T, D), np.float32)
    for core in range(NCORES):
        b, half = core // 2, core % 2
        out[b, half * TOWN:(half + 1) * TOWN] = res.results[core]["out_y"]
    return out
